# version 26
# baseline (speedup 1.0000x reference)
import numpy as np
import concourse.bass as bass
import concourse.mybir as mybir
from concourse.alu_op_type import AluOpType as Op
from concourse.bass_utils import run_bass_kernel_spmd

F32 = mybir.dt.float32
BF16 = mybir.dt.bfloat16
AF = mybir.ActivationFunctionType
AX = mybir.AxisListType
P = 128
ENGINES = ("sp", "act", "dve", "pe", "pool")
RG_PAIRS = [[0, 1], [2, 3], [4, 5], [6, 7]]


class CFG:
    def __init__(self, **kw):
        self.Dm = 1024; self.DI = 2048; self.NST = 16; self.RK = 64
        self.CONV = 4; self.F = 4096; self.E = 4; self.L = 4096
        self.C = 256
        self.GT = 512
        self.FQ = 1024
        self.CPG = 0
        self.PFADV = 0
        self.DBB = 2
        self.EPS = 1e-5
        self.debug = False
        for k, v in kw.items():
            setattr(self, k, v)
        self.DH = self.DI // 2
        self.NCH = self.L // self.C
        self.TP = self.L // 2
        self.TT = self.C // P
        self.DBLK = self.DH // P
        self.CBLK = self.DI // P
        self.DMB = self.Dm // P
        self.NG = self.TP // self.GT
        self.SPG = self.GT // self.C
        self.NFQ = self.F // self.FQ
        self.FB = self.FQ // P
        self.RKC = self.RK + 2 * self.NST
        self.CPG = self.NCH // self.NG
        self.DBB = min(self.DBB, self.DBLK)
        assert self.C % P == 0 and self.DH % P == 0 and self.Dm % P == 0
        assert self.TP % self.GT == 0 and self.GT % self.C == 0
        assert self.F % self.FQ == 0 and self.FQ % P == 0
        assert self.DBLK % self.DBB == 0
        assert self.RK <= P and self.NST * 2 + self.RK <= P
        assert self.NCH % 2 == 0


def cdiv(a, b):
    return (a + b - 1) // b


class Builder:
    def __init__(self, cfg: CFG):
        self.cfg = cfg
        self.nc = bass.Bass("TRN2", target_bir_lowering=False, num_devices=8)
        self.streams = {e: [] for e in ENGINES}
        self.counts = {}
        self.last_wait = {}
        self.sem_names = set()
        self.dram = {}

    def ev(self, sem, n=1):
        self.sem_names.add(sem)
        v = self.counts.get(sem, 0) + n
        self.counts[sem] = v
        return (sem, v)

    def op(self, eng, fn, deps=(), inc=None, inc_n=1):
        e = self.ev(inc, inc_n) if inc else None
        agg = {}
        for d in deps:
            if d is None:
                continue
            sem, val = d
            if val > 0:
                agg[sem] = max(agg.get(sem, 0), val)
        waits = []
        for sem, val in agg.items():
            key = (eng, sem)
            if self.last_wait.get(key, 0) < val:
                self.last_wait[key] = val
                waits.append((sem, val))

        def run(h, sems):
            for sem, val in waits:
                h.wait_ge(sems[sem], val)
            inst = fn(h)
            if e is not None and inst is not None:
                inst.then_inc(sems[e[0]], inc_n)

        self.streams[eng].append(run)
        return e

    def drain(self, eng):
        self.streams[eng].append(lambda h, sems: h.drain())

    def dma(self, eng, out_ap, in_ap, deps, qsem):
        return self.op(eng, lambda h: h.dma_start(out=out_ap, in_=in_ap),
                       deps=deps, inc=qsem, inc_n=16)

    def act(self, fn, deps=(), inc="s_act"):
        return self.op("act", fn, deps=deps, inc=inc)

    def dve(self, fn, deps=(), inc="s_dve"):
        return self.op("dve", fn, deps=deps, inc=inc)

    def pe(self, fn, deps=(), inc="s_pe"):
        return self.op("pe", fn, deps=deps, inc=inc)

    def pool(self, fn, deps=(), inc="s_pool"):
        return self.op("pool", fn, deps=deps, inc=inc)

    def mm_chain(self, bank_ap, pairs, deps, inc="s_pe"):
        n = len(pairs)
        ev = None
        for i, (st, mv) in enumerate(pairs):
            ev = self.pe(lambda h, st=st, mv=mv, i=i:
                         h.matmul(bank_ap, st(), mv(), start=(i == 0),
                                  stop=(i == n - 1)),
                         deps=deps if i == 0 else (),
                         inc=inc if i == n - 1 else None)
        return ev

    def din(self, name, shape, dtype=F32):
        self.dram[name] = self.nc.dram_tensor(
            name, list(shape), dtype, kind="ExternalInput").ap()

    def dout(self, name, shape, dtype=F32):
        self.dram[name] = self.nc.dram_tensor(
            name, list(shape), dtype, kind="ExternalOutput").ap()

    def dint(self, name, shape, dtype=F32):
        self.dram[name] = self.nc.dram_tensor(
            name, list(shape), dtype, kind="Internal").ap()

    def build(self):
        from contextlib import ExitStack
        c = self.cfg
        self.din("x_seq", (c.L, c.Dm))
        self.din("x_post", (c.TP, c.Dm))
        self.din("w_in_tiled", ((c.CBLK + c.DH // P) * (c.Dm // P) * P, P))
        self.din("conv_w", (c.DI, c.CONV))
        self.din("conv_b", (c.DI,))
        self.din("x_proj_T", (c.DI, c.RKC))
        self.din("dt_w_T", (c.RK, c.DH))
        self.din("dt_b", (c.DH,))
        self.din("A_log_h", (c.DH, c.NST))
        self.din("Dp_h", (c.DH,))
        self.din("op_w_T", (c.DH, c.Dm))
        self.din("norm_w", (c.Dm,))
        self.din("router_w", (c.Dm, c.E))
        self.din("router_b", (c.E,))
        NS = c.E * c.NFQ
        W13C = c.FB * 2 * c.DMB * P
        self.din("w13_tiled", (NS * P, W13C))
        self.din("w2_tiled", (NS * P, c.FB * c.Dm))
        self.dint("w13_b", (NS * P, W13C), BF16)
        self.dint("w2_b", (NS * P, c.FB * c.Dm), BF16)
        NCW = (c.CBLK + c.DBLK) * c.DMB * P
        self.dint("w_in_b", (NCW, P), BF16)
        self.dint("conv_d", (P, c.CBLK * c.CONV * P), BF16)
        self.dint("sel_d", (2 * c.NST, 2 * c.NST * P), BF16)
        self.dint("rs_in", (c.L, c.Dm))
        self.dint("comb_d", (c.TP, c.Dm))
        self.dint("x1_spill", (c.TP, c.Dm))
        self.dout("out_post", (c.TP, c.Dm))
        if c.debug:
            self.dout("dbg_xc", (c.NCH, P, c.CBLK * c.C))
            self.dout("dbg_delta", (c.NCH, P, c.DBLK * c.C))
            self.dout("dbg_y", (c.NCH, P, c.DBLK * c.C))
            self.dout("dbg_x1", (c.TP, c.Dm))
            self.dout("dbg_gates", (c.TP // P, P, c.E))

        with ExitStack() as es:
            self.es = es
            self.alloc_sbuf()
            self.emit_preload()
            self.emit_driver()
            self.emit_final_wait()
            sems = {}
            for name in sorted(self.sem_names):
                sems[name] = es.enter_context(self.nc.semaphore(name))
            block = es.enter_context(self.nc.Block())
            for attr, eng in [("sync", "sp"), ("scalar", "act"),
                              ("vector", "dve"), ("tensor", "pe"),
                              ("gpsimd", "pool")]:
                def replay(h, eng=eng):
                    for fn in self.streams[eng]:
                        fn(h, sems)
                getattr(block, attr)(replay)
        return self.nc

    def sb(self, name, shape, dtype):
        return self.es.enter_context(self.nc.sbuf_tensor(name, shape, dtype))

    def psum_t(self, name, shape, dtype=F32):
        return self.es.enter_context(self.nc.psum_tensor(name, shape, dtype))

    def alloc_sbuf(self):
        c = self.cfg
        nb = mybir.dt.size
        self.ident_b = self.sb("ident_b", [P, P], BF16)
        self.iota_f = self.sb("iota_f", [P, P], F32)
        self.ones_f = self.sb("ones_f", [1, P], F32)
        self.sel_r = self.sb("sel_r", [2 * c.NST, 4 * 2 * P], BF16)
        self.A_sb = self.sb("A_sb", [P, c.DBLK * c.NST], F32)
        self.convw_f = self.sb("convw_f", [P, c.CBLK * c.CONV], F32)
        self.convb_sb = self.sb("convb_sb", [P, c.CBLK], F32)
        self.dtb_sb = self.sb("dtb_sb", [P, c.DBLK], F32)
        self.dp_sb = self.sb("dp_sb", [P, c.DBLK], F32)
        self.nw_sb = self.sb("nw_sb", [P, c.DMB], F32)
        self.rw_sb = self.sb("rw_sb", [P, c.DMB * c.E], BF16)
        self.rb_bc = self.sb("rb_bc", [P, c.E], F32)
        self.hstate = self.sb("hstate", [P, c.DBLK * c.NST], F32)
        self.gates_all = self.sb("gates_all", [P, (c.TP // P) * c.E], F32)

        m_shapes = dict(
            x0_a=([P, c.TT * c.Dm], F32),
            xn_b=([P, c.Dm], BF16),
            xnT=([P, c.DMB * c.C], BF16),
            xi=([P, c.CBLK * (c.C + c.CONV - 1)], BF16),
            xcb=([P, (c.CBLK - c.DBLK) * c.C], BF16),
            xcf=([P, 2 * c.DBLK * c.C], BF16),
            z_b=([P, 2 * c.DBLK * c.C], BF16),
            delta=([P, 2 * c.DBLK * c.C], BF16),
            sp_tmp=([P, max(1, c.DBLK // 4) * c.C], F32),
            du=([P, c.DBLK * c.C], BF16),
            dA0=([P, c.DBB * c.C], F32), dA1=([P, c.DBB * c.C], F32),
            dBu=([P, c.DBB * c.C], F32),
            h_b0=([P, c.DBB * c.C], F32), h_b1=([P, c.DBB * c.C], F32),
            ymul=([P, c.DBB * c.C], F32),
            y_acc=([P, c.DBLK * c.C], F32),
            yg=([P, c.DBLK * c.C], BF16),
            dblb=([c.RK, c.C], BF16),
            bc16=([P, 2 * 3 * c.C], BF16),
            opev=([P, c.TT * c.Dm], F32),
            lnst=([P, 16], F32),
            w_in_ring=([P, 4 * c.DMB * P], BF16),
            opT=([P, c.DBLK * c.Dm], BF16),
            conv_ring=([P, 2 * c.CONV * P], BF16),
            xprojT=([P, c.CBLK * c.RKC], BF16),
            dtprojT=([c.RK, c.DBLK * P], BF16),
        )
        f_shapes = dict(
            comb=([P, c.Dm], F32), xp=([P, c.Dm], F32),
            t1b=([P, c.Dm], F32),
            x1b=([P, c.Dm], F32), xn2=([P, c.Dm], BF16),
            silu_t=([P, 2 * 512], BF16),
            lg=([P, c.E], F32), lgm=([P, c.E], F32),
            lgn=([P, c.E], F32),
            gcl=([P, 8], F32),
            flnst=([P, 16], F32),
            xn2T=([P, c.DMB * c.GT], BF16),
            f_acc=([P, (c.GT // P) * c.Dm], F32),
            h_q=([P, c.FB * c.GT], BF16),
            w2r=([P, max(8 * c.Dm, 2 * c.NST * P,
                          c.CBLK * c.CONV * P)], BF16),
            w13s=([P, 2 * 2 * c.DMB * P], BF16),
        )
        m_shapes["brep"] = ([P, 2 * c.DBB * c.C], BF16)
        m_shapes["crep"] = ([P, 2 * c.DBB * c.C], BF16)

        self.M = {k: self.sb("m_" + k, s, d) for k, (s, d) in
                  m_shapes.items()}
        self.F = {k: self.sb("f_" + k, s, d) for k, (s, d) in
                  f_shapes.items()}
        self.ps_mm = self.psum_t("ps_mm", [P, max(c.C, 512)])
        self.ps_tr = self.psum_t("ps_tr", [P, P], BF16)
        self.ps_bc = [self.psum_t("ps_bc0", [P, 512]),
                      self.psum_t("ps_bc1", [P, 512])]
        self.ps_h1 = self.psum_t("ps_h1", [P, 512])
        self.ps_h3 = self.psum_t("ps_h3", [P, 512])
        self.ps_f = [self.psum_t("ps_f0", [P, 512]),
                     self.psum_t("ps_f1", [P, 512])]

    def emit_preload(self):
        c, M = self.cfg, self.M
        ev_iota = self.pool(lambda h: h.iota(
            self.iota_f[:], [[-1, P]], base=0, channel_multiplier=1,
            allow_small_or_imprecise_dtypes=True))
        self.ev_id = self.dve(lambda h: h.tensor_single_scalar(
            self.ident_b[:], self.iota_f[:], 0.0, Op.is_equal),
            deps=[ev_iota])
        self.ev_ones = self.dve(lambda h: h.memset(self.ones_f[:], 1.0))
        NS2 = 2 * c.NST
        ev_sel = None
        for n in range(c.NST):
            for kk, hot in ((0, n), (1, c.NST + n)):
                col = 2 * n + kk
                evi = self.pool(lambda h, hot=hot: h.iota(
                    self.iota_f[0:NS2, 0:P], [[0, P]], base=-hot,
                    channel_multiplier=1,
                    allow_small_or_imprecise_dtypes=True),
                    deps=[ev_sel if ev_sel else self.ev_id])
                ev_sel = self.dve(lambda h, col=col: h.tensor_single_scalar(
                    self.F["w2r"][0:NS2, col * P:(col + 1) * P],
                    self.iota_f[0:NS2, 0:P], 0.0, Op.is_equal),
                    deps=[evi])
        self.ev_sel = self.dma(
            "sp", self.dram["sel_d"][:, :],
            self.F["w2r"][0:NS2, 0:NS2 * P], [ev_sel], "q_seld")

        def load_cols(dst, src, nblk):
            in_ap = self.dram[src].rearrange("(b p) -> p b", p=P)
            return self.op(
                "sp", lambda h, d=dst, i=in_ap: h.dma_start(
                    out=d[:, 0:nblk], in_=i, allow_slow_non_contiguous=True),
                deps=[], inc="q_pre", inc_n=16)

        pre = [load_cols(self.convb_sb, "conv_b", c.CBLK),
               load_cols(self.dtb_sb, "dt_b", c.DBLK),
               load_cols(self.dp_sb, "Dp_h", c.DBLK),
               load_cols(self.nw_sb, "norm_w", c.DMB),
               self.dma("sp",
                        self.A_sb[:, 0:c.DBLK * c.NST].rearrange(
                            "p (b n) -> p b n", b=c.DBLK),
                        self.dram["A_log_h"].rearrange(
                            "(b p) n -> p b n", p=P), [], "q_pre"),
               self.dma("sp",
                        self.convw_f[:, 0:c.CBLK * c.CONV].rearrange(
                            "p (b k) -> p b k", b=c.CBLK),
                        self.dram["conv_w"].rearrange(
                            "(b p) k -> p b k", p=P), [], "q_pre"),
               self.dma("sp", self.rb_bc[0:1, 0:c.E],
                        self.dram["router_b"].rearrange("(o e) -> o e", o=1),
                        [], "q_pre")]
        self.pre = pre
        ev = self.act(lambda h: h.activation(
            self.A_sb[:, 0:c.DBLK * c.NST], self.A_sb[:, 0:c.DBLK * c.NST],
            AF.Exp), deps=pre)
        self.ev_A = self.dve(lambda h: h.tensor_scalar_mul(
            self.A_sb[:, 0:c.DBLK * c.NST], self.A_sb[:, 0:c.DBLK * c.NST],
            -1.0), deps=[ev])
        ev = self.pe(lambda h: h.matmul(
            self.ps_mm[:, 0:c.E], self.ones_f[:], self.rb_bc[0:1, 0:c.E],
            start=True, stop=True), deps=pre + [self.ev_ones])
        self.ev_rb = self.act(lambda h: h.activation(
            self.rb_bc[:, 0:c.E], self.ps_mm[:, 0:c.E], AF.Copy), deps=[ev])
        self.ev_rw = self.dma("pool",
                              self.rw_sb[:, 0:c.DMB * c.E].rearrange(
                                  "p (b e) -> p b e", b=c.DMB),
                              self.dram["router_w"].rearrange(
                                  "(b p) e -> p b e", p=P), [], "q_pool")
        ev_cd = None
        ntile = c.CBLK * c.CONV
        assert ntile * P <= 8 * c.Dm
        for col in range(ntile):
            o = col * P
            ev_cd = self.dve(lambda h, o=o, col=col:
                h.tensor_scalar_mul(self.F["w2r"][:, o:o + P],
                                    self.ident_b[:],
                                    self.convw_f[:, col:col + 1]),
                deps=[self.ev_id, self.ev_sel] + pre)
        self.ev_cd = self.dma("sp", self.dram["conv_d"][:, 0:ntile * P],
                              self.F["w2r"][:, 0:ntile * P], [ev_cd],
                              "q_cvd")
        self.ev_h0 = self.dve(lambda h: h.memset(
            self.hstate[:, 0:c.DBLK * c.NST], 0.0))
        self.ev_halo0 = None
        for cblk in range(c.CBLK):
            o = cblk * (c.C + c.CONV - 1)
            self.ev_halo0 = self.dve(lambda h, o=o: h.memset(
                M["xi"][:, o:o + c.CONV - 1], 0.0))

        def wload3(dst_name, src, nb_):
            return self.dma("pool",
                            M[dst_name][:, :].rearrange("p (b o) -> p b o",
                                                        b=nb_),
                            self.dram[src].rearrange("(b p) o -> p b o",
                                                     p=P),
                            (), "q_pool")

        NWA = c.CBLK + c.DBLK
        self.ev_winb = self.dma(
            "pool",
            self.dram["w_in_b"].rearrange("(a x) p -> a (x p)", a=NWA),
            self.dram["w_in_tiled"].rearrange("(a x) p -> a (x p)", a=NWA),
            [], "q_winb")
        self.ev_rc = []
        self.ev_opw = wload3("opT", "op_w_T", c.DBLK)
        self.ev_xpw = wload3("xprojT", "x_proj_T", c.CBLK)
        self.ev_dtw = self.dma(
            "pool", M["dtprojT"][0:c.RK, :], self.dram["dt_w_T"][:, :],
            [], "q_pool")

    def emit_driver(self):
        c = self.cfg
        self._m = dict(
            x0_ev=None, x0_free=None,
            xi_tail=None, lnst_busy=None, tr_prev=None,
            bank_free={0: None, 1: None},
            wslab_free=[None] * 4, wslab=0,
            bc_free=[None, None], dA_free=[None, None], h_free=[None, None],
            cv_free=[None, None], sel_free=[None] * 4, hs_evs={},
            selbank_busy={},
            hs_prev=self.ev_h0, pf_free=None, partial_last=None,
            opmm_last=None, sp_free=None, dBu_busy=None,
            selmm_last=[None, None], dup_free=[None, None],
            prep=[None, None],
        )
        DONE = object()
        pf = []
        ng_emitted = 0

        def adv(gen):
            return next(gen, DONE) is DONE

        NS = c.E * c.NFQ
        W13C = c.FB * 2 * c.DMB * P
        W2C = c.FB * c.Dm
        dve_marks = []
        for j in range(c.NCH + 1):
            dve_marks.append(self.counts.get("s_dve", 0))
            while len(self.ev_rc) < NS:
                s = len(self.ev_rc)
                if j < c.NCH and s > 2 * (j - 6) - 1:
                    break
                qn = "q_rc" + str(s)
                gate = ("s_dve", dve_marks[max(j - 1, 0)])
                for k in range(4):
                    r0, r1 = s * P + k * P // 4, s * P + (k + 1) * P // 4
                    self.dma("pool",
                             self.dram["w13_b"][r0:r1, :],
                             self.dram["w13_tiled"][r0:r1, :], [gate], qn)
                for k in range(2):
                    r0, r1 = s * P + k * P // 2, s * P + (k + 1) * P // 2
                    ev = self.dma(
                        "pool", self.dram["w2_b"][r0:r1, :],
                        self.dram["w2_tiled"][r0:r1, :], [gate], qn)
                self.ev_rc.append(ev)
            pg = self.emit_prep(j) if j < c.NCH else None
            sg = self.emit_scan(j - 1) if j >= 1 else None
            done_p, done_s = pg is None, sg is None
            while not (done_p and done_s):
                if not done_s:
                    done_s = adv(sg)
                for _ in range(3):
                    if not done_p:
                        done_p = adv(pg)
                for _ in range(c.PFADV):
                    if pf and adv(pf[0]):
                        pf.pop(0)
            if j >= c.CPG and j % c.CPG == 0:
                g = j // c.CPG - 1
                rs_ev = self.pool(
                    lambda h, g=g: h.collective_compute(
                        "ReduceScatter", mybir.AluOpType.add, RG_PAIRS,
                        [self.dram["rs_in"][g * c.CPG * c.C:
                                            (g + 1) * c.CPG * c.C, :]],
                        [self.dram["comb_d"][g * c.GT:(g + 1) * c.GT, :]]),
                    deps=[("q_rs", 16 * c.CPG * (g + 1))], inc="s_pool")
                pf.append(self.emit_postffn_group(g, rs_ev))
                ng_emitted += 1
        while pf:
            if adv(pf[0]):
                pf.pop(0)

    def emit_prep(self, j):
        c, M, S = self.cfg, self.M, self._m
        x0 = M["x0_a"]
        sl = j % 2
        W = c.C + c.CONV - 1
        wait_weights = [self.ev_opw, self.ev_xpw, self.ev_dtw, self.ev_cd,
                        self.ev_A, self.ev_sel, self.ev_halo0]
        if j == 0:
            S["x0_ev"] = self.dma(
                "sp",
                x0[:, 0:c.TT * c.Dm].rearrange("p (t d) -> p t d", t=c.TT),
                self.dram["x_seq"][0:c.C, :].rearrange(
                    "(t p) d -> p t d", p=P),
                [], "q_x0")

        ev_xnT = None
        for tt in range(c.TT):
            xt = x0[:, tt * c.Dm:(tt + 1) * c.Dm]
            deps = [S["x0_ev"], S["lnst_busy"]]
            nst = cdiv(c.Dm, 512)
            ev_bs = None
            for si in range(nst):
                w = min(512, c.Dm - si * 512)
                ev_bs = self.dve(lambda h, xt=xt, si=si, w=w: h.bn_stats(
                    M["lnst"][:, si * 6:si * 6 + 6],
                    xt[:, si * 512:si * 512 + w]), deps=deps)
            ev_ag = self.dve(lambda h, nst=nst: h.bn_aggr(
                M["lnst"][:, 12:14], M["lnst"][:, 0:6 * nst]),
                deps=[ev_bs])
            ev_v = self.dve(lambda h: h.tensor_scalar(
                M["lnst"][:, 14:15], M["lnst"][:, 13:14], 1.0 + c.EPS,
                c.EPS * c.EPS, Op.mult, Op.add), deps=[ev_ag])
            ev_ln = self.act(lambda h: h.activation(
                M["lnst"][:, 14:15], M["lnst"][:, 14:15], AF.Ln),
                deps=[ev_v])
            ev_ex = self.act(lambda h: h.activation(
                M["lnst"][:, 14:15], M["lnst"][:, 14:15], AF.Exp,
                scale=-0.5), deps=[ev_ln])
            ev_mu = self.dve(lambda h: h.tensor_tensor(
                M["lnst"][:, 15:16], M["lnst"][:, 12:13],
                M["lnst"][:, 14:15], Op.mult), deps=[ev_ex])
            ev = self.dve(lambda h: h.tensor_scalar_mul(
                M["lnst"][:, 15:16], M["lnst"][:, 15:16], -1.0),
                deps=[ev_mu])
            ev_xn = self.act(lambda h, xt=xt: h.activation(
                M["xn_b"][:, 0:c.Dm], xt, AF.Identity,
                bias=M["lnst"][:, 15:16], scale=M["lnst"][:, 14:15]),
                deps=[ev, ev_ex, S["tr_prev"]])
            S["lnst_busy"] = ev_xn
            for dmb in range(c.DMB):
                evt = self.pe(lambda h, dmb=dmb: h.transpose(
                    self.ps_tr[:], M["xn_b"][:, dmb * P:(dmb + 1) * P],
                    self.ident_b[:]),
                    deps=[ev_xn, self.ev_id, S["tr_prev"]])
                ev_xnT = self.act(lambda h, dmb=dmb, tt=tt:
                    h.activation(
                        M["xnT"][:, dmb * c.C + tt * P:
                                 dmb * c.C + tt * P + P],
                        self.ps_tr[:], AF.Identity,
                        scale=self.nw_sb[:, dmb:dmb + 1]), deps=[evt])
                S["tr_prev"] = ev_xnT
            yield
        S["x0_free"] = ev_xnT
        if j + 1 < c.NCH:
            S["x0_ev"] = self.dma(
                "sp",
                x0[:, 0:c.TT * c.Dm].rearrange("p (t d) -> p t d", t=c.TT),
                self.dram["x_seq"][(j + 1) * c.C:(j + 2) * c.C, :]
                .rearrange("(t p) d -> p t d", p=P),
                [S["x0_free"]], "q_x0")

        ev_xi_ev = []
        ev_z = None
        doff = sl * c.DBLK * c.C
        NCOL = c.CBLK + c.DBLK
        NSP = min(4, c.DMB)
        BPS = c.DMB // NSP
        ld_map = {}

        def issue_wload(cg):
            ws = cg % 4
            so_l = ws * c.DMB * P
            evs = []
            for sp_i in range(NSP):
                evs.append(self.dma(
                    "sp",
                    M["w_in_ring"][:, so_l + sp_i * BPS * P:
                                   so_l + (sp_i + 1) * BPS * P].rearrange(
                        "p (b cc) -> p b cc", b=BPS),
                    self.dram["w_in_b"][
                        cg * c.DMB * P + sp_i * BPS * P:
                        cg * c.DMB * P + (sp_i + 1) * BPS * P, :]
                    .rearrange("(b p) cc -> p b cc", p=P),
                    [S["wslab_free"][ws], self.ev_winb],
                    "q_wi" + str(ws) + "_" + str(sp_i)))
            ld_map[cg] = evs

        issue_wload(0)
        if NCOL > 1:
            issue_wload(1)
        for colg in range(NCOL):
            if colg + 2 < NCOL:
                issue_wload(colg + 2)
            ws = colg % 4
            so = ws * c.DMB * P
            ld_evs = ld_map.pop(colg)
            bank = [self.ps_mm, self.ps_h1][colg % 2]
            pairs = []
            for dmb in range(c.DMB):
                pairs.append((
                    lambda so=so, dmb=dmb:
                    M["w_in_ring"][:, so + dmb * P:so + dmb * P + P],
                    lambda dmb=dmb: M["xnT"][:, dmb * c.C:
                                             dmb * c.C + c.C]))
            evm = self.mm_chain(
                bank[:, 0:c.C], pairs,
                deps=ld_evs + [ev_xnT,
                               S["bank_free"][colg % 2]] + wait_weights)
            S["wslab_free"][ws] = evm
            if colg < c.CBLK:
                ev = self.act(lambda h, colg=colg, bank=bank:
                    h.activation(
                        M["xi"][:, colg * W + c.CONV - 1:
                                colg * W + c.CONV - 1 + c.C],
                        bank[:, 0:c.C], AF.Copy),
                    deps=[evm, S["xi_tail"]])
                ev_xi_ev.append(ev)
            else:
                zb = colg - c.CBLK
                ev = self.act(lambda h, zb=zb, bank=bank, doff=doff:
                    h.activation(
                        M["z_b"][:, doff + zb * c.C:doff + (zb + 1) * c.C],
                        bank[:, 0:c.C], AF.Silu), deps=[evm])
                ev_z = ev
            S["bank_free"][colg % 2] = ev
            if colg % 2 == 1:
                yield
        S["prep_z"] = ev_z

        ev_xcb = []
        ev_xcf = {}
        for cblk in range(c.CBLK):
            cvs = cblk % 2
            co = cvs * c.CONV * P
            ld_cv = self.dma(
                "sp", M["conv_ring"][:, co:co + c.CONV * P],
                self.dram["conv_d"][:, cblk * c.CONV * P:
                                    (cblk + 1) * c.CONV * P],
                [S["cv_free"][cvs], self.ev_cd], "q_cv" + str(cvs))
            bank = [self.ps_mm, self.ps_h1][cblk % 2]
            pairs = []
            for k in range(c.CONV):
                o = co + k * P
                xo = cblk * W + k
                pairs.append((lambda o=o: M["conv_ring"][:, o:o + P],
                              lambda xo=xo: M["xi"][:, xo:xo + c.C]))
            evm = self.mm_chain(bank[:, 0:c.C], pairs,
                                deps=[ev_xi_ev[cblk], ev_xi_ev[-1], ld_cv,
                                      S["bank_free"][cblk % 2]])
            S["cv_free"][cvs] = evm
            if cblk < c.DBLK:
                ev2 = self.act(lambda h, cblk=cblk, bank=bank, doff=doff:
                    h.activation(
                        M["xcf"][:, doff + cblk * c.C:
                                 doff + (cblk + 1) * c.C],
                        bank[:, 0:c.C], AF.Silu,
                        bias=self.convb_sb[:, cblk:cblk + 1]),
                    deps=[evm, S["xcf_free"] if "xcf_free" in S else None,
                          getattr(self, "ev_dbg_xc", None)])
                ev_xcf[cblk] = ev2
                ev_xcb.append(ev2)
                S["bank_free"][cblk % 2] = ev2
            else:
                ev1 = self.act(lambda h, cblk=cblk, bank=bank:
                    h.activation(
                        M["xcb"][:, (cblk - c.DBLK) * c.C:
                                 (cblk - c.DBLK + 1) * c.C],
                        bank[:, 0:c.C], AF.Silu,
                        bias=self.convb_sb[:, cblk:cblk + 1]),
                    deps=[evm, getattr(self, "ev_dbg_xc", None)])
                ev_xcb.append(ev1)
                S["bank_free"][cblk % 2] = ev1
            if cblk % 2 == 1:
                yield
        ev_tail = None
        for cblk in range(c.CBLK):
            o = cblk * W
            ev_tail = self.dve(lambda h, o=o: h.tensor_copy(
                M["xi"][:, o:o + c.CONV - 1],
                M["xi"][:, o + c.C:o + c.C + c.CONV - 1]),
                deps=[ev_xcb[-1], ev_xi_ev[-1]])
        S["xi_tail"] = ev_tail

        bank = self.ps_mm
        pairs = []
        for cblk in range(c.CBLK):
            if cblk < c.DBLK:
                mv = (lambda cblk=cblk, doff=doff:
                      M["xcf"][:, doff + cblk * c.C:
                               doff + (cblk + 1) * c.C])
            else:
                mv = (lambda cblk=cblk:
                      M["xcb"][:, (cblk - c.DBLK) * c.C:
                               (cblk - c.DBLK + 1) * c.C])
            pairs.append((
                lambda cblk=cblk: M["xprojT"][:, cblk * c.RKC:
                                              (cblk + 1) * c.RKC],
                mv))
        evm = self.mm_chain(bank[0:c.RKC, 0:c.C], pairs,
                            deps=[ev_xcb[-1], S["bank_free"][0]])
        ev_dt = self.act(lambda h, bank=bank: h.activation(
            M["dblb"][0:c.RK, 0:c.C], bank[0:c.RK, 0:c.C], AF.Copy),
            deps=[evm])
        boff = sl * 3 * c.C
        ev_bcev = self.act(lambda h, bank=bank, boff=boff: h.activation(
            M["bc16"][c.RK:c.RK + 2 * c.NST, boff:boff + c.C],
            bank[c.RK:c.RK + 2 * c.NST, 0:c.C], AF.Copy),
            deps=[evm, S["dup_free"][sl]])
        ev_bc1 = self.dma(
            "act", M["bc16"][0:2 * c.NST, boff + c.C:boff + 2 * c.C],
            M["bc16"][c.RK:c.RK + 2 * c.NST, boff:boff + c.C],
            [ev_bcev], "q_bc" + ("a" if sl == 0 else "b"))
        ev_bc2 = self.dma(
            "act", M["bc16"][0:2 * c.NST, boff + 2 * c.C:boff + 3 * c.C],
            M["bc16"][c.RK:c.RK + 2 * c.NST, boff:boff + c.C],
            [ev_bcev], "q_bc" + ("a" if sl == 0 else "b"))
        S["prep_bc"] = (ev_bc1, ev_bc2)
        S["bank_free"][0] = ev_bcev

        HB = c.DBLK // 4 if c.DBLK >= 4 else c.DBLK
        ev_delta = None
        for hh in range(c.DBLK // HB):
            ev_dl = []
            for q in range(HB):
                dblk = hh * HB + q
                bank = [self.ps_mm, self.ps_h1][dblk % 2]
                evm = self.mm_chain(
                    bank[:, 0:c.C],
                    [(lambda dblk=dblk: M["dtprojT"][0:c.RK, dblk * P:
                                                     (dblk + 1) * P],
                      lambda: M["dblb"][0:c.RK, 0:c.C])],
                    deps=[ev_dt, S["bank_free"][dblk % 2]])
                eva = self.act(lambda h, q=q, dblk=dblk, bank=bank:
                    h.activation(
                        M["sp_tmp"][:, q * c.C:(q + 1) * c.C],
                        bank[:, 0:c.C], AF.Abs,
                        bias=self.dtb_sb[:, dblk:dblk + 1]),
                    deps=[evm, S["sp_free"]])
                evr = self.act(lambda h, dblk=dblk, bank=bank, doff=doff:
                    h.activation(
                        M["delta"][:, doff + dblk * c.C:
                                   doff + (dblk + 1) * c.C],
                        bank[:, 0:c.C], AF.Relu,
                        bias=self.dtb_sb[:, dblk:dblk + 1]),
                    deps=[evm, getattr(self, "ev_dbg_dl", None)])
                ev_dl.append((eva, evr))
                S["bank_free"][dblk % 2] = evr
            eves = []
            for q in range(HB):
                eves.append(self.act(lambda h, q=q: h.activation(
                    M["sp_tmp"][:, q * c.C:(q + 1) * c.C],
                    M["sp_tmp"][:, q * c.C:(q + 1) * c.C],
                    AF.Exp, scale=-1.0), deps=[ev_dl[q][0]]))
            evls = []
            for q in range(HB):
                evls.append(self.act(lambda h, q=q: h.activation(
                    M["sp_tmp"][:, q * c.C:(q + 1) * c.C],
                    M["sp_tmp"][:, q * c.C:(q + 1) * c.C],
                    AF.Ln, bias=1.0), deps=[eves[q]]))
            for q in range(HB):
                dblk = hh * HB + q
                ev_delta = self.dve(lambda h, q=q, dblk=dblk, doff=doff:
                    h.tensor_add(
                        M["delta"][:, doff + dblk * c.C:
                                   doff + (dblk + 1) * c.C],
                        M["delta"][:, doff + dblk * c.C:
                                   doff + (dblk + 1) * c.C],
                        M["sp_tmp"][:, q * c.C:(q + 1) * c.C]),
                    deps=[evls[q], ev_dl[q][1]])
            S["sp_free"] = ev_delta
            yield
        yield
        S["prep"][sl] = dict(delta=ev_delta, xcf=ev_xcf,
                             z=ev_z, bc=S["prep_bc"], xcb=ev_xcb[-1])

        if c.debug:
            self.ev_dbg_xc = self.dma(
                "pool", self.dram["dbg_xc"][j, :, 0:c.DBLK * c.C],
                M["xcf"][:, doff:doff + c.DBLK * c.C],
                [ev_xcb[-1]], "q_dbgp")
            self.ev_dbg_xc = self.dma(
                "pool", self.dram["dbg_xc"][j, :, c.DBLK * c.C:],
                M["xcb"][:, :], [ev_xcb[-1], self.ev_dbg_xc], "q_dbgp")
            self.ev_dbg_dl = self.dma(
                "pool", self.dram["dbg_delta"][j, :, :],
                M["delta"][:, doff:doff + c.DBLK * c.C], [ev_delta],
                "q_dbg1")

    def emit_scan(self, j):
        c, M, S = self.cfg, self.M, self._m
        sl = j % 2
        doff = sl * c.DBLK * c.C
        boff = sl * 3 * c.C
        pr = S["prep"][sl]
        dA = [M["dA0"], M["dA1"]]
        h_b = [M["h_b0"], M["h_b1"]]
        nbt = c.DBLK // c.DBB
        DBC = c.DBB * c.C
        ev_y = None
        ev_hs_new = None
        bcdep = pr["bc"]
        du_evs = []
        for bb in range(nbt):
            du_evs.append(self.dve(lambda h, bb=bb, doff=doff:
                h.tensor_mul(
                    M["du"][:, bb * DBC:(bb + 1) * DBC],
                    M["delta"][:, doff + bb * DBC:doff + (bb + 1) * DBC],
                    M["xcf"][:, doff + bb * DBC:doff + (bb + 1) * DBC]),
                deps=[pr["delta"],
                      pr["xcf"].get(bb * c.DBB), pr["xcf"].get(
                          min(bb * c.DBB + c.DBB - 1, c.DBLK - 1))]))
        DBC2 = c.DBB * c.C
        hs_prev_map = S["hs_evs"]
        S["hs_evs"] = {}
        for n in range(c.NST):
            npar = n % 2
            np4 = n % 4
            ro = npar * DBC2
            so2 = np4 * 2 * P
            ld_sel = self.dma(
                "act", self.sel_r[:, so2:so2 + 2 * P],
                self.dram["sel_d"][:, 2 * n * P:2 * (n + 1) * P],
                [S["sel_free"][np4], self.ev_sel], "q_sel" + str(np4))
            if c.PFADV == 0:
                bkB = [self.ps_bc[0], self.ps_h3][npar]
                bkC = [self.ps_bc[1], self.ps_f[1]][npar]
                kB, kC = "b" + str(npar), "c" + str(npar)
                bfB = S["selbank_busy"].get(kB)
                bfC = S["selbank_busy"].get(kC)
            else:
                bkB = [self.ps_mm, self.ps_h1][0]
                bkC = [self.ps_mm, self.ps_h1][1]
                kB = kC = None
                bfB = S["bank_free"][0]
                bfC = S["bank_free"][1]
            evB = self.pe(lambda h, boff=boff, bkB=bkB, so2=so2: h.matmul(
                bkB[:, 0:2 * c.C],
                self.sel_r[:, so2:so2 + P],
                M["bc16"][0:2 * c.NST, boff + c.C:boff + 3 * c.C],
                start=True, stop=True),
                deps=[bcdep[0], bcdep[1], S["bc_free"][npar],
                      ld_sel, bfB])
            evBe = self.act(lambda h, ro=ro, bkB=bkB: h.activation(
                M["brep"][:, ro:ro + DBC2], bkB[:, 0:DBC2],
                AF.Copy), deps=[evB])
            if kB is None:
                S["bank_free"][0] = evBe
            else:
                S["selbank_busy"][kB] = evBe
            evC = self.pe(lambda h, boff=boff, bkC=bkC, so2=so2: h.matmul(
                bkC[:, 0:2 * c.C],
                self.sel_r[:, so2 + P:so2 + 2 * P],
                M["bc16"][0:2 * c.NST, boff + c.C:boff + 3 * c.C],
                start=True, stop=True), deps=[bfC])
            evCe = self.act(lambda h, ro=ro, bkC=bkC: h.activation(
                M["crep"][:, ro:ro + DBC2], bkC[:, 0:DBC2],
                AF.Copy), deps=[evC])
            if kC is None:
                S["bank_free"][1] = evCe
            else:
                S["selbank_busy"][kC] = evCe
            S["sel_free"][np4] = evC
            S["selmm_last"][sl] = evC
            for bb in range(nbt):
                par = (n * nbt + bb) % 2
                ev_dA = None
                for q in range(c.DBB):
                    dblk = bb * c.DBB + q
                    ev_dA = self.act(lambda h, q=q, dblk=dblk, n=n,
                                     par=par, doff=doff: h.activation(
                        dA[par][:, q * c.C:(q + 1) * c.C],
                        M["delta"][:, doff + dblk * c.C:
                                   doff + (dblk + 1) * c.C],
                        AF.Exp,
                        scale=self.A_sb[:, dblk * c.NST + n:
                                        dblk * c.NST + n + 1]),
                        deps=[pr["delta"], S["dA_free"][par]])
                evd = self.dve(lambda h, bb=bb, ro=ro:
                    h.tensor_mul(
                        M["dBu"][:, 0:DBC],
                        M["du"][:, bb * DBC:(bb + 1) * DBC],
                        M["brep"][:, ro:ro + DBC]),
                    deps=[du_evs[bb], evBe, S["h_free"][par],
                          S["dBu_busy"]])
                evs = None
                for q in range(c.DBB):
                    dblk = bb * c.DBB + q
                    col = dblk * c.NST + n
                    evs = self.dve(lambda h, q=q, col=col, par=par:
                        h.tensor_tensor_scan(
                            h_b[par][:, q * c.C:(q + 1) * c.C],
                            dA[par][:, q * c.C:(q + 1) * c.C],
                            M["dBu"][:, q * c.C:(q + 1) * c.C],
                            self.hstate[:, col:col + 1],
                            Op.mult, Op.add),
                        deps=[ev_dA,
                              hs_prev_map.get((n, bb), self.ev_h0), evd])
                S["dA_free"][par] = evs
                S["dBu_busy"] = evs
                col0 = bb * c.DBB * c.NST + n
                ev_hs_new = self.act(lambda h, par=par, col0=col0:
                    h.activation(
                        self.hstate[:, col0:col0 + (c.DBB - 1) * c.NST
                                    + 1:c.NST],
                        h_b[par][:, c.C - 1:c.DBB * c.C:c.C], AF.Copy),
                    deps=[evs])
                S["hs_evs"][(n, bb)] = ev_hs_new
                if n == 0:
                    ev_y = self.dve(lambda h, bb=bb, ro=ro, par=par:
                        h.tensor_mul(
                            M["y_acc"][:, bb * DBC:(bb + 1) * DBC],
                            h_b[par][:, 0:DBC],
                            M["crep"][:, ro:ro + DBC]),
                        deps=[evs, evCe, getattr(self, "ev_dbg_y", None)])
                else:
                    ev_ym = self.dve(lambda h, ro=ro, par=par:
                        h.tensor_mul(
                            M["ymul"][:, 0:DBC], h_b[par][:, 0:DBC],
                            M["crep"][:, ro:ro + DBC]), deps=[evs, evCe])
                    ev_y = self.dve(lambda h, bb=bb: h.tensor_add(
                        M["y_acc"][:, bb * DBC:(bb + 1) * DBC],
                        M["y_acc"][:, bb * DBC:(bb + 1) * DBC],
                        M["ymul"][:, 0:DBC]), deps=[ev_ym])
                S["h_free"][par] = ev_y
            S["bc_free"][npar] = ev_y
            yield
        S["dup_free"][sl] = S["selmm_last"][sl]

        zs_busy = [None, None]
        evg1 = None
        evg2 = None
        ev_yg = None
        for dblk in range(c.DBLK):
            zsl = dblk % 2
            ev_zs = self.act(lambda h, dblk=dblk, doff=doff, zsl=zsl:
                h.activation(
                    M["dBu"][:, zsl * c.C:(zsl + 1) * c.C],
                    M["z_b"][:, doff + dblk * c.C:doff + (dblk + 1) * c.C],
                    AF.Silu),
                deps=[pr["z"], ev_y, zs_busy[zsl]])
            evg1 = self.dve(lambda h, dblk=dblk, doff=doff, zsl=zsl:
                h.tensor_scalar(
                    M["ymul"][:, zsl * c.C:(zsl + 1) * c.C],
                    M["xcf"][:, doff + dblk * c.C:
                             doff + (dblk + 1) * c.C],
                    self.dp_sb[:, dblk:dblk + 1], None, Op.mult),
                deps=[ev_y])
            evg2 = self.dve(lambda h, dblk=dblk, zsl=zsl: h.tensor_add(
                M["y_acc"][:, dblk * c.C:(dblk + 1) * c.C],
                M["y_acc"][:, dblk * c.C:(dblk + 1) * c.C],
                M["ymul"][:, zsl * c.C:(zsl + 1) * c.C]), deps=[evg1])
            ev_yg = self.dve(lambda h, dblk=dblk, zsl=zsl: h.tensor_mul(
                M["yg"][:, dblk * c.C:(dblk + 1) * c.C],
                M["y_acc"][:, dblk * c.C:(dblk + 1) * c.C],
                M["dBu"][:, zsl * c.C:(zsl + 1) * c.C]),
                deps=[evg2, ev_zs, S["opmm_last"]])
            zs_busy[zsl] = ev_yg
            S["dBu_busy"] = ev_yg
            if dblk % 2 == 1:
                yield
        S["xcf_free"] = evg1
        ev_yg_last = ev_yg

        if c.debug:
            self.ev_dbg_y = self.dma(
                "sp", self.dram["dbg_y"][j, :, :],
                M["y_acc"][:, :], [evg2], "q_dbg2")

        ev_opev = []
        idx = 0
        for tt in range(c.TT):
            for half in range(cdiv(c.Dm, 512)):
                wcols = min(512, c.Dm - half * 512)
                bank = [self.ps_mm, self.ps_h1][idx % 2]
                pairs = []
                for cblk in range(c.DBLK):
                    pairs.append((
                        lambda cblk=cblk, tt=tt:
                        M["yg"][:, cblk * c.C + tt * P:
                                cblk * c.C + tt * P + P],
                        lambda cblk=cblk, half=half, wcols=wcols:
                        M["opT"][:, cblk * c.Dm + half * 512:
                                 cblk * c.Dm + half * 512 + wcols]))
                evm = self.mm_chain(bank[:, 0:wcols], pairs,
                                    deps=[ev_yg_last,
                                          S["bank_free"][idx % 2]])
                S["opmm_last"] = evm
                evo = self.act(lambda h, tt=tt, half=half, wcols=wcols,
                               bank=bank: h.activation(
                    M["opev"][:, tt * c.Dm + half * 512:
                              tt * c.Dm + half * 512 + wcols],
                    bank[:, 0:wcols], AF.Copy),
                    deps=[evm, S["partial_last"]])
                S["bank_free"][idx % 2] = evo
                ev_opev.append(evo)
                idx += 1
            yield
        rb = ((j // c.CPG) * c.CPG * c.C + (j % 2) * (c.CPG * c.C // 2)
              + ((j % c.CPG) // 2) * c.C)
        S["partial_last"] = self.dma(
            "sp",
            self.dram["rs_in"][rb:rb + c.C, :].rearrange(
                "(t p) d -> p t d", p=P),
            M["opev"][:, 0:c.TT * c.Dm].rearrange("p (t d) -> p t d",
                                                  t=c.TT),
            [ev_opev[-1]], "q_rs")

    def emit_postffn_group(self, g, rs_ev):
        c, F = self.cfg, self.F
        if not hasattr(self, "_pf"):
            self._pf = dict(flnst_busy=None, x1_guard=[], w2_free=None,
                            hq_free=None, slab=0, slab_free=[None] * 2,
                            si_busy={}, h_busy={}, psf_busy={},
                            facc_ev={}, w2rot=0,
                            facc_free=None, fout_free=None, out_last=None,
                            gate_done=None, step_free=None, h3_busy=None,
                            comb_busy=None, xp_busy=None, xn2_busy=None)
        PF = self._pf
        SPT = c.GT // P

        def ln_scale_bias(src_ap, extra_deps):
            extra_deps = list(extra_deps) + [PF["flnst_busy"]]
            nst = cdiv(c.Dm, 512)
            ev_bs = None
            for si in range(nst):
                w = min(512, c.Dm - si * 512)
                ev_bs = self.dve(lambda h, si=si, w=w, src_ap=src_ap:
                    h.bn_stats(
                        F["flnst"][:, si * 6:si * 6 + 6],
                        src_ap[:, si * 512:si * 512 + w]),
                    deps=extra_deps)
            ev_ag = self.dve(lambda h, nst=nst: h.bn_aggr(
                F["flnst"][:, 12:14], F["flnst"][:, 0:6 * nst]),
                deps=[ev_bs])
            ev_v = self.dve(lambda h: h.tensor_scalar_add(
                F["flnst"][:, 11:12], F["flnst"][:, 13:14], c.EPS),
                deps=[ev_ag])
            ev_ln = self.act(lambda h: h.activation(
                F["flnst"][:, 14:15], F["flnst"][:, 11:12], AF.Ln),
                deps=[ev_v])
            ev_ex = self.act(lambda h: h.activation(
                F["flnst"][:, 14:15], F["flnst"][:, 14:15], AF.Exp,
                scale=-0.5), deps=[ev_ln])
            ev_mu = self.dve(lambda h: h.tensor_mul(
                F["flnst"][:, 15:16], F["flnst"][:, 12:13],
                F["flnst"][:, 14:15]), deps=[ev_ex])
            ev = self.dve(lambda h: h.tensor_scalar_mul(
                F["flnst"][:, 15:16], F["flnst"][:, 15:16], -1.0),
                deps=[ev_mu])
            return (ev, ev_ex)

        ev_xn2T = None
        for ti in range(SPT):
            row = g * c.GT + ti * P
            ld_cb = self.dma(
                "sp", F["comb"][:, :],
                self.dram["comb_d"][row:row + P, :],
                [rs_ev, PF["comb_busy"]], "q_cb")
            ld_xp = self.dma(
                "sp", F["xp"][:, :],
                self.dram["x_post"][row:row + P, :],
                [PF["xp_busy"]], "q_cb2")
            ev, ev_x = ln_scale_bias(F["xp"], [ld_xp])
            ev_t1 = self.act(lambda h: h.activation(
                F["t1b"][:, :], F["xp"][:, :], AF.Identity,
                bias=F["flnst"][:, 15:16], scale=F["flnst"][:, 14:15]),
                deps=[ev, ev_x, PF.get("t1b_busy")])
            PF["flnst_busy"] = ev_t1
            ev_ab = self.dve(lambda h: h.tensor_add(
                F["comb"][:, :], F["t1b"][:, :], F["comb"][:, :]),
                deps=[ev_t1, ld_cb])
            ev, ev_x = ln_scale_bias(F["comb"], [ev_ab])
            ev_t2 = self.act(lambda h: h.activation(
                F["t1b"][:, :], F["comb"][:, :], AF.Identity,
                bias=F["flnst"][:, 15:16], scale=F["flnst"][:, 14:15]),
                deps=[ev, ev_x, ev_ab])
            PF["flnst_busy"] = ev_t2
            PF["comb_busy"] = ev_t2
            ev_x1 = self.dve(lambda h: h.tensor_add(
                F["x1b"][:, :], F["t1b"][:, :], F["xp"][:, :]),
                deps=[ev_t2] + PF["x1_guard"])
            PF["xp_busy"] = ev_x1
            PF["t1b_busy"] = ev_x1
            ev, ev_x = ln_scale_bias(F["x1b"], [ev_x1])
            ev_xn2 = self.act(lambda h: h.activation(
                F["xn2"][:, :], F["x1b"][:, :], AF.Identity,
                bias=F["flnst"][:, 15:16], scale=F["flnst"][:, 14:15]),
                deps=[ev, ev_x, ev_x1, PF["xn2_busy"]])
            PF["flnst_busy"] = ev_xn2
            ev_sp = self.dma(
                "sp", self.dram["x1_spill"][row:row + P, :],
                F["x1b"][:, :], [ev_x1], "q_x1")
            PF["x1_guard"] = [ev_sp]
            if c.debug:
                PF["x1_guard"].append(self.dma(
                    "sp", self.dram["dbg_x1"][row:row + P, :],
                    F["x1b"][:, :], [ev_x1], "q_dbg"))
            for dmb in range(c.DMB):
                evt = self.pe(lambda h, dmb=dmb: h.transpose(
                    self.ps_tr[:],
                    F["xn2"][:, dmb * P:(dmb + 1) * P],
                    self.ident_b[:]),
                    deps=[ev_xn2, self._m["tr_prev"], self.ev_id])
                ev_xn2T = self.act(lambda h, dmb=dmb, ti=ti:
                    h.activation(
                        F["xn2T"][:, dmb * c.GT + ti * P:
                                  dmb * c.GT + ti * P + P],
                        self.ps_tr[:], AF.Copy), deps=[evt])
                self._m["tr_prev"] = ev_xn2T
            PF["xn2_busy"] = ev_xn2T
            yield
            gti = (g * c.GT) // P + ti
            pairs = []
            for dmb in range(c.DMB):
                pairs.append((
                    lambda dmb=dmb, ti=ti:
                    F["xn2T"][:, dmb * c.GT + ti * P:
                              dmb * c.GT + ti * P + P],
                    lambda dmb=dmb: self.rw_sb[:, dmb * c.E:
                                               (dmb + 1) * c.E]))
            evm = self.mm_chain(self.ps_mm[:, 0:c.E], pairs,
                                deps=[ev_xn2T, self.ev_rw,
                                      self._m["bank_free"][0],
                                      PF["gate_done"]])
            evl = self.dve(lambda h: h.tensor_add(
                F["lg"][:, :], self.ps_mm[:, 0:c.E], self.rb_bc[:, 0:c.E]),
                deps=[evm, self.ev_rb])
            self._m["bank_free"][0] = evl
            r1 = self.dve(lambda h: h.tensor_reduce(
                F["gcl"][:, 0:1], F["lg"][:, :], AX.X, Op.max),
                deps=[evl])
            r2 = self.dve(lambda h: h.tensor_scalar(
                F["lgm"][:, :], F["lg"][:, :], F["gcl"][:, 0:1], None,
                Op.is_ge), deps=[r1])
            r3 = self.dve(lambda h: h.scalar_tensor_tensor(
                F["lgm"][:, :], F["lgm"][:, :], -1e30, F["lg"][:, :],
                Op.mult, Op.add), deps=[r2])
            r4 = self.dve(lambda h: h.tensor_reduce(
                F["gcl"][:, 1:2], F["lgm"][:, :], AX.X, Op.max),
                deps=[r3])
            ev5 = self.dve(lambda h: h.tensor_scalar_mul(
                F["gcl"][:, 2:3], F["gcl"][:, 0:1], -1.0), deps=[r1])
            ev6 = self.dve(lambda h: h.tensor_tensor(
                F["gcl"][:, 3:4], F["gcl"][:, 1:2], F["gcl"][:, 2:3],
                Op.add), deps=[r4, ev5])
            ev7 = self.act(lambda h: h.activation(
                F["lgn"][:, :], F["lg"][:, :], AF.Exp,
                bias=F["gcl"][:, 2:3]), deps=[ev5])
            ev8 = self.act(lambda h: h.activation(
                F["gcl"][:, 4:5], F["gcl"][:, 3:4], AF.Exp), deps=[ev6])
            ev9 = self.act(lambda h: h.activation(
                F["gcl"][:, 5:6], F["gcl"][:, 4:5], AF.Ln, bias=1.0),
                deps=[ev8])
            ev10 = self.act(lambda h: h.activation(
                F["gcl"][:, 5:6], F["gcl"][:, 5:6], AF.Exp, scale=-1.0),
                deps=[ev9])
            ev11 = self.dve(lambda h: h.tensor_scalar(
                F["lgm"][:, :], F["lg"][:, :], F["gcl"][:, 1:2], None,
                Op.is_ge), deps=[r4])
            ev12 = self.dve(lambda h: h.tensor_mul(
                F["lgn"][:, :], F["lgn"][:, :], F["lgm"][:, :]),
                deps=[ev7, ev11])
            PF["gate_done"] = self.dve(lambda h, gti=gti: h.tensor_scalar(
                self.gates_all[:, gti * c.E:(gti + 1) * c.E],
                F["lgn"][:, :], F["gcl"][:, 5:6], None, Op.mult),
                deps=[ev10, ev12])
            yield
        ev_gate_done = PF["gate_done"]

        if c.debug:
            ti0 = g * c.GT // P
            self.dma("sp",
                     self.dram["dbg_gates"][ti0:ti0 + SPT, :, :]
                     .rearrange("t p e -> p t e"),
                     self.gates_all[:, ti0 * c.E:(ti0 + SPT) * c.E]
                     .rearrange("p (t e) -> p t e", t=SPT),
                     [ev_gate_done], "q_dbgg")

        ev_facc = None
        for e in range(c.E):
            for fq in range(c.NFQ):
                first = (e == 0 and fq == 0)
                stage = e * c.NFQ + fq
                ev_h_last = None
                ld2 = self.dma(
                    "sp" if c.PFADV == 0 else "pool",
                    F["w2r"][:, 0:c.FB * c.Dm],
                    self.dram["w2_b"][stage * P:(stage + 1) * P, :],
                    [PF["w2_free"], self.ev_cd, self.ev_rc[stage]],
                    "q_w2r")
                for fb in range(c.FB):
                    sl3 = PF["slab"] % 2
                    PF["slab"] += 1
                    so = sl3 * 2 * c.DMB * P
                    bo = fb * 2 * c.DMB * P
                    HW = c.DMB * P
                    ld13a = self.dma(
                        "pool", F["w13s"][:, so:so + HW],
                        self.dram["w13_b"][stage * P:(stage + 1) * P,
                                           bo:bo + HW],
                        [PF["slab_free"][sl3], self.ev_rc[stage]],
                        "q_w" + str(sl3))
                    ld13 = self.dma(
                        "pool", F["w13s"][:, so + HW:so + 2 * HW],
                        self.dram["w13_b"][stage * P:(stage + 1) * P,
                                           bo + HW:bo + 2 * HW],
                        [PF["slab_free"][sl3], self.ev_rc[stage]],
                        "q_wb" + str(sl3))
                    for th in range(cdiv(c.GT, 512)):
                        tw = min(512, c.GT - th * 512)
                        hpar = (fb * cdiv(c.GT, 512) + th) % 2
                        bank1 = [self.ps_bc[0], self.ps_h3][hpar]
                        bank3 = [self.ps_bc[1], self.ps_f[1]][hpar]
                        pairs1, pairs3 = [], []
                        for dmb in range(c.DMB):
                            pairs1.append((
                                lambda so=so, dmb=dmb:
                                F["w13s"][:, so + dmb * P:
                                          so + dmb * P + P],
                                lambda dmb=dmb, th=th, tw=tw:
                                F["xn2T"][:, dmb * c.GT + th * 512:
                                          dmb * c.GT + th * 512 + tw]))
                            pairs3.append((
                                lambda so=so, dmb=dmb:
                                F["w13s"][:, so + c.DMB * P + dmb * P:
                                          so + c.DMB * P + dmb * P + P],
                                lambda dmb=dmb, th=th, tw=tw:
                                F["xn2T"][:, dmb * c.GT + th * 512:
                                          dmb * c.GT + th * 512 + tw]))
                        evm1 = self.mm_chain(
                            bank1[:, 0:tw], pairs1,
                            deps=[(ld13a if th == 0 else None), ev_xn2T,
                                  PF["si_busy"].get(hpar),
                                  self._m["selbank_busy"].get(
                                      "b" + str(hpar))])
                        ev_si = self.act(lambda h, tw=tw, hpar=hpar,
                                         bank1=bank1: h.activation(
                            F["silu_t"][:, hpar * 512:hpar * 512 + tw],
                            bank1[:, 0:tw], AF.Silu), deps=[evm1])
                        PF["si_busy"][hpar] = ev_si
                        yield
                        evm3 = self.mm_chain(
                            bank3[:, 0:tw], pairs3,
                            deps=[(ld13 if th == 0 else None),
                                  PF["h_busy"].get(hpar),
                                  self._m["selbank_busy"].get(
                                      "c" + str(hpar))])
                        ev_h_last = self.dve(
                            lambda h, fb=fb, th=th, tw=tw, hpar=hpar,
                            bank3=bank3: h.tensor_mul(
                                F["h_q"][:, fb * c.GT + th * 512:
                                         fb * c.GT + th * 512 + tw],
                                F["silu_t"][:, hpar * 512:
                                            hpar * 512 + tw],
                                bank3[:, 0:tw]),
                            deps=[ev_si, evm3, PF["hq_free"]])
                        PF["h_busy"][hpar] = ev_h_last
                    PF["slab_free"][sl3] = ev_h_last
                    yield
                for tt in range(SPT):
                    gti = g * SPT + tt
                    for half in range(cdiv(c.Dm, 512)):
                        wcols = min(512, c.Dm - half * 512)
                        wrot = ([self.ps_f[0], self.ps_mm, self.ps_h1]
                                if c.PFADV == 0 else [self.ps_f[0]])
                        bidx = PF["w2rot"] % len(wrot)
                        PF["w2rot"] += 1
                        bank = wrot[bidx]
                        pairs = []
                        for fb in range(c.FB):
                            pairs.append((
                                lambda fb=fb, tt=tt:
                                F["h_q"][:, fb * c.GT + tt * P:
                                         fb * c.GT + tt * P + P],
                                lambda fb=fb, half=half, wcols=wcols:
                                F["w2r"][:, fb * c.Dm + half * 512:
                                         fb * c.Dm + half * 512
                                         + wcols]))
                        evm = self.mm_chain(
                            bank[:, 0:wcols], pairs,
                            deps=[ev_h_last, ld2,
                                  PF["psf_busy"].get(bidx),
                                  (self._m["bank_free"][bidx - 1]
                                   if bidx > 0 else None)])
                        go = gti * c.E + e
                        if first:
                            ev_facc = self.dve(
                                lambda h, tt=tt, half=half, wcols=wcols,
                                bank=bank, go=go: h.tensor_scalar(
                                    F["f_acc"][:, tt * c.Dm + half * 512:
                                               tt * c.Dm + half * 512
                                               + wcols],
                                    bank[:, 0:wcols],
                                    self.gates_all[:, go:go + 1], None,
                                    Op.mult),
                                deps=[evm, ev_gate_done,
                                      PF["facc_free"]])
                        else:
                            ev_facc = self.dve(
                                lambda h, tt=tt, half=half, wcols=wcols,
                                bank=bank, go=go:
                                h.scalar_tensor_tensor(
                                    F["f_acc"][:, tt * c.Dm + half * 512:
                                               tt * c.Dm + half * 512
                                               + wcols],
                                    bank[:, 0:wcols],
                                    self.gates_all[:, go:go + 1],
                                    F["f_acc"][:, tt * c.Dm + half * 512:
                                               tt * c.Dm + half * 512
                                               + wcols],
                                    Op.mult, Op.add),
                                deps=[evm, ev_gate_done,
                                      PF["facc_ev"].get((tt, half))])
                        PF["psf_busy"][bidx] = ev_facc
                        PF["facc_ev"][(tt, half)] = ev_facc
                        if bidx > 0:
                            self._m["bank_free"][bidx - 1] = ev_facc
                        yield
                PF["hq_free"] = ev_facc
                PF["w2_free"] = ev_facc

        for ti in range(SPT):
            row = g * c.GT + ti * P
            ld_x1 = self.dma(
                "sp", F["x1b"][:, :],
                self.dram["x1_spill"][row:row + P, :],
                [("q_x1", self.counts["q_x1"]), ev_facc,
                 PF["out_last"]] +
                ([("q_dbg", self.counts.get("q_dbg", 0))]
                 if c.debug else []), "q_x1r")
            fat = F["f_acc"][:, ti * c.Dm:(ti + 1) * c.Dm]
            ev, ev_x = ln_scale_bias(fat, [ev_facc])
            ev_t4 = self.act(lambda h, fat=fat: h.activation(
                F["t1b"][:, :], fat, AF.Identity,
                bias=F["flnst"][:, 15:16], scale=F["flnst"][:, 14:15]),
                deps=[ev, ev_x, PF["fout_free"], PF.get("t1b_busy")])
            PF["flnst_busy"] = ev_t4
            ev_last = self.dve(lambda h: h.tensor_add(
                F["t1b"][:, :], F["t1b"][:, :], F["x1b"][:, :]),
                deps=[ev_t4, ld_x1])
            PF["t1b_busy"] = ev_last
            ev_out = self.dma(
                "sp", self.dram["out_post"][row:row + P, :],
                F["t1b"][:, :], [ev_last], "q_out")
            PF["fout_free"] = ev_out
            PF["out_last"] = ev_out
            PF["t1b_busy"] = ev_out
            self.n_out_dmas = self.counts["q_out"] // 16
            yield
        PF["facc_free"] = ev_t4

    def emit_final_wait(self):
        c = self.cfg
        deps = [("q_out", 16 * self.n_out_dmas)]
        if c.debug:
            for qq in ("q_dbg", "q_dbg1", "q_dbg2", "q_dbgp", "q_dbgg"):
                if self.counts.get(qq, 0):
                    deps.append((qq, self.counts[qq]))
        self.op("pool", lambda h: None, deps=deps)


_NC_CACHE = {}


def _get_nc(debug=False):
    key = ("full", debug)
    if key not in _NC_CACHE:
        _NC_CACHE[key] = Builder(CFG(debug=debug)).build()
    return _NC_CACHE[key]


def _tile_win(w_in, perm, lo, hi, cfg):
    DMB = cfg.DMB
    w_cat_T = np.concatenate([w_in[perm, :].T,
                              w_in[cfg.DI + lo:cfg.DI + hi, :].T], axis=1)
    ncol = w_cat_T.shape[1] // P
    t = w_cat_T.reshape(DMB, P, ncol, P).transpose(2, 0, 1, 3)
    return np.ascontiguousarray(t.reshape(ncol * DMB * P, P))


def make_in_maps(inputs, cfg):
    x = np.ascontiguousarray(inputs["L0_attn_x"], dtype=np.float32)
    B, L, Dm = x.shape
    DI, DH = cfg.DI, cfg.DH
    w_in = np.asarray(inputs["L0_attn_in_proj_w"], np.float32)
    E, NFQ, FB, DMB = cfg.E, cfg.NFQ, cfg.FB, cfg.DMB
    w1 = np.asarray(inputs["L0_ffn_w1"], np.float32)
    w3 = np.asarray(inputs["L0_ffn_w3"], np.float32)
    w2 = np.asarray(inputs["L0_ffn_w2"], np.float32)
    a1 = w1.reshape(E, DMB, P, NFQ, FB, P).transpose(0, 3, 2, 4, 1, 5)
    a3 = w3.reshape(E, DMB, P, NFQ, FB, P).transpose(0, 3, 2, 4, 1, 5)
    w13_tiled = np.ascontiguousarray(
        np.stack([a1, a3], axis=4).reshape(E * NFQ * P, -1))
    w2_tiled = np.ascontiguousarray(
        w2.reshape(E, NFQ, FB, P, cfg.Dm).transpose(0, 1, 3, 2, 4)
        .reshape(E * NFQ * P, -1))
    in_maps = []
    for c in range(8):
        b, D = c // 2, c % 2
        lo, hi = D * DH, (D + 1) * DH
        perm = np.concatenate([np.arange(lo, hi),
                               np.arange(0, lo),
                               np.arange(hi, DI)])
        xb = x[b]
        x_post = np.ascontiguousarray(
            xb.reshape(cfg.NCH, cfg.C, Dm)[D::2].reshape(cfg.TP, Dm))
        m = {
            "x_seq": np.ascontiguousarray(xb),
            "x_post": x_post,
            "w_in_tiled": _tile_win(w_in, perm, lo, hi, cfg),
            "conv_w": np.ascontiguousarray(
                np.asarray(inputs["L0_attn_conv_w"], np.float32)[perm]),
            "conv_b": np.ascontiguousarray(
                np.asarray(inputs["L0_attn_conv_b"], np.float32)[perm]),
            "x_proj_T": np.ascontiguousarray(
                np.asarray(inputs["L0_attn_x_proj_w"], np.float32)[:, perm].T),
            "dt_w_T": np.ascontiguousarray(
                np.asarray(inputs["L0_attn_dt_proj_w"],
                           np.float32)[lo:hi].T),
            "dt_b": np.ascontiguousarray(
                np.asarray(inputs["L0_attn_dt_proj_b"], np.float32)[lo:hi]),
            "A_log_h": np.ascontiguousarray(
                np.asarray(inputs["L0_attn_A_log"], np.float32)[lo:hi]),
            "Dp_h": np.ascontiguousarray(
                np.asarray(inputs["L0_attn_Dp"], np.float32)[lo:hi]),
            "op_w_T": np.ascontiguousarray(
                np.asarray(inputs["L0_attn_out_proj_w"],
                           np.float32)[:, lo:hi].T),
            "norm_w": np.asarray(inputs["L0_attn_norm_w"], np.float32),
            "router_w": np.asarray(inputs["L0_ffn_router_w"], np.float32),
            "router_b": np.asarray(inputs["L0_ffn_router_b"], np.float32),
            "w13_tiled": w13_tiled,
            "w2_tiled": w2_tiled,
        }
        in_maps.append(m)
    return in_maps


def kernel(**inputs):
    cfg = CFG()
    nc = _get_nc()
    in_maps = make_in_maps(inputs, cfg)
    res = run_bass_kernel_spmd(nc, in_maps, core_ids=list(range(8)))
    x = np.asarray(inputs["L0_attn_x"])
    B, L, Dm = x.shape
    out = np.empty((B, L, Dm), np.float32)
    for c in range(8):
        b, D = c // 2, c % 2
        op = res.results[c]["out_post"].reshape(cfg.TP // cfg.C, cfg.C, Dm)
        out.reshape(B, cfg.NCH, cfg.C, Dm)[b, D::2] = op
    return out


# version 28
# speedup vs baseline: 1.0440x; 1.0440x over previous
import numpy as np
import concourse.bass as bass
import concourse.mybir as mybir
from concourse.alu_op_type import AluOpType as Op
from concourse.bass_utils import run_bass_kernel_spmd

F32 = mybir.dt.float32
BF16 = mybir.dt.bfloat16
AF = mybir.ActivationFunctionType
AX = mybir.AxisListType
P = 128
ENGINES = ("sp", "act", "dve", "pe", "pool")
RG_PAIRS = [[0, 1], [2, 3], [4, 5], [6, 7]]


class CFG:
    def __init__(self, **kw):
        self.Dm = 1024; self.DI = 2048; self.NST = 16; self.RK = 64
        self.CONV = 4; self.F = 4096; self.E = 4; self.L = 4096
        self.C = 256
        self.GT = 512
        self.FQ = 1024
        self.CPG = 0
        self.PFADV = 0
        self.DBB = 2
        self.EPS = 1e-5
        self.debug = False
        for k, v in kw.items():
            setattr(self, k, v)
        self.DH = self.DI // 2
        self.NCH = self.L // self.C
        self.TP = self.L // 2
        self.TT = self.C // P
        self.DBLK = self.DH // P
        self.CBLK = self.DI // P
        self.DMB = self.Dm // P
        self.NG = self.TP // self.GT
        self.SPG = self.GT // self.C
        self.NFQ = self.F // self.FQ
        self.FB = self.FQ // P
        self.RKC = self.RK + 2 * self.NST
        self.CPG = self.NCH // self.NG
        self.DBB = min(self.DBB, self.DBLK)
        assert self.C % P == 0 and self.DH % P == 0 and self.Dm % P == 0
        assert self.TP % self.GT == 0 and self.GT % self.C == 0
        assert self.F % self.FQ == 0 and self.FQ % P == 0
        assert self.DBLK % self.DBB == 0
        assert self.RK <= P and self.NST * 2 + self.RK <= P
        assert self.NCH % 2 == 0


def cdiv(a, b):
    return (a + b - 1) // b


class Builder:
    def __init__(self, cfg: CFG):
        self.cfg = cfg
        self.nc = bass.Bass("TRN2", target_bir_lowering=False, num_devices=8)
        self.streams = {e: [] for e in ENGINES}
        self.counts = {}
        self.last_wait = {}
        self.sem_names = set()
        self.dram = {}

    def ev(self, sem, n=1):
        self.sem_names.add(sem)
        v = self.counts.get(sem, 0) + n
        self.counts[sem] = v
        return (sem, v)

    def op(self, eng, fn, deps=(), inc=None, inc_n=1):
        e = self.ev(inc, inc_n) if inc else None
        agg = {}
        for d in deps:
            if d is None:
                continue
            sem, val = d
            if val > 0:
                agg[sem] = max(agg.get(sem, 0), val)
        waits = []
        for sem, val in agg.items():
            key = (eng, sem)
            if self.last_wait.get(key, 0) < val:
                self.last_wait[key] = val
                waits.append((sem, val))

        def run(h, sems):
            for sem, val in waits:
                h.wait_ge(sems[sem], val)
            inst = fn(h)
            if e is not None and inst is not None:
                inst.then_inc(sems[e[0]], inc_n)

        self.streams[eng].append(run)
        return e

    def drain(self, eng):
        self.streams[eng].append(lambda h, sems: h.drain())

    def dma(self, eng, out_ap, in_ap, deps, qsem):
        return self.op(eng, lambda h: h.dma_start(out=out_ap, in_=in_ap),
                       deps=deps, inc=qsem, inc_n=16)

    def act(self, fn, deps=(), inc="s_act"):
        return self.op("act", fn, deps=deps, inc=inc)

    def dve(self, fn, deps=(), inc="s_dve"):
        return self.op("dve", fn, deps=deps, inc=inc)

    def pe(self, fn, deps=(), inc="s_pe"):
        return self.op("pe", fn, deps=deps, inc=inc)

    def pool(self, fn, deps=(), inc="s_pool"):
        return self.op("pool", fn, deps=deps, inc=inc)

    def mm_chain(self, bank_ap, pairs, deps, inc="s_pe"):
        n = len(pairs)
        ev = None
        for i, (st, mv) in enumerate(pairs):
            ev = self.pe(lambda h, st=st, mv=mv, i=i:
                         h.matmul(bank_ap, st(), mv(), start=(i == 0),
                                  stop=(i == n - 1)),
                         deps=deps if i == 0 else (),
                         inc=inc if i == n - 1 else None)
        return ev

    def din(self, name, shape, dtype=F32):
        self.dram[name] = self.nc.dram_tensor(
            name, list(shape), dtype, kind="ExternalInput").ap()

    def dout(self, name, shape, dtype=F32):
        self.dram[name] = self.nc.dram_tensor(
            name, list(shape), dtype, kind="ExternalOutput").ap()

    def dint(self, name, shape, dtype=F32):
        self.dram[name] = self.nc.dram_tensor(
            name, list(shape), dtype, kind="Internal").ap()

    def build(self):
        from contextlib import ExitStack
        c = self.cfg
        self.din("x_seq", (c.L, c.Dm))
        self.din("x_post", (c.TP, c.Dm))
        self.din("w_in_tiled", ((c.CBLK + c.DH // P) * (c.Dm // P) * P, P))
        self.din("conv_w", (c.DI, c.CONV))
        self.din("conv_b", (c.DI,))
        self.din("x_proj_T", (c.DI, c.RKC))
        self.din("dt_w_T", (c.RK, c.DH))
        self.din("dt_b", (c.DH,))
        self.din("A_log_h", (c.DH, c.NST))
        self.din("Dp_h", (c.DH,))
        self.din("op_w_T", (c.DH, c.Dm))
        self.din("norm_w", (c.Dm,))
        self.din("router_w", (c.Dm, c.E))
        self.din("router_b", (c.E,))
        NS = c.E * c.NFQ
        W13C = c.FB * 2 * c.DMB * P
        self.din("w13_tiled", (NS * P, W13C))
        self.din("w2_tiled", (NS * P, c.FB * c.Dm))
        self.dint("w13_b", (NS * P, W13C), BF16)
        self.dint("w2_b", (NS * P, c.FB * c.Dm), BF16)
        NCW = (c.CBLK + c.DBLK) * c.DMB * P
        self.dint("w_in_b", (NCW, P), BF16)
        self.dint("conv_d", (P, c.CBLK * c.CONV * P), BF16)
        self.dint("sel_d", (2 * c.NST, 2 * c.NST * P), BF16)
        self.dint("rs_in", (c.L, c.Dm))
        self.dint("comb_d", (c.TP, c.Dm))
        self.dint("x1_spill", (c.TP, c.Dm))
        self.dout("out_post", (c.TP, c.Dm))
        if c.debug:
            self.dout("dbg_xc", (c.NCH, P, c.CBLK * c.C))
            self.dout("dbg_delta", (c.NCH, P, c.DBLK * c.C))
            self.dout("dbg_y", (c.NCH, P, c.DBLK * c.C))
            self.dout("dbg_x1", (c.TP, c.Dm))
            self.dout("dbg_gates", (c.TP // P, P, c.E))

        with ExitStack() as es:
            self.es = es
            self.alloc_sbuf()
            self.emit_preload()
            self.emit_driver()
            self.emit_final_wait()
            sems = {}
            for name in sorted(self.sem_names):
                sems[name] = es.enter_context(self.nc.semaphore(name))
            block = es.enter_context(self.nc.Block())
            for attr, eng in [("sync", "sp"), ("scalar", "act"),
                              ("vector", "dve"), ("tensor", "pe"),
                              ("gpsimd", "pool")]:
                def replay(h, eng=eng):
                    for fn in self.streams[eng]:
                        fn(h, sems)
                getattr(block, attr)(replay)
        return self.nc

    def sb(self, name, shape, dtype):
        return self.es.enter_context(self.nc.sbuf_tensor(name, shape, dtype))

    def psum_t(self, name, shape, dtype=F32):
        return self.es.enter_context(self.nc.psum_tensor(name, shape, dtype))

    def alloc_sbuf(self):
        c = self.cfg
        nb = mybir.dt.size
        self.ident_b = self.sb("ident_b", [P, P], BF16)
        self.iota_f = self.sb("iota_f", [P, P], F32)
        self.ones_f = self.sb("ones_f", [1, P], F32)
        self.sel_r = self.sb("sel_r", [2 * c.NST, 4 * 2 * P], BF16)
        self.A_sb = self.sb("A_sb", [P, c.DBLK * c.NST], F32)
        self.convw_f = self.sb("convw_f", [P, c.CBLK * c.CONV], F32)
        self.convb_sb = self.sb("convb_sb", [P, c.CBLK], F32)
        self.dtb_sb = self.sb("dtb_sb", [P, c.DBLK], F32)
        self.dp_sb = self.sb("dp_sb", [P, c.DBLK], F32)
        self.nw_sb = self.sb("nw_sb", [P, c.DMB], F32)
        self.rw_sb = self.sb("rw_sb", [P, c.DMB * c.E], BF16)
        self.rb_bc = self.sb("rb_bc", [P, c.E], F32)
        self.hstate = self.sb("hstate", [P, c.DBLK * c.NST], F32)
        self.gates_all = self.sb("gates_all", [P, (c.TP // P) * c.E], F32)

        m_shapes = dict(
            x0_a=([P, c.TT * c.Dm], F32),
            xn_b=([P, c.Dm], BF16),
            xnT=([P, c.DMB * c.C], BF16),
            xi=([P, c.CBLK * (c.C + c.CONV - 1)], BF16),
            xcb=([P, (c.CBLK - c.DBLK) * c.C], BF16),
            xcf=([P, 2 * c.DBLK * c.C], BF16),
            z_b=([P, 2 * c.DBLK * c.C], BF16),
            delta=([P, 2 * c.DBLK * c.C], BF16),
            sp_tmp=([P, max(1, c.DBLK // 4) * c.C], F32),
            du=([P, c.DBLK * c.C], BF16),
            dA0=([P, c.DBB * c.C], F32), dA1=([P, c.DBB * c.C], F32),
            dBu=([P, c.DBB * c.C], F32),
            h_b0=([P, c.DBB * c.C], F32), h_b1=([P, c.DBB * c.C], F32),
            ymul=([P, c.DBB * c.C], F32),
            y_acc=([P, c.DBLK * c.C], F32),
            yg=([P, c.DBLK * c.C], BF16),
            dblb=([c.RK, c.C], BF16),
            bc16=([P, 2 * 3 * c.C], BF16),
            opev=([P, c.TT * c.Dm], F32),
            lnst=([P, 16], F32),
            w_in_ring=([P, 4 * c.DMB * P], BF16),
            opT=([P, c.DBLK * c.Dm], BF16),
            conv_ring=([P, 2 * c.CONV * P], BF16),
            xprojT=([P, c.CBLK * c.RKC], BF16),
            dtprojT=([c.RK, c.DBLK * P], BF16),
        )
        f_shapes = dict(
            comb=([P, c.Dm], F32), xp=([P, c.Dm], F32),
            t1b=([P, c.Dm], F32),
            x1b=([P, c.Dm], F32), xn2=([P, c.Dm], BF16),
            silu_t=([P, 2 * 512], BF16),
            lg=([P, c.E], F32), lgm=([P, c.E], F32),
            lgn=([P, c.E], F32),
            gcl=([P, 8], F32),
            flnst=([P, 16], F32),
            xn2T=([P, c.DMB * c.GT], BF16),
            f_acc=([P, (c.GT // P) * c.Dm], F32),
            h_q=([P, c.FB * c.GT], BF16),
            w2r=([P, max(8 * c.Dm, 2 * c.NST * P,
                          c.CBLK * c.CONV * P)], BF16),
            w13s=([P, 2 * 2 * c.DMB * P], BF16),
        )

        self.M = {k: self.sb("m_" + k, s, d) for k, (s, d) in
                  m_shapes.items()}
        self.F = {k: self.sb("f_" + k, s, d) for k, (s, d) in
                  f_shapes.items()}
        self.ps_mm = self.psum_t("ps_mm", [P, max(c.C, 512)])
        self.ps_tr = self.psum_t("ps_tr", [P, P], BF16)
        self.ps_bc = [self.psum_t("ps_bc0", [P, 512]),
                      self.psum_t("ps_bc1", [P, 512])]
        self.ps_h1 = self.psum_t("ps_h1", [P, 512])
        self.ps_h3 = self.psum_t("ps_h3", [P, 512])
        self.ps_f = [self.psum_t("ps_f0", [P, 512]),
                     self.psum_t("ps_f1", [P, 512])]

    def emit_preload(self):
        c, M = self.cfg, self.M
        ev_iota = self.pool(lambda h: h.iota(
            self.iota_f[:], [[-1, P]], base=0, channel_multiplier=1,
            allow_small_or_imprecise_dtypes=True))
        self.ev_id = self.dve(lambda h: h.tensor_single_scalar(
            self.ident_b[:], self.iota_f[:], 0.0, Op.is_equal),
            deps=[ev_iota])
        self.ev_ones = self.dve(lambda h: h.memset(self.ones_f[:], 1.0))
        NS2 = 2 * c.NST
        ev_sel = None
        for n in range(c.NST):
            for kk, hot in ((0, n), (1, c.NST + n)):
                col = 2 * n + kk
                evi = self.pool(lambda h, hot=hot: h.iota(
                    self.iota_f[0:NS2, 0:P], [[0, P]], base=-hot,
                    channel_multiplier=1,
                    allow_small_or_imprecise_dtypes=True),
                    deps=[ev_sel if ev_sel else self.ev_id])
                ev_sel = self.dve(lambda h, col=col: h.tensor_single_scalar(
                    self.F["w2r"][0:NS2, col * P:(col + 1) * P],
                    self.iota_f[0:NS2, 0:P], 0.0, Op.is_equal),
                    deps=[evi])
        self.ev_sel = self.dma(
            "sp", self.dram["sel_d"][:, :],
            self.F["w2r"][0:NS2, 0:NS2 * P], [ev_sel], "q_seld")

        def load_cols(dst, src, nblk):
            in_ap = self.dram[src].rearrange("(b p) -> p b", p=P)
            return self.op(
                "sp", lambda h, d=dst, i=in_ap: h.dma_start(
                    out=d[:, 0:nblk], in_=i, allow_slow_non_contiguous=True),
                deps=[], inc="q_pre", inc_n=16)

        pre = [load_cols(self.convb_sb, "conv_b", c.CBLK),
               load_cols(self.dtb_sb, "dt_b", c.DBLK),
               load_cols(self.dp_sb, "Dp_h", c.DBLK),
               load_cols(self.nw_sb, "norm_w", c.DMB),
               self.dma("sp",
                        self.A_sb[:, 0:c.DBLK * c.NST].rearrange(
                            "p (b n) -> p b n", b=c.DBLK),
                        self.dram["A_log_h"].rearrange(
                            "(b p) n -> p b n", p=P), [], "q_pre"),
               self.dma("sp",
                        self.convw_f[:, 0:c.CBLK * c.CONV].rearrange(
                            "p (b k) -> p b k", b=c.CBLK),
                        self.dram["conv_w"].rearrange(
                            "(b p) k -> p b k", p=P), [], "q_pre"),
               self.dma("sp", self.rb_bc[0:1, 0:c.E],
                        self.dram["router_b"].rearrange("(o e) -> o e", o=1),
                        [], "q_pre")]
        self.pre = pre
        ev = self.act(lambda h: h.activation(
            self.A_sb[:, 0:c.DBLK * c.NST], self.A_sb[:, 0:c.DBLK * c.NST],
            AF.Exp), deps=pre)
        self.ev_A = self.dve(lambda h: h.tensor_scalar_mul(
            self.A_sb[:, 0:c.DBLK * c.NST], self.A_sb[:, 0:c.DBLK * c.NST],
            -1.0), deps=[ev])
        ev = self.pe(lambda h: h.matmul(
            self.ps_mm[:, 0:c.E], self.ones_f[:], self.rb_bc[0:1, 0:c.E],
            start=True, stop=True), deps=pre + [self.ev_ones])
        self.ev_rb = self.act(lambda h: h.activation(
            self.rb_bc[:, 0:c.E], self.ps_mm[:, 0:c.E], AF.Copy), deps=[ev])
        self.ev_rw = self.dma("pool",
                              self.rw_sb[:, 0:c.DMB * c.E].rearrange(
                                  "p (b e) -> p b e", b=c.DMB),
                              self.dram["router_w"].rearrange(
                                  "(b p) e -> p b e", p=P), [], "q_pool")
        ev_cd = None
        ntile = c.CBLK * c.CONV
        assert ntile * P <= 8 * c.Dm
        for col in range(ntile):
            o = col * P
            ev_cd = self.dve(lambda h, o=o, col=col:
                h.tensor_scalar_mul(self.F["w2r"][:, o:o + P],
                                    self.ident_b[:],
                                    self.convw_f[:, col:col + 1]),
                deps=[self.ev_id, self.ev_sel] + pre)
        self.ev_cd = self.dma("sp", self.dram["conv_d"][:, 0:ntile * P],
                              self.F["w2r"][:, 0:ntile * P], [ev_cd],
                              "q_cvd")
        self.ev_h0 = self.dve(lambda h: h.memset(
            self.hstate[:, 0:c.DBLK * c.NST], 0.0))
        self.ev_halo0 = None
        for cblk in range(c.CBLK):
            o = cblk * (c.C + c.CONV - 1)
            self.ev_halo0 = self.dve(lambda h, o=o: h.memset(
                M["xi"][:, o:o + c.CONV - 1], 0.0))

        def wload3(dst_name, src, nb_):
            return self.dma("pool",
                            M[dst_name][:, :].rearrange("p (b o) -> p b o",
                                                        b=nb_),
                            self.dram[src].rearrange("(b p) o -> p b o",
                                                     p=P),
                            (), "q_pool")

        NWA = c.CBLK + c.DBLK
        self.ev_winb = self.dma(
            "pool",
            self.dram["w_in_b"].rearrange("(a x) p -> a (x p)", a=NWA),
            self.dram["w_in_tiled"].rearrange("(a x) p -> a (x p)", a=NWA),
            [], "q_winb")
        self.ev_rc = []
        self.ev_opw = wload3("opT", "op_w_T", c.DBLK)
        self.ev_xpw = wload3("xprojT", "x_proj_T", c.CBLK)
        self.ev_dtw = self.dma(
            "pool", M["dtprojT"][0:c.RK, :], self.dram["dt_w_T"][:, :],
            [], "q_pool")

    def emit_driver(self):
        c = self.cfg
        self._m = dict(
            x0_ev=None, x0_free=None,
            xi_tail=None, lnst_busy=None, tr_prev=None,
            bank_free={0: None, 1: None},
            wslab_free=[None] * 4, wslab=0,
            bc_free=[None, None], dA_free=[None, None], h_free=[None, None],
            cv_free=[None, None], sel_free=[None] * 4, hs_evs={},
            selbank_busy={},
            hs_prev=self.ev_h0, pf_free=None, partial_last=None,
            opmm_last=None, sp_free=None, dBu_busy=None,
            selmm_last=[None, None], dup_free=[None, None],
            prep=[None, None],
        )
        DONE = object()
        pf = []
        ng_emitted = 0

        def adv(gen):
            return next(gen, DONE) is DONE

        NS = c.E * c.NFQ
        W13C = c.FB * 2 * c.DMB * P
        W2C = c.FB * c.Dm
        dve_marks = []
        for j in range(c.NCH + 1):
            dve_marks.append(self.counts.get("s_dve", 0))
            while len(self.ev_rc) < NS:
                s = len(self.ev_rc)
                if j < c.NCH and s > 2 * (j - 6) - 1:
                    break
                qn = "q_rc" + str(s)
                gate = ("s_dve", dve_marks[max(j - 1, 0)])
                for k in range(4):
                    r0, r1 = s * P + k * P // 4, s * P + (k + 1) * P // 4
                    self.dma("pool",
                             self.dram["w13_b"][r0:r1, :],
                             self.dram["w13_tiled"][r0:r1, :], [gate], qn)
                for k in range(2):
                    r0, r1 = s * P + k * P // 2, s * P + (k + 1) * P // 2
                    ev = self.dma(
                        "pool", self.dram["w2_b"][r0:r1, :],
                        self.dram["w2_tiled"][r0:r1, :], [gate], qn)
                self.ev_rc.append(ev)
            pg = self.emit_prep(j) if j < c.NCH else None
            sg = self.emit_scan(j - 1) if j >= 1 else None
            done_p, done_s = pg is None, sg is None
            while not (done_p and done_s):
                if not done_s:
                    done_s = adv(sg)
                for _ in range(3):
                    if not done_p:
                        done_p = adv(pg)
                for _ in range(c.PFADV):
                    if pf and adv(pf[0]):
                        pf.pop(0)
            if j >= c.CPG and j % c.CPG == 0:
                g = j // c.CPG - 1
                rs_ev = self.pool(
                    lambda h, g=g: h.collective_compute(
                        "ReduceScatter", mybir.AluOpType.add, RG_PAIRS,
                        [self.dram["rs_in"][g * c.CPG * c.C:
                                            (g + 1) * c.CPG * c.C, :]],
                        [self.dram["comb_d"][g * c.GT:(g + 1) * c.GT, :]]),
                    deps=[("q_rs", 16 * c.CPG * (g + 1))], inc="s_pool")
                pf.append(self.emit_postffn_group(g, rs_ev))
                ng_emitted += 1
        while pf:
            if adv(pf[0]):
                pf.pop(0)

    def emit_prep(self, j):
        c, M, S = self.cfg, self.M, self._m
        x0 = M["x0_a"]
        sl = j % 2
        W = c.C + c.CONV - 1
        wait_weights = [self.ev_opw, self.ev_xpw, self.ev_dtw, self.ev_cd,
                        self.ev_A, self.ev_sel, self.ev_halo0]
        if j == 0:
            S["x0_ev"] = self.dma(
                "sp",
                x0[:, 0:c.TT * c.Dm].rearrange("p (t d) -> p t d", t=c.TT),
                self.dram["x_seq"][0:c.C, :].rearrange(
                    "(t p) d -> p t d", p=P),
                [], "q_x0")

        ev_xnT = None
        for tt in range(c.TT):
            xt = x0[:, tt * c.Dm:(tt + 1) * c.Dm]
            deps = [S["x0_ev"], S["lnst_busy"]]
            nst = cdiv(c.Dm, 512)
            ev_bs = None
            for si in range(nst):
                w = min(512, c.Dm - si * 512)
                ev_bs = self.dve(lambda h, xt=xt, si=si, w=w: h.bn_stats(
                    M["lnst"][:, si * 6:si * 6 + 6],
                    xt[:, si * 512:si * 512 + w]), deps=deps)
            ev_ag = self.dve(lambda h, nst=nst: h.bn_aggr(
                M["lnst"][:, 12:14], M["lnst"][:, 0:6 * nst]),
                deps=[ev_bs])
            ev_v = self.dve(lambda h: h.tensor_scalar(
                M["lnst"][:, 14:15], M["lnst"][:, 13:14], 1.0 + c.EPS,
                c.EPS * c.EPS, Op.mult, Op.add), deps=[ev_ag])
            ev_ln = self.act(lambda h: h.activation(
                M["lnst"][:, 14:15], M["lnst"][:, 14:15], AF.Ln),
                deps=[ev_v])
            ev_ex = self.act(lambda h: h.activation(
                M["lnst"][:, 14:15], M["lnst"][:, 14:15], AF.Exp,
                scale=-0.5), deps=[ev_ln])
            ev_ng = self.act(lambda h: h.activation(
                M["lnst"][:, 10:11], M["lnst"][:, 14:15], AF.Identity,
                scale=-1.0), deps=[ev_ex])
            ev = self.act(lambda h: h.activation(
                M["lnst"][:, 15:16], M["lnst"][:, 12:13], AF.Identity,
                scale=M["lnst"][:, 10:11]), deps=[ev_ng])
            ev_xn = self.act(lambda h, xt=xt: h.activation(
                M["xn_b"][:, 0:c.Dm], xt, AF.Identity,
                bias=M["lnst"][:, 15:16], scale=M["lnst"][:, 14:15]),
                deps=[ev, ev_ex, S["tr_prev"]])
            S["lnst_busy"] = ev_xn
            for dmb in range(c.DMB):
                evt = self.pe(lambda h, dmb=dmb: h.transpose(
                    self.ps_tr[:], M["xn_b"][:, dmb * P:(dmb + 1) * P],
                    self.ident_b[:]),
                    deps=[ev_xn, self.ev_id, S["tr_prev"]])
                ev_xnT = self.act(lambda h, dmb=dmb, tt=tt:
                    h.activation(
                        M["xnT"][:, dmb * c.C + tt * P:
                                 dmb * c.C + tt * P + P],
                        self.ps_tr[:], AF.Identity,
                        scale=self.nw_sb[:, dmb:dmb + 1]), deps=[evt])
                S["tr_prev"] = ev_xnT
            yield
        S["x0_free"] = ev_xnT
        if j + 1 < c.NCH:
            S["x0_ev"] = self.dma(
                "sp",
                x0[:, 0:c.TT * c.Dm].rearrange("p (t d) -> p t d", t=c.TT),
                self.dram["x_seq"][(j + 1) * c.C:(j + 2) * c.C, :]
                .rearrange("(t p) d -> p t d", p=P),
                [S["x0_free"]], "q_x0")

        ev_xi_ev = []
        ev_z = None
        doff = sl * c.DBLK * c.C
        NCOL = c.CBLK + c.DBLK
        NSP = min(4, c.DMB)
        BPS = c.DMB // NSP
        ld_map = {}

        def issue_wload(cg):
            ws = cg % 4
            so_l = ws * c.DMB * P
            evs = []
            for sp_i in range(NSP):
                evs.append(self.dma(
                    "sp",
                    M["w_in_ring"][:, so_l + sp_i * BPS * P:
                                   so_l + (sp_i + 1) * BPS * P].rearrange(
                        "p (b cc) -> p b cc", b=BPS),
                    self.dram["w_in_b"][
                        cg * c.DMB * P + sp_i * BPS * P:
                        cg * c.DMB * P + (sp_i + 1) * BPS * P, :]
                    .rearrange("(b p) cc -> p b cc", p=P),
                    [S["wslab_free"][ws], self.ev_winb],
                    "q_wi" + str(ws) + "_" + str(sp_i)))
            ld_map[cg] = evs

        issue_wload(0)
        if NCOL > 1:
            issue_wload(1)
        for colg in range(NCOL):
            if colg + 2 < NCOL:
                issue_wload(colg + 2)
            ws = colg % 4
            so = ws * c.DMB * P
            ld_evs = ld_map.pop(colg)
            bank = [self.ps_mm, self.ps_h1][colg % 2]
            pairs = []
            for dmb in range(c.DMB):
                pairs.append((
                    lambda so=so, dmb=dmb:
                    M["w_in_ring"][:, so + dmb * P:so + dmb * P + P],
                    lambda dmb=dmb: M["xnT"][:, dmb * c.C:
                                             dmb * c.C + c.C]))
            evm = self.mm_chain(
                bank[:, 0:c.C], pairs,
                deps=ld_evs + [ev_xnT,
                               S["bank_free"][colg % 2]] + wait_weights)
            S["wslab_free"][ws] = evm
            if colg < c.CBLK:
                ev = self.act(lambda h, colg=colg, bank=bank:
                    h.activation(
                        M["xi"][:, colg * W + c.CONV - 1:
                                colg * W + c.CONV - 1 + c.C],
                        bank[:, 0:c.C], AF.Copy),
                    deps=[evm, S["xi_tail"]])
                ev_xi_ev.append(ev)
            else:
                zb = colg - c.CBLK
                ev = self.act(lambda h, zb=zb, bank=bank, doff=doff:
                    h.activation(
                        M["z_b"][:, doff + zb * c.C:doff + (zb + 1) * c.C],
                        bank[:, 0:c.C], AF.Silu), deps=[evm])
                ev_z = ev
            S["bank_free"][colg % 2] = ev
            if colg % 2 == 1:
                yield
        S["prep_z"] = ev_z

        ev_xcb = []
        ev_xcf = {}
        for cblk in range(c.CBLK):
            cvs = cblk % 2
            co = cvs * c.CONV * P
            ld_cv = self.dma(
                "sp", M["conv_ring"][:, co:co + c.CONV * P],
                self.dram["conv_d"][:, cblk * c.CONV * P:
                                    (cblk + 1) * c.CONV * P],
                [S["cv_free"][cvs], self.ev_cd], "q_cv" + str(cvs))
            bank = [self.ps_mm, self.ps_h1][cblk % 2]
            pairs = []
            for k in range(c.CONV):
                o = co + k * P
                xo = cblk * W + k
                pairs.append((lambda o=o: M["conv_ring"][:, o:o + P],
                              lambda xo=xo: M["xi"][:, xo:xo + c.C]))
            evm = self.mm_chain(bank[:, 0:c.C], pairs,
                                deps=[ev_xi_ev[cblk], ev_xi_ev[-1], ld_cv,
                                      S["bank_free"][cblk % 2]])
            S["cv_free"][cvs] = evm
            if cblk < c.DBLK:
                ev2 = self.act(lambda h, cblk=cblk, bank=bank, doff=doff:
                    h.activation(
                        M["xcf"][:, doff + cblk * c.C:
                                 doff + (cblk + 1) * c.C],
                        bank[:, 0:c.C], AF.Silu,
                        bias=self.convb_sb[:, cblk:cblk + 1]),
                    deps=[evm, S["xcf_free"] if "xcf_free" in S else None,
                          getattr(self, "ev_dbg_xc", None)])
                ev_xcf[cblk] = ev2
                ev_xcb.append(ev2)
                S["bank_free"][cblk % 2] = ev2
            else:
                ev1 = self.act(lambda h, cblk=cblk, bank=bank:
                    h.activation(
                        M["xcb"][:, (cblk - c.DBLK) * c.C:
                                 (cblk - c.DBLK + 1) * c.C],
                        bank[:, 0:c.C], AF.Silu,
                        bias=self.convb_sb[:, cblk:cblk + 1]),
                    deps=[evm, getattr(self, "ev_dbg_xc", None)])
                ev_xcb.append(ev1)
                S["bank_free"][cblk % 2] = ev1
            if cblk % 2 == 1:
                yield
        ev_tail = None
        for cblk in range(c.CBLK):
            o = cblk * W
            ev_tail = self.dve(lambda h, o=o: h.tensor_copy(
                M["xi"][:, o:o + c.CONV - 1],
                M["xi"][:, o + c.C:o + c.C + c.CONV - 1]),
                deps=[ev_xcb[-1], ev_xi_ev[-1]])
        S["xi_tail"] = ev_tail

        bank = self.ps_mm
        pairs = []
        for cblk in range(c.CBLK):
            if cblk < c.DBLK:
                mv = (lambda cblk=cblk, doff=doff:
                      M["xcf"][:, doff + cblk * c.C:
                               doff + (cblk + 1) * c.C])
            else:
                mv = (lambda cblk=cblk:
                      M["xcb"][:, (cblk - c.DBLK) * c.C:
                               (cblk - c.DBLK + 1) * c.C])
            pairs.append((
                lambda cblk=cblk: M["xprojT"][:, cblk * c.RKC:
                                              (cblk + 1) * c.RKC],
                mv))
        evm = self.mm_chain(bank[0:c.RKC, 0:c.C], pairs,
                            deps=[ev_xcb[-1], S["bank_free"][0]])
        ev_dt = self.act(lambda h, bank=bank: h.activation(
            M["dblb"][0:c.RK, 0:c.C], bank[0:c.RK, 0:c.C], AF.Copy),
            deps=[evm])
        boff = sl * 3 * c.C
        ev_bcev = self.act(lambda h, bank=bank, boff=boff: h.activation(
            M["bc16"][c.RK:c.RK + 2 * c.NST, boff:boff + c.C],
            bank[c.RK:c.RK + 2 * c.NST, 0:c.C], AF.Copy),
            deps=[evm, S["dup_free"][sl]])
        ev_bc1 = self.dma(
            "act", M["bc16"][0:2 * c.NST, boff + c.C:boff + 2 * c.C],
            M["bc16"][c.RK:c.RK + 2 * c.NST, boff:boff + c.C],
            [ev_bcev], "q_bc" + ("a" if sl == 0 else "b"))
        ev_bc2 = self.dma(
            "act", M["bc16"][0:2 * c.NST, boff + 2 * c.C:boff + 3 * c.C],
            M["bc16"][c.RK:c.RK + 2 * c.NST, boff:boff + c.C],
            [ev_bcev], "q_bc" + ("a" if sl == 0 else "b"))
        S["prep_bc"] = (ev_bc1, ev_bc2)
        S["bank_free"][0] = ev_bcev

        HB = c.DBLK // 4 if c.DBLK >= 4 else c.DBLK
        ev_delta = None
        for hh in range(c.DBLK // HB):
            ev_dl = []
            for q in range(HB):
                dblk = hh * HB + q
                bank = [self.ps_mm, self.ps_h1][dblk % 2]
                evm = self.mm_chain(
                    bank[:, 0:c.C],
                    [(lambda dblk=dblk: M["dtprojT"][0:c.RK, dblk * P:
                                                     (dblk + 1) * P],
                      lambda: M["dblb"][0:c.RK, 0:c.C])],
                    deps=[ev_dt, S["bank_free"][dblk % 2]])
                eva = self.act(lambda h, q=q, dblk=dblk, bank=bank:
                    h.activation(
                        M["sp_tmp"][:, q * c.C:(q + 1) * c.C],
                        bank[:, 0:c.C], AF.Abs,
                        bias=self.dtb_sb[:, dblk:dblk + 1]),
                    deps=[evm, S["sp_free"]])
                evr = self.act(lambda h, dblk=dblk, bank=bank, doff=doff:
                    h.activation(
                        M["delta"][:, doff + dblk * c.C:
                                   doff + (dblk + 1) * c.C],
                        bank[:, 0:c.C], AF.Relu,
                        bias=self.dtb_sb[:, dblk:dblk + 1]),
                    deps=[evm, getattr(self, "ev_dbg_dl", None)])
                ev_dl.append((eva, evr))
                S["bank_free"][dblk % 2] = evr
            eves = []
            for q in range(HB):
                eves.append(self.act(lambda h, q=q: h.activation(
                    M["sp_tmp"][:, q * c.C:(q + 1) * c.C],
                    M["sp_tmp"][:, q * c.C:(q + 1) * c.C],
                    AF.Exp, scale=-1.0), deps=[ev_dl[q][0]]))
            evls = []
            for q in range(HB):
                evls.append(self.act(lambda h, q=q: h.activation(
                    M["sp_tmp"][:, q * c.C:(q + 1) * c.C],
                    M["sp_tmp"][:, q * c.C:(q + 1) * c.C],
                    AF.Ln, bias=1.0), deps=[eves[q]]))
            for q in range(HB):
                dblk = hh * HB + q
                ev_delta = self.dve(lambda h, q=q, dblk=dblk, doff=doff:
                    h.tensor_add(
                        M["delta"][:, doff + dblk * c.C:
                                   doff + (dblk + 1) * c.C],
                        M["delta"][:, doff + dblk * c.C:
                                   doff + (dblk + 1) * c.C],
                        M["sp_tmp"][:, q * c.C:(q + 1) * c.C]),
                    deps=[evls[q], ev_dl[q][1]])
            S["sp_free"] = ev_delta
            yield
        yield
        S["prep"][sl] = dict(delta=ev_delta, xcf=ev_xcf,
                             z=ev_z, bc=S["prep_bc"], xcb=ev_xcb[-1])

        if c.debug:
            self.ev_dbg_xc = self.dma(
                "pool", self.dram["dbg_xc"][j, :, 0:c.DBLK * c.C],
                M["xcf"][:, doff:doff + c.DBLK * c.C],
                [ev_xcb[-1]], "q_dbgp")
            self.ev_dbg_xc = self.dma(
                "pool", self.dram["dbg_xc"][j, :, c.DBLK * c.C:],
                M["xcb"][:, :], [ev_xcb[-1], self.ev_dbg_xc], "q_dbgp")
            self.ev_dbg_dl = self.dma(
                "pool", self.dram["dbg_delta"][j, :, :],
                M["delta"][:, doff:doff + c.DBLK * c.C], [ev_delta],
                "q_dbg1")

    def emit_scan(self, j):
        c, M, S = self.cfg, self.M, self._m
        sl = j % 2
        doff = sl * c.DBLK * c.C
        boff = sl * 3 * c.C
        pr = S["prep"][sl]
        dA = [M["dA0"], M["dA1"]]
        h_b = [M["h_b0"], M["h_b1"]]
        nbt = c.DBLK // c.DBB
        DBC = c.DBB * c.C
        ev_y = None
        ev_hs_new = None
        bcdep = pr["bc"]
        du_evs = []
        for bb in range(nbt):
            du_evs.append(self.dve(lambda h, bb=bb, doff=doff:
                h.tensor_mul(
                    M["du"][:, bb * DBC:(bb + 1) * DBC],
                    M["delta"][:, doff + bb * DBC:doff + (bb + 1) * DBC],
                    M["xcf"][:, doff + bb * DBC:doff + (bb + 1) * DBC]),
                deps=[pr["delta"],
                      pr["xcf"].get(bb * c.DBB), pr["xcf"].get(
                          min(bb * c.DBB + c.DBB - 1, c.DBLK - 1))]))
        DBC2 = c.DBB * c.C
        hs_prev_map = S["hs_evs"]
        S["hs_evs"] = {}
        for n in range(c.NST):
            npar = n % 2
            np4 = n % 4
            ro = npar * DBC2
            so2 = np4 * 2 * P
            ld_sel = self.dma(
                "act", self.sel_r[:, so2:so2 + 2 * P],
                self.dram["sel_d"][:, 2 * n * P:2 * (n + 1) * P],
                [S["sel_free"][np4], self.ev_sel], "q_sel" + str(np4))
            bkB = [self.ps_bc[0], self.ps_h3][npar]
            bkC = [self.ps_bc[1], self.ps_f[1]][npar]
            kB, kC = "b" + str(npar), "c" + str(npar)
            evB = self.pe(lambda h, boff=boff, bkB=bkB, so2=so2: h.matmul(
                bkB[:, 0:2 * c.C],
                self.sel_r[:, so2:so2 + P],
                M["bc16"][0:2 * c.NST, boff + c.C:boff + 3 * c.C],
                start=True, stop=True),
                deps=[bcdep[0], bcdep[1], S["selbank_busy"].get(kB),
                      ld_sel])
            evC = self.pe(lambda h, boff=boff, bkC=bkC, so2=so2: h.matmul(
                bkC[:, 0:2 * c.C],
                self.sel_r[:, so2 + P:so2 + 2 * P],
                M["bc16"][0:2 * c.NST, boff + c.C:boff + 3 * c.C],
                start=True, stop=True),
                deps=[S["bc_free"][npar],
                      S["selbank_busy"].get(kC)])
            S["sel_free"][np4] = evC
            S["selmm_last"][sl] = evC
            for bb in range(nbt):
                par = (n * nbt + bb) % 2
                ev_dA = None
                for q in range(c.DBB):
                    dblk = bb * c.DBB + q
                    ev_dA = self.act(lambda h, q=q, dblk=dblk, n=n,
                                     par=par, doff=doff: h.activation(
                        dA[par][:, q * c.C:(q + 1) * c.C],
                        M["delta"][:, doff + dblk * c.C:
                                   doff + (dblk + 1) * c.C],
                        AF.Exp,
                        scale=self.A_sb[:, dblk * c.NST + n:
                                        dblk * c.NST + n + 1]),
                        deps=[pr["delta"], S["dA_free"][par]])
                evd = self.dve(lambda h, bb=bb, bkB=bkB:
                    h.tensor_mul(
                        M["dBu"][:, 0:DBC],
                        M["du"][:, bb * DBC:(bb + 1) * DBC],
                        bkB[:, 0:DBC]),
                    deps=[du_evs[bb], evB, S["h_free"][par],
                          S["dBu_busy"]])
                evd_last = evd
                evs = None
                for q in range(c.DBB):
                    dblk = bb * c.DBB + q
                    col = dblk * c.NST + n
                    evs = self.dve(lambda h, q=q, col=col, par=par:
                        h.tensor_tensor_scan(
                            h_b[par][:, q * c.C:(q + 1) * c.C],
                            dA[par][:, q * c.C:(q + 1) * c.C],
                            M["dBu"][:, q * c.C:(q + 1) * c.C],
                            self.hstate[:, col:col + 1],
                            Op.mult, Op.add),
                        deps=[ev_dA,
                              hs_prev_map.get((n, bb), self.ev_h0), evd])
                S["dA_free"][par] = evs
                S["dBu_busy"] = evs
                col0 = bb * c.DBB * c.NST + n
                ev_hs_new = self.act(lambda h, par=par, col0=col0:
                    h.activation(
                        self.hstate[:, col0:col0 + (c.DBB - 1) * c.NST
                                    + 1:c.NST],
                        h_b[par][:, c.C - 1:c.DBB * c.C:c.C], AF.Copy),
                    deps=[evs])
                S["hs_evs"][(n, bb)] = ev_hs_new
                if n == 0:
                    ev_y = self.dve(lambda h, bb=bb, bkC=bkC, par=par:
                        h.tensor_mul(
                            M["y_acc"][:, bb * DBC:(bb + 1) * DBC],
                            h_b[par][:, 0:DBC], bkC[:, 0:DBC]),
                        deps=[evs, evC, getattr(self, "ev_dbg_y", None)])
                else:
                    ev_ym = self.dve(lambda h, bkC=bkC, par=par:
                        h.tensor_mul(
                            M["ymul"][:, 0:DBC], h_b[par][:, 0:DBC],
                            bkC[:, 0:DBC]), deps=[evs, evC])
                    ev_y = self.dve(lambda h, bb=bb: h.tensor_add(
                        M["y_acc"][:, bb * DBC:(bb + 1) * DBC],
                        M["y_acc"][:, bb * DBC:(bb + 1) * DBC],
                        M["ymul"][:, 0:DBC]), deps=[ev_ym])
                S["h_free"][par] = ev_y
            S["bc_free"][npar] = ev_y
            S["selbank_busy"][kB] = evd_last
            S["selbank_busy"][kC] = ev_y
            yield
        S["dup_free"][sl] = S["selmm_last"][sl]

        zs_busy = [None, None]
        evg1 = None
        evg2 = None
        ev_yg = None
        for dblk in range(c.DBLK):
            zsl = dblk % 2
            ev_zs = self.act(lambda h, dblk=dblk, doff=doff, zsl=zsl:
                h.activation(
                    M["dBu"][:, zsl * c.C:(zsl + 1) * c.C],
                    M["z_b"][:, doff + dblk * c.C:doff + (dblk + 1) * c.C],
                    AF.Silu),
                deps=[pr["z"], ev_y, zs_busy[zsl]])
            evg1 = self.dve(lambda h, dblk=dblk, doff=doff, zsl=zsl:
                h.tensor_scalar(
                    M["ymul"][:, zsl * c.C:(zsl + 1) * c.C],
                    M["xcf"][:, doff + dblk * c.C:
                             doff + (dblk + 1) * c.C],
                    self.dp_sb[:, dblk:dblk + 1], None, Op.mult),
                deps=[ev_y])
            evg2 = self.dve(lambda h, dblk=dblk, zsl=zsl: h.tensor_add(
                M["y_acc"][:, dblk * c.C:(dblk + 1) * c.C],
                M["y_acc"][:, dblk * c.C:(dblk + 1) * c.C],
                M["ymul"][:, zsl * c.C:(zsl + 1) * c.C]), deps=[evg1])
            ev_yg = self.dve(lambda h, dblk=dblk, zsl=zsl: h.tensor_mul(
                M["yg"][:, dblk * c.C:(dblk + 1) * c.C],
                M["y_acc"][:, dblk * c.C:(dblk + 1) * c.C],
                M["dBu"][:, zsl * c.C:(zsl + 1) * c.C]),
                deps=[evg2, ev_zs, S["opmm_last"]])
            zs_busy[zsl] = ev_yg
            S["dBu_busy"] = ev_yg
            if dblk % 2 == 1:
                yield
        S["xcf_free"] = evg1
        ev_yg_last = ev_yg

        if c.debug:
            self.ev_dbg_y = self.dma(
                "sp", self.dram["dbg_y"][j, :, :],
                M["y_acc"][:, :], [evg2], "q_dbg2")

        ev_opev = []
        idx = 0
        for tt in range(c.TT):
            for half in range(cdiv(c.Dm, 512)):
                wcols = min(512, c.Dm - half * 512)
                bank = [self.ps_mm, self.ps_h1][idx % 2]
                pairs = []
                for cblk in range(c.DBLK):
                    pairs.append((
                        lambda cblk=cblk, tt=tt:
                        M["yg"][:, cblk * c.C + tt * P:
                                cblk * c.C + tt * P + P],
                        lambda cblk=cblk, half=half, wcols=wcols:
                        M["opT"][:, cblk * c.Dm + half * 512:
                                 cblk * c.Dm + half * 512 + wcols]))
                evm = self.mm_chain(bank[:, 0:wcols], pairs,
                                    deps=[ev_yg_last,
                                          S["bank_free"][idx % 2]])
                S["opmm_last"] = evm
                evo = self.act(lambda h, tt=tt, half=half, wcols=wcols,
                               bank=bank: h.activation(
                    M["opev"][:, tt * c.Dm + half * 512:
                              tt * c.Dm + half * 512 + wcols],
                    bank[:, 0:wcols], AF.Copy),
                    deps=[evm, S["partial_last"]])
                S["bank_free"][idx % 2] = evo
                ev_opev.append(evo)
                idx += 1
            yield
        rb = ((j // c.CPG) * c.CPG * c.C + (j % 2) * (c.CPG * c.C // 2)
              + ((j % c.CPG) // 2) * c.C)
        S["partial_last"] = self.dma(
            "sp",
            self.dram["rs_in"][rb:rb + c.C, :].rearrange(
                "(t p) d -> p t d", p=P),
            M["opev"][:, 0:c.TT * c.Dm].rearrange("p (t d) -> p t d",
                                                  t=c.TT),
            [ev_opev[-1]], "q_rs")

    def emit_postffn_group(self, g, rs_ev):
        c, F = self.cfg, self.F
        if not hasattr(self, "_pf"):
            self._pf = dict(flnst_busy=None, x1_guard=[], w2_free=None,
                            hq_free=None, slab=0, slab_free=[None] * 2,
                            si_busy={}, h_busy={}, psf_busy={},
                            facc_ev={}, w2rot=0,
                            facc_free=None, fout_free=None, out_last=None,
                            gate_done=None, step_free=None, h3_busy=None,
                            comb_busy=None, xp_busy=None, xn2_busy=None)
        PF = self._pf
        SPT = c.GT // P

        def ln_scale_bias(src_ap, extra_deps):
            extra_deps = list(extra_deps) + [PF["flnst_busy"]]
            nst = cdiv(c.Dm, 512)
            ev_bs = None
            for si in range(nst):
                w = min(512, c.Dm - si * 512)
                ev_bs = self.dve(lambda h, si=si, w=w, src_ap=src_ap:
                    h.bn_stats(
                        F["flnst"][:, si * 6:si * 6 + 6],
                        src_ap[:, si * 512:si * 512 + w]),
                    deps=extra_deps)
            ev_ag = self.dve(lambda h, nst=nst: h.bn_aggr(
                F["flnst"][:, 12:14], F["flnst"][:, 0:6 * nst]),
                deps=[ev_bs])
            ev_v = self.dve(lambda h: h.tensor_scalar_add(
                F["flnst"][:, 11:12], F["flnst"][:, 13:14], c.EPS),
                deps=[ev_ag])
            ev_ln = self.act(lambda h: h.activation(
                F["flnst"][:, 14:15], F["flnst"][:, 11:12], AF.Ln),
                deps=[ev_v])
            ev_ex = self.act(lambda h: h.activation(
                F["flnst"][:, 14:15], F["flnst"][:, 14:15], AF.Exp,
                scale=-0.5), deps=[ev_ln])
            ev_ng = self.act(lambda h: h.activation(
                F["flnst"][:, 10:11], F["flnst"][:, 14:15], AF.Identity,
                scale=-1.0), deps=[ev_ex])
            ev = self.act(lambda h: h.activation(
                F["flnst"][:, 15:16], F["flnst"][:, 12:13], AF.Identity,
                scale=F["flnst"][:, 10:11]), deps=[ev_ng])
            return (ev, ev_ex)

        ev_xn2T = None
        for ti in range(SPT):
            row = g * c.GT + ti * P
            ld_cb = self.dma(
                "sp", F["comb"][:, :],
                self.dram["comb_d"][row:row + P, :],
                [rs_ev, PF["comb_busy"]], "q_cb")
            ld_xp = self.dma(
                "sp", F["xp"][:, :],
                self.dram["x_post"][row:row + P, :],
                [PF["xp_busy"]], "q_cb2")
            ev, ev_x = ln_scale_bias(F["xp"], [ld_xp])
            ev_t1 = self.act(lambda h: h.activation(
                F["t1b"][:, :], F["xp"][:, :], AF.Identity,
                bias=F["flnst"][:, 15:16], scale=F["flnst"][:, 14:15]),
                deps=[ev, ev_x, PF.get("t1b_busy")])
            PF["flnst_busy"] = ev_t1
            ev_ab = self.dve(lambda h: h.tensor_add(
                F["comb"][:, :], F["t1b"][:, :], F["comb"][:, :]),
                deps=[ev_t1, ld_cb])
            ev, ev_x = ln_scale_bias(F["comb"], [ev_ab])
            ev_t2 = self.act(lambda h: h.activation(
                F["t1b"][:, :], F["comb"][:, :], AF.Identity,
                bias=F["flnst"][:, 15:16], scale=F["flnst"][:, 14:15]),
                deps=[ev, ev_x, ev_ab])
            PF["flnst_busy"] = ev_t2
            PF["comb_busy"] = ev_t2
            ev_x1 = self.dve(lambda h: h.tensor_add(
                F["x1b"][:, :], F["t1b"][:, :], F["xp"][:, :]),
                deps=[ev_t2] + PF["x1_guard"])
            PF["xp_busy"] = ev_x1
            PF["t1b_busy"] = ev_x1
            ev, ev_x = ln_scale_bias(F["x1b"], [ev_x1])
            ev_xn2 = self.act(lambda h: h.activation(
                F["xn2"][:, :], F["x1b"][:, :], AF.Identity,
                bias=F["flnst"][:, 15:16], scale=F["flnst"][:, 14:15]),
                deps=[ev, ev_x, ev_x1, PF["xn2_busy"]])
            PF["flnst_busy"] = ev_xn2
            ev_sp = self.dma(
                "sp", self.dram["x1_spill"][row:row + P, :],
                F["x1b"][:, :], [ev_x1], "q_x1")
            PF["x1_guard"] = [ev_sp]
            if c.debug:
                PF["x1_guard"].append(self.dma(
                    "sp", self.dram["dbg_x1"][row:row + P, :],
                    F["x1b"][:, :], [ev_x1], "q_dbg"))
            for dmb in range(c.DMB):
                evt = self.pe(lambda h, dmb=dmb: h.transpose(
                    self.ps_tr[:],
                    F["xn2"][:, dmb * P:(dmb + 1) * P],
                    self.ident_b[:]),
                    deps=[ev_xn2, self._m["tr_prev"], self.ev_id])
                ev_xn2T = self.act(lambda h, dmb=dmb, ti=ti:
                    h.activation(
                        F["xn2T"][:, dmb * c.GT + ti * P:
                                  dmb * c.GT + ti * P + P],
                        self.ps_tr[:], AF.Copy), deps=[evt])
                self._m["tr_prev"] = ev_xn2T
            PF["xn2_busy"] = ev_xn2T
            yield
            gti = (g * c.GT) // P + ti
            pairs = []
            for dmb in range(c.DMB):
                pairs.append((
                    lambda dmb=dmb, ti=ti:
                    F["xn2T"][:, dmb * c.GT + ti * P:
                              dmb * c.GT + ti * P + P],
                    lambda dmb=dmb: self.rw_sb[:, dmb * c.E:
                                               (dmb + 1) * c.E]))
            evm = self.mm_chain(self.ps_mm[:, 0:c.E], pairs,
                                deps=[ev_xn2T, self.ev_rw,
                                      self._m["bank_free"][0],
                                      PF["gate_done"]])
            evl = self.dve(lambda h: h.tensor_add(
                F["lg"][:, :], self.ps_mm[:, 0:c.E], self.rb_bc[:, 0:c.E]),
                deps=[evm, self.ev_rb])
            self._m["bank_free"][0] = evl
            r1 = self.dve(lambda h: h.tensor_reduce(
                F["gcl"][:, 0:1], F["lg"][:, :], AX.X, Op.max),
                deps=[evl])
            r2 = self.dve(lambda h: h.tensor_scalar(
                F["lgm"][:, :], F["lg"][:, :], F["gcl"][:, 0:1], None,
                Op.is_ge), deps=[r1])
            r3 = self.dve(lambda h: h.scalar_tensor_tensor(
                F["lgm"][:, :], F["lgm"][:, :], -1e30, F["lg"][:, :],
                Op.mult, Op.add), deps=[r2])
            r4 = self.dve(lambda h: h.tensor_reduce(
                F["gcl"][:, 1:2], F["lgm"][:, :], AX.X, Op.max),
                deps=[r3])
            ev5 = self.dve(lambda h: h.tensor_scalar_mul(
                F["gcl"][:, 2:3], F["gcl"][:, 0:1], -1.0), deps=[r1])
            ev6 = self.dve(lambda h: h.tensor_tensor(
                F["gcl"][:, 3:4], F["gcl"][:, 1:2], F["gcl"][:, 2:3],
                Op.add), deps=[r4, ev5])
            ev7 = self.act(lambda h: h.activation(
                F["lgn"][:, :], F["lg"][:, :], AF.Exp,
                bias=F["gcl"][:, 2:3]), deps=[ev5])
            ev8 = self.act(lambda h: h.activation(
                F["gcl"][:, 4:5], F["gcl"][:, 3:4], AF.Exp), deps=[ev6])
            ev9 = self.act(lambda h: h.activation(
                F["gcl"][:, 5:6], F["gcl"][:, 4:5], AF.Ln, bias=1.0),
                deps=[ev8])
            ev10 = self.act(lambda h: h.activation(
                F["gcl"][:, 5:6], F["gcl"][:, 5:6], AF.Exp, scale=-1.0),
                deps=[ev9])
            ev11 = self.dve(lambda h: h.tensor_scalar(
                F["lgm"][:, :], F["lg"][:, :], F["gcl"][:, 1:2], None,
                Op.is_ge), deps=[r4])
            ev12 = self.dve(lambda h: h.tensor_mul(
                F["lgn"][:, :], F["lgn"][:, :], F["lgm"][:, :]),
                deps=[ev7, ev11])
            PF["gate_done"] = self.dve(lambda h, gti=gti: h.tensor_scalar(
                self.gates_all[:, gti * c.E:(gti + 1) * c.E],
                F["lgn"][:, :], F["gcl"][:, 5:6], None, Op.mult),
                deps=[ev10, ev12])
            yield
        ev_gate_done = PF["gate_done"]

        if c.debug:
            ti0 = g * c.GT // P
            self.dma("sp",
                     self.dram["dbg_gates"][ti0:ti0 + SPT, :, :]
                     .rearrange("t p e -> p t e"),
                     self.gates_all[:, ti0 * c.E:(ti0 + SPT) * c.E]
                     .rearrange("p (t e) -> p t e", t=SPT),
                     [ev_gate_done], "q_dbgg")

        ev_facc = None
        for e in range(c.E):
            for fq in range(c.NFQ):
                first = (e == 0 and fq == 0)
                stage = e * c.NFQ + fq
                ev_h_last = None
                ld2 = self.dma(
                    "sp" if c.PFADV == 0 else "pool",
                    F["w2r"][:, 0:c.FB * c.Dm],
                    self.dram["w2_b"][stage * P:(stage + 1) * P, :],
                    [PF["w2_free"], self.ev_cd, self.ev_rc[stage]],
                    "q_w2r")
                for fb in range(c.FB):
                    sl3 = PF["slab"] % 2
                    PF["slab"] += 1
                    so = sl3 * 2 * c.DMB * P
                    bo = fb * 2 * c.DMB * P
                    HW = c.DMB * P
                    ld13a = self.dma(
                        "pool", F["w13s"][:, so:so + HW],
                        self.dram["w13_b"][stage * P:(stage + 1) * P,
                                           bo:bo + HW],
                        [PF["slab_free"][sl3], self.ev_rc[stage]],
                        "q_w" + str(sl3))
                    ld13 = self.dma(
                        "pool", F["w13s"][:, so + HW:so + 2 * HW],
                        self.dram["w13_b"][stage * P:(stage + 1) * P,
                                           bo + HW:bo + 2 * HW],
                        [PF["slab_free"][sl3], self.ev_rc[stage]],
                        "q_wb" + str(sl3))
                    for th in range(cdiv(c.GT, 512)):
                        tw = min(512, c.GT - th * 512)
                        hpar = (fb * cdiv(c.GT, 512) + th) % 2
                        bank1 = [self.ps_bc[0], self.ps_h3][hpar]
                        bank3 = [self.ps_bc[1], self.ps_f[1]][hpar]
                        pairs1, pairs3 = [], []
                        for dmb in range(c.DMB):
                            pairs1.append((
                                lambda so=so, dmb=dmb:
                                F["w13s"][:, so + dmb * P:
                                          so + dmb * P + P],
                                lambda dmb=dmb, th=th, tw=tw:
                                F["xn2T"][:, dmb * c.GT + th * 512:
                                          dmb * c.GT + th * 512 + tw]))
                            pairs3.append((
                                lambda so=so, dmb=dmb:
                                F["w13s"][:, so + c.DMB * P + dmb * P:
                                          so + c.DMB * P + dmb * P + P],
                                lambda dmb=dmb, th=th, tw=tw:
                                F["xn2T"][:, dmb * c.GT + th * 512:
                                          dmb * c.GT + th * 512 + tw]))
                        evm1 = self.mm_chain(
                            bank1[:, 0:tw], pairs1,
                            deps=[(ld13a if th == 0 else None), ev_xn2T,
                                  PF["si_busy"].get(hpar),
                                  self._m["selbank_busy"].get(
                                      "b" + str(hpar))])
                        ev_si = self.act(lambda h, tw=tw, hpar=hpar,
                                         bank1=bank1: h.activation(
                            F["silu_t"][:, hpar * 512:hpar * 512 + tw],
                            bank1[:, 0:tw], AF.Silu), deps=[evm1])
                        PF["si_busy"][hpar] = ev_si
                        yield
                        evm3 = self.mm_chain(
                            bank3[:, 0:tw], pairs3,
                            deps=[(ld13 if th == 0 else None),
                                  PF["h_busy"].get(hpar),
                                  self._m["selbank_busy"].get(
                                      "c" + str(hpar))])
                        ev_h_last = self.dve(
                            lambda h, fb=fb, th=th, tw=tw, hpar=hpar,
                            bank3=bank3: h.tensor_mul(
                                F["h_q"][:, fb * c.GT + th * 512:
                                         fb * c.GT + th * 512 + tw],
                                F["silu_t"][:, hpar * 512:
                                            hpar * 512 + tw],
                                bank3[:, 0:tw]),
                            deps=[ev_si, evm3, PF["hq_free"]])
                        PF["h_busy"][hpar] = ev_h_last
                    PF["slab_free"][sl3] = ev_h_last
                    yield
                for tt in range(SPT):
                    gti = g * SPT + tt
                    for half in range(cdiv(c.Dm, 512)):
                        wcols = min(512, c.Dm - half * 512)
                        wrot = ([self.ps_f[0], self.ps_mm, self.ps_h1]
                                if c.PFADV == 0 else [self.ps_f[0]])
                        bidx = PF["w2rot"] % len(wrot)
                        PF["w2rot"] += 1
                        bank = wrot[bidx]
                        pairs = []
                        for fb in range(c.FB):
                            pairs.append((
                                lambda fb=fb, tt=tt:
                                F["h_q"][:, fb * c.GT + tt * P:
                                         fb * c.GT + tt * P + P],
                                lambda fb=fb, half=half, wcols=wcols:
                                F["w2r"][:, fb * c.Dm + half * 512:
                                         fb * c.Dm + half * 512
                                         + wcols]))
                        evm = self.mm_chain(
                            bank[:, 0:wcols], pairs,
                            deps=[ev_h_last, ld2,
                                  PF["psf_busy"].get(bidx),
                                  (self._m["bank_free"][bidx - 1]
                                   if bidx > 0 else None)])
                        go = gti * c.E + e
                        if first:
                            ev_facc = self.dve(
                                lambda h, tt=tt, half=half, wcols=wcols,
                                bank=bank, go=go: h.tensor_scalar(
                                    F["f_acc"][:, tt * c.Dm + half * 512:
                                               tt * c.Dm + half * 512
                                               + wcols],
                                    bank[:, 0:wcols],
                                    self.gates_all[:, go:go + 1], None,
                                    Op.mult),
                                deps=[evm, ev_gate_done,
                                      PF["facc_free"]])
                        else:
                            ev_facc = self.dve(
                                lambda h, tt=tt, half=half, wcols=wcols,
                                bank=bank, go=go:
                                h.scalar_tensor_tensor(
                                    F["f_acc"][:, tt * c.Dm + half * 512:
                                               tt * c.Dm + half * 512
                                               + wcols],
                                    bank[:, 0:wcols],
                                    self.gates_all[:, go:go + 1],
                                    F["f_acc"][:, tt * c.Dm + half * 512:
                                               tt * c.Dm + half * 512
                                               + wcols],
                                    Op.mult, Op.add),
                                deps=[evm, ev_gate_done,
                                      PF["facc_ev"].get((tt, half))])
                        PF["psf_busy"][bidx] = ev_facc
                        PF["facc_ev"][(tt, half)] = ev_facc
                        if bidx > 0:
                            self._m["bank_free"][bidx - 1] = ev_facc
                        yield
                PF["hq_free"] = ev_facc
                PF["w2_free"] = ev_facc

        for ti in range(SPT):
            row = g * c.GT + ti * P
            ld_x1 = self.dma(
                "sp", F["x1b"][:, :],
                self.dram["x1_spill"][row:row + P, :],
                [("q_x1", self.counts["q_x1"]), ev_facc,
                 PF["out_last"]] +
                ([("q_dbg", self.counts.get("q_dbg", 0))]
                 if c.debug else []), "q_x1r")
            fat = F["f_acc"][:, ti * c.Dm:(ti + 1) * c.Dm]
            ev, ev_x = ln_scale_bias(fat, [ev_facc])
            ev_t4 = self.act(lambda h, fat=fat: h.activation(
                F["t1b"][:, :], fat, AF.Identity,
                bias=F["flnst"][:, 15:16], scale=F["flnst"][:, 14:15]),
                deps=[ev, ev_x, PF["fout_free"], PF.get("t1b_busy")])
            PF["flnst_busy"] = ev_t4
            ev_last = self.dve(lambda h: h.tensor_add(
                F["t1b"][:, :], F["t1b"][:, :], F["x1b"][:, :]),
                deps=[ev_t4, ld_x1])
            PF["t1b_busy"] = ev_last
            ev_out = self.dma(
                "sp", self.dram["out_post"][row:row + P, :],
                F["t1b"][:, :], [ev_last], "q_out")
            PF["fout_free"] = ev_out
            PF["out_last"] = ev_out
            PF["t1b_busy"] = ev_out
            self.n_out_dmas = self.counts["q_out"] // 16
            yield
        PF["facc_free"] = ev_t4

    def emit_final_wait(self):
        c = self.cfg
        deps = [("q_out", 16 * self.n_out_dmas)]
        if c.debug:
            for qq in ("q_dbg", "q_dbg1", "q_dbg2", "q_dbgp", "q_dbgg"):
                if self.counts.get(qq, 0):
                    deps.append((qq, self.counts[qq]))
        self.op("pool", lambda h: None, deps=deps)


_NC_CACHE = {}


def _get_nc(debug=False):
    key = ("full", debug)
    if key not in _NC_CACHE:
        _NC_CACHE[key] = Builder(CFG(debug=debug)).build()
    return _NC_CACHE[key]


def _tile_win(w_in, perm, lo, hi, cfg):
    DMB = cfg.DMB
    w_cat_T = np.concatenate([w_in[perm, :].T,
                              w_in[cfg.DI + lo:cfg.DI + hi, :].T], axis=1)
    ncol = w_cat_T.shape[1] // P
    t = w_cat_T.reshape(DMB, P, ncol, P).transpose(2, 0, 1, 3)
    return np.ascontiguousarray(t.reshape(ncol * DMB * P, P))


def make_in_maps(inputs, cfg):
    x = np.ascontiguousarray(inputs["L0_attn_x"], dtype=np.float32)
    B, L, Dm = x.shape
    DI, DH = cfg.DI, cfg.DH
    w_in = np.asarray(inputs["L0_attn_in_proj_w"], np.float32)
    E, NFQ, FB, DMB = cfg.E, cfg.NFQ, cfg.FB, cfg.DMB
    w1 = np.asarray(inputs["L0_ffn_w1"], np.float32)
    w3 = np.asarray(inputs["L0_ffn_w3"], np.float32)
    w2 = np.asarray(inputs["L0_ffn_w2"], np.float32)
    a1 = w1.reshape(E, DMB, P, NFQ, FB, P).transpose(0, 3, 2, 4, 1, 5)
    a3 = w3.reshape(E, DMB, P, NFQ, FB, P).transpose(0, 3, 2, 4, 1, 5)
    w13_tiled = np.ascontiguousarray(
        np.stack([a1, a3], axis=4).reshape(E * NFQ * P, -1))
    w2_tiled = np.ascontiguousarray(
        w2.reshape(E, NFQ, FB, P, cfg.Dm).transpose(0, 1, 3, 2, 4)
        .reshape(E * NFQ * P, -1))
    in_maps = []
    for c in range(8):
        b, D = c // 2, c % 2
        lo, hi = D * DH, (D + 1) * DH
        perm = np.concatenate([np.arange(lo, hi),
                               np.arange(0, lo),
                               np.arange(hi, DI)])
        xb = x[b]
        x_post = np.ascontiguousarray(
            xb.reshape(cfg.NCH, cfg.C, Dm)[D::2].reshape(cfg.TP, Dm))
        m = {
            "x_seq": np.ascontiguousarray(xb),
            "x_post": x_post,
            "w_in_tiled": _tile_win(w_in, perm, lo, hi, cfg),
            "conv_w": np.ascontiguousarray(
                np.asarray(inputs["L0_attn_conv_w"], np.float32)[perm]),
            "conv_b": np.ascontiguousarray(
                np.asarray(inputs["L0_attn_conv_b"], np.float32)[perm]),
            "x_proj_T": np.ascontiguousarray(
                np.asarray(inputs["L0_attn_x_proj_w"], np.float32)[:, perm].T),
            "dt_w_T": np.ascontiguousarray(
                np.asarray(inputs["L0_attn_dt_proj_w"],
                           np.float32)[lo:hi].T),
            "dt_b": np.ascontiguousarray(
                np.asarray(inputs["L0_attn_dt_proj_b"], np.float32)[lo:hi]),
            "A_log_h": np.ascontiguousarray(
                np.asarray(inputs["L0_attn_A_log"], np.float32)[lo:hi]),
            "Dp_h": np.ascontiguousarray(
                np.asarray(inputs["L0_attn_Dp"], np.float32)[lo:hi]),
            "op_w_T": np.ascontiguousarray(
                np.asarray(inputs["L0_attn_out_proj_w"],
                           np.float32)[:, lo:hi].T),
            "norm_w": np.asarray(inputs["L0_attn_norm_w"], np.float32),
            "router_w": np.asarray(inputs["L0_ffn_router_w"], np.float32),
            "router_b": np.asarray(inputs["L0_ffn_router_b"], np.float32),
            "w13_tiled": w13_tiled,
            "w2_tiled": w2_tiled,
        }
        in_maps.append(m)
    return in_maps


def kernel(**inputs):
    cfg = CFG()
    nc = _get_nc()
    in_maps = make_in_maps(inputs, cfg)
    res = run_bass_kernel_spmd(nc, in_maps, core_ids=list(range(8)))
    x = np.asarray(inputs["L0_attn_x"])
    B, L, Dm = x.shape
    out = np.empty((B, L, Dm), np.float32)
    for c in range(8):
        b, D = c // 2, c % 2
        op = res.results[c]["out_post"].reshape(cfg.TP // cfg.C, cfg.C, Dm)
        out.reshape(B, cfg.NCH, cfg.C, Dm)[b, D::2] = op
    return out


# version 29
# speedup vs baseline: 1.1572x; 1.1084x over previous
import numpy as np
import concourse.bass as bass
import concourse.mybir as mybir
from concourse.alu_op_type import AluOpType as Op
from concourse.bass_utils import run_bass_kernel_spmd

F32 = mybir.dt.float32
BF16 = mybir.dt.bfloat16
AF = mybir.ActivationFunctionType
AX = mybir.AxisListType
P = 128
ENGINES = ("sp", "act", "dve", "pe", "pool")
RG_PAIRS = [[0, 1], [2, 3], [4, 5], [6, 7]]


class CFG:
    def __init__(self, **kw):
        self.Dm = 1024; self.DI = 2048; self.NST = 16; self.RK = 64
        self.CONV = 4; self.F = 4096; self.E = 4; self.L = 4096
        self.C = 256
        self.GT = 1024
        self.FQ = 1024
        self.DBB = 2
        self.EPS = 1e-5
        self.debug = False
        for k, v in kw.items():
            setattr(self, k, v)
        self.DH = self.DI // 2
        self.NCH = self.L // self.C
        self.TP = self.L // 2
        self.TT = self.C // P
        self.DBLK = self.DH // P
        self.CBLK = self.DI // P
        self.DMB = self.Dm // P
        self.NG = self.TP // self.GT
        self.SPG = self.GT // self.C
        self.NFQ = self.F // self.FQ
        self.FB = self.FQ // P
        self.RKC = self.RK + 2 * self.NST
        self.DBB = min(self.DBB, self.DBLK)
        assert self.C % P == 0 and self.DH % P == 0 and self.Dm % P == 0
        assert self.TP % self.GT == 0 and self.GT % self.C == 0
        assert self.F % self.FQ == 0 and self.FQ % P == 0
        assert self.DBLK % self.DBB == 0
        assert self.RK <= P and self.NST * 2 + self.RK <= P
        assert self.NCH % 2 == 0


def cdiv(a, b):
    return (a + b - 1) // b


class Builder:
    def __init__(self, cfg: CFG):
        self.cfg = cfg
        self.nc = bass.Bass("TRN2", target_bir_lowering=False, num_devices=8)
        self.streams = {e: [] for e in ENGINES}
        self.counts = {}
        self.last_wait = {}
        self.sem_names = set()
        self.dram = {}

    def ev(self, sem, n=1):
        self.sem_names.add(sem)
        v = self.counts.get(sem, 0) + n
        self.counts[sem] = v
        return (sem, v)

    def op(self, eng, fn, deps=(), inc=None, inc_n=1):
        e = self.ev(inc, inc_n) if inc else None
        agg = {}
        for d in deps:
            if d is None:
                continue
            sem, val = d
            if val > 0:
                agg[sem] = max(agg.get(sem, 0), val)
        waits = []
        for sem, val in agg.items():
            key = (eng, sem)
            if self.last_wait.get(key, 0) < val:
                self.last_wait[key] = val
                waits.append((sem, val))

        def run(h, sems):
            for sem, val in waits:
                h.wait_ge(sems[sem], val)
            inst = fn(h)
            if e is not None and inst is not None:
                inst.then_inc(sems[e[0]], inc_n)

        self.streams[eng].append(run)
        return e

    def drain(self, eng):
        self.streams[eng].append(lambda h, sems: h.drain())

    def dma(self, eng, out_ap, in_ap, deps, qsem):
        return self.op(eng, lambda h: h.dma_start(out=out_ap, in_=in_ap),
                       deps=deps, inc=qsem, inc_n=16)

    def act(self, fn, deps=(), inc="s_act"):
        return self.op("act", fn, deps=deps, inc=inc)

    def dve(self, fn, deps=(), inc="s_dve"):
        return self.op("dve", fn, deps=deps, inc=inc)

    def pe(self, fn, deps=(), inc="s_pe"):
        return self.op("pe", fn, deps=deps, inc=inc)

    def pool(self, fn, deps=(), inc="s_pool"):
        return self.op("pool", fn, deps=deps, inc=inc)

    def mm_chain(self, bank_ap, pairs, deps, inc="s_pe"):
        n = len(pairs)
        ev = None
        for i, (st, mv) in enumerate(pairs):
            ev = self.pe(lambda h, st=st, mv=mv, i=i:
                         h.matmul(bank_ap, st(), mv(), start=(i == 0),
                                  stop=(i == n - 1)),
                         deps=deps if i == 0 else (),
                         inc=inc if i == n - 1 else None)
        return ev

    def din(self, name, shape, dtype=F32):
        self.dram[name] = self.nc.dram_tensor(
            name, list(shape), dtype, kind="ExternalInput").ap()

    def dout(self, name, shape, dtype=F32):
        self.dram[name] = self.nc.dram_tensor(
            name, list(shape), dtype, kind="ExternalOutput").ap()

    def dint(self, name, shape, dtype=F32):
        self.dram[name] = self.nc.dram_tensor(
            name, list(shape), dtype, kind="Internal").ap()

    def build(self):
        from contextlib import ExitStack
        c = self.cfg
        self.din("x_seq", (c.L, c.Dm))
        self.din("x_post", (c.TP, c.Dm))
        self.din("w_in_tiled", ((c.CBLK + c.DH // P) * (c.Dm // P) * P, P))
        self.din("conv_w", (c.DI, c.CONV))
        self.din("conv_b", (c.DI,))
        self.din("x_proj_T", (c.DI, c.RKC))
        self.din("dt_w_T", (c.RK, c.DH))
        self.din("dt_b", (c.DH,))
        self.din("A_log_h", (c.DH, c.NST))
        self.din("Dp_h", (c.DH,))
        self.din("op_w_T", (c.DH, c.Dm))
        self.din("norm_w", (c.Dm,))
        self.din("router_w", (c.Dm, c.E))
        self.din("router_b", (c.E,))
        self.din("w1", (c.E, c.Dm, c.F))
        self.din("w3", (c.E, c.Dm, c.F))
        self.din("w2", (c.E, c.F, c.Dm))
        self.dint("rs_in", (c.L, c.Dm))
        self.dint("comb_d", (c.TP, c.Dm))
        self.dint("x1_spill", (c.TP, c.Dm))
        self.dout("out_post", (c.TP, c.Dm))
        if c.debug:
            self.dout("dbg_xc", (c.NCH, P, c.CBLK * c.C))
            self.dout("dbg_delta", (c.NCH, P, c.DBLK * c.C))
            self.dout("dbg_y", (c.NCH, P, c.DBLK * c.C))
            self.dout("dbg_x1", (c.TP, c.Dm))
            self.dout("dbg_gates", (c.TP // P, P, c.E))

        with ExitStack() as es:
            self.es = es
            self.alloc_sbuf()
            self.emit_preload()
            partials_done = self.emit_mamba()
            rs_done = self.pool(
                lambda h: h.collective_compute(
                    "ReduceScatter", mybir.AluOpType.add, RG_PAIRS,
                    [self.dram["rs_in"][:, :]], [self.dram["comb_d"][:, :]]),
                deps=[partials_done], inc="s_pool")
            self.emit_post_ffn(rs_done)
            self.emit_final_wait()
            sems = {}
            for name in sorted(self.sem_names):
                sems[name] = es.enter_context(self.nc.semaphore(name))
            block = es.enter_context(self.nc.Block())
            for attr, eng in [("sync", "sp"), ("scalar", "act"),
                              ("vector", "dve"), ("tensor", "pe"),
                              ("gpsimd", "pool")]:
                def replay(h, eng=eng):
                    for fn in self.streams[eng]:
                        fn(h, sems)
                getattr(block, attr)(replay)
        return self.nc

    def sb(self, name, shape, dtype):
        return self.es.enter_context(self.nc.sbuf_tensor(name, shape, dtype))

    def psum_t(self, name, shape, dtype=F32):
        return self.es.enter_context(self.nc.psum_tensor(name, shape, dtype))

    def alloc_sbuf(self):
        c = self.cfg
        nb = mybir.dt.size
        self.ident_b = self.sb("ident_b", [P, P], BF16)
        self.iota_f = self.sb("iota_f", [P, P], F32)
        self.ones_f = self.sb("ones_f", [1, P], F32)
        self.sel_b = self.sb("sel_b", [2 * c.NST, 2 * c.NST * P], BF16)
        self.A_sb = self.sb("A_sb", [P, c.DBLK * c.NST], F32)
        self.convw_f = self.sb("convw_f", [P, c.CBLK * c.CONV], F32)
        self.convb_sb = self.sb("convb_sb", [P, c.CBLK], F32)
        self.dtb_sb = self.sb("dtb_sb", [P, c.DBLK], F32)
        self.dp_sb = self.sb("dp_sb", [P, c.DBLK], F32)
        self.nw_sb = self.sb("nw_sb", [P, c.DMB], F32)
        self.rw_sb = self.sb("rw_sb", [P, c.DMB * c.E], BF16)
        self.rb_bc = self.sb("rb_bc", [P, c.E], F32)
        self.hstate = self.sb("hstate", [P, c.DBLK * c.NST], F32)
        self.gates_all = self.sb("gates_all", [P, (c.TP // P) * c.E], F32)

        m_shapes = dict(
            x0_a=([P, c.TT * c.Dm], F32), x0_b=([P, c.TT * c.Dm], F32),
            xn_b=([P, c.Dm], BF16),
            xnT=([P, c.DMB * c.C], BF16),
            xi=([P, c.CBLK * (c.C + c.CONV - 1)], BF16),
            xcb=([P, c.CBLK * c.C], BF16),
            xcf=([P, 2 * c.DBLK * c.C], F32),
            z_b=([P, 2 * c.DBLK * c.C], BF16),
            delta=([P, 2 * c.DBLK * c.C], F32),
            sp_tmp=([P, c.DBLK * c.C], F32),
            du=([P, 2 * c.DBLK * c.C], F32),
            dA0=([P, c.DBB * c.C], F32), dA1=([P, c.DBB * c.C], F32),
            dBu=([P, c.DBB * c.C], F32),
            h_b0=([P, c.DBB * c.C], F32), h_b1=([P, c.DBB * c.C], F32),
            ymul=([P, c.DBB * c.C], F32),
            y_acc=([P, c.DBLK * c.C], F32),
            yg=([P, c.DBLK * c.C], BF16),
            dblb=([c.RK, c.C], BF16),
            bc16=([P, 2 * 3 * c.C], BF16),
            opev=([P, c.TT * c.Dm], F32),
            lnst=([P, 16], F32),
            w_in_ring=([P, 6 * c.DMB * P], BF16),
            opT=([P, c.DBLK * c.Dm], BF16),
            conv_diag=([P, c.CBLK * c.CONV * P], BF16),
            xprojT=([P, c.CBLK * c.RKC], BF16),
            dtprojT=([c.RK, c.DBLK * P], BF16),
        )
        f_shapes = dict(
            comb=([P, c.TT * c.Dm], F32), xp=([P, c.TT * c.Dm], F32),
            t1b=([P, c.TT * c.Dm], F32), ab=([P, c.TT * c.Dm], F32),
            x1b=([P, c.TT * c.Dm], F32), xn2=([P, c.TT * c.Dm], BF16),
            silu_t=([P, 2 * 512], F32),
            lg=([P, c.TT * c.E], F32), lgm=([P, c.TT * c.E], F32),
            lgn=([P, c.TT * c.E], F32),
            gcl=([P, 8 * c.TT], F32),
            fout=([P, c.TT * c.Dm], F32),
            flnst=([P, 16], F32),
            xn2T=([P, c.DMB * c.GT], BF16),
            f_acc=([P, (c.GT // P) * c.Dm], F32),
            h_q=([P, c.FB * c.GT], BF16),
            w2q=([P, c.FB * c.Dm], BF16),
            w13s=([P, 4 * 2 * c.DMB * P], BF16),
        )

        def region(shapes):
            tot = 0
            for shape, dt in shapes.values():
                tot = (tot + 31) // 32 * 32
                tot += int(np.prod(shape[1:])) * nb(dt)
            return tot

        arena_bytes = max(region(m_shapes), region(f_shapes))
        arena = self.sb("arena", [P, arena_bytes], mybir.dt.uint8)
        base = self.nc.lookup_mloc(arena).addr

        def place(prefix, shapes):
            out, cur = {}, 0
            for name, (shape, dt) in shapes.items():
                cur = (cur + 31) // 32 * 32
                h = self.nc.virtual_sbuf_tensor(prefix + name, shape, dt)
                self.nc._tensor(h.name, h.shape, h.dtype, type="SB")
                m = self.nc.lookup_mloc(h)
                m.allocated = True
                m.addr = base + cur
                cur += int(np.prod(shape[1:])) * nb(dt)
                out[name] = h
            return out

        self.M = place("m_", m_shapes)
        self.F = place("f_", f_shapes)
        self.ps_mm = self.psum_t("ps_mm", [P, max(c.C, 512)])
        self.ps_tr = self.psum_t("ps_tr", [P, P], BF16)
        self.ps_bc = [self.psum_t("ps_bc0", [P, 2 * c.C]),
                      self.psum_t("ps_bc1", [P, 2 * c.C])]
        self.ps_h1 = self.psum_t("ps_h1", [P, 512])
        self.ps_h3 = self.psum_t("ps_h3", [P, 512])
        self.ps_f = [self.psum_t("ps_f0", [P, 512]),
                     self.psum_t("ps_f1", [P, 512])]

    def emit_preload(self):
        c, M = self.cfg, self.M
        ev_iota = self.pool(lambda h: h.iota(
            self.iota_f[:], [[-1, P]], base=0, channel_multiplier=1,
            allow_small_or_imprecise_dtypes=True))
        self.ev_id = self.dve(lambda h: h.tensor_single_scalar(
            self.ident_b[:], self.iota_f[:], 0.0, Op.is_equal),
            deps=[ev_iota])
        self.ev_ones = self.dve(lambda h: h.memset(self.ones_f[:], 1.0))
        NS2 = 2 * c.NST
        ev_sel = None
        for n in range(NS2):
            evi = self.pool(lambda h, n=n: h.iota(
                self.iota_f[0:NS2, 0:P], [[0, P]], base=-n,
                channel_multiplier=1,
                allow_small_or_imprecise_dtypes=True),
                deps=[ev_sel if ev_sel else self.ev_id])
            ev_sel = self.dve(lambda h, n=n: h.tensor_single_scalar(
                self.sel_b[:, n * P:(n + 1) * P],
                self.iota_f[0:NS2, 0:P], 0.0, Op.is_equal), deps=[evi])
        self.ev_sel = ev_sel

        def load_cols(dst, src, nblk):
            in_ap = self.dram[src].rearrange("(b p) -> p b", p=P)
            return self.op(
                "sp", lambda h, d=dst, i=in_ap: h.dma_start(
                    out=d[:, 0:nblk], in_=i, allow_slow_non_contiguous=True),
                deps=[], inc="q_pre", inc_n=16)

        pre = [load_cols(self.convb_sb, "conv_b", c.CBLK),
               load_cols(self.dtb_sb, "dt_b", c.DBLK),
               load_cols(self.dp_sb, "Dp_h", c.DBLK),
               load_cols(self.nw_sb, "norm_w", c.DMB),
               self.dma("sp",
                        self.A_sb[:, 0:c.DBLK * c.NST].rearrange(
                            "p (b n) -> p b n", b=c.DBLK),
                        self.dram["A_log_h"].rearrange(
                            "(b p) n -> p b n", p=P), [], "q_pre"),
               self.dma("sp",
                        self.convw_f[:, 0:c.CBLK * c.CONV].rearrange(
                            "p (b k) -> p b k", b=c.CBLK),
                        self.dram["conv_w"].rearrange(
                            "(b p) k -> p b k", p=P), [], "q_pre"),
               self.dma("sp", self.rb_bc[0:1, 0:c.E],
                        self.dram["router_b"].rearrange("(o e) -> o e", o=1),
                        [], "q_pre")]
        self.pre = pre
        ev = self.act(lambda h: h.activation(
            self.A_sb[:, 0:c.DBLK * c.NST], self.A_sb[:, 0:c.DBLK * c.NST],
            AF.Exp), deps=pre)
        self.ev_A = self.dve(lambda h: h.tensor_scalar_mul(
            self.A_sb[:, 0:c.DBLK * c.NST], self.A_sb[:, 0:c.DBLK * c.NST],
            -1.0), deps=[ev])
        ev = self.pe(lambda h: h.matmul(
            self.ps_mm[:, 0:c.E], self.ones_f[:], self.rb_bc[0:1, 0:c.E],
            start=True, stop=True), deps=pre + [self.ev_ones])
        self.ev_rb = self.act(lambda h: h.activation(
            self.rb_bc[:, 0:c.E], self.ps_mm[:, 0:c.E], AF.Copy), deps=[ev])
        self.ev_rw = self.dma("pool",
                              self.rw_sb[:, 0:c.DMB * c.E].rearrange(
                                  "p (b e) -> p b e", b=c.DMB),
                              self.dram["router_w"].rearrange(
                                  "(b p) e -> p b e", p=P), [], "q_pool")
        ev_cd = None
        for cblk in range(c.CBLK):
            for k in range(c.CONV):
                o = (cblk * c.CONV + k) * P
                col = cblk * c.CONV + k
                ev_cd = self.dve(lambda h, o=o, col=col:
                    h.tensor_scalar_mul(M["conv_diag"][:, o:o + P],
                                        self.ident_b[:],
                                        self.convw_f[:, col:col + 1]),
                    deps=[self.ev_id] + pre)
        self.ev_cd = ev_cd
        self.ev_h0 = self.dve(lambda h: h.memset(
            self.hstate[:, 0:c.DBLK * c.NST], 0.0))
        self.ev_halo0 = None
        for cblk in range(c.CBLK):
            o = cblk * (c.C + c.CONV - 1)
            self.ev_halo0 = self.dve(lambda h, o=o: h.memset(
                M["xi"][:, o:o + c.CONV - 1], 0.0))

        def wload3(dst_name, src, nb_):
            return self.dma("pool",
                            M[dst_name][:, :].rearrange("p (b o) -> p b o",
                                                        b=nb_),
                            self.dram[src].rearrange("(b p) o -> p b o",
                                                     p=P),
                            (), "q_pool")

        self.ev_opw = wload3("opT", "op_w_T", c.DBLK)
        self.ev_xpw = wload3("xprojT", "x_proj_T", c.CBLK)
        self.ev_dtw = self.dma(
            "pool", M["dtprojT"][0:c.RK, :], self.dram["dt_w_T"][:, :],
            [], "q_pool")

    def emit_mamba(self):
        c = self.cfg
        self._m = dict(
            x0_ev=[None, None], x0_free=[None, None],
            xi_tail=None, lnst_busy=None, tr_prev=None,
            bank_free={0: None, 1: None},
            wslab_free=[None] * 6, wslab_ev=[None] * 6, wslab=0,
            bc_free=[None, None], dA_free=[None, None], h_free=[None, None],
            hs_prev=self.ev_h0, pf_free=None, partial_last=None,
            opmm_last=None, sp_free=None, dBu_busy=None, hs_evs={},
            selmm_last=[None, None], dup_free=[None, None],
            prep=[None, None],
        )
        DONE = object()
        for j in range(c.NCH + 1):
            pg = self.emit_prep(j) if j < c.NCH else None
            sg = self.emit_scan(j - 1) if j >= 1 else None
            done_p, done_s = pg is None, sg is None
            while not (done_p and done_s):
                if not done_s:
                    done_s = next(sg, DONE) is DONE
                for _ in range(3):
                    if not done_p:
                        done_p = next(pg, DONE) is DONE
        return ("q_rs", 16 * c.NCH)

    def emit_prep(self, j):
        c, M, S = self.cfg, self.M, self._m
        x0 = [M["x0_a"], M["x0_b"]]
        sl = j % 2
        W = c.C + c.CONV - 1
        wait_weights = [self.ev_opw, self.ev_xpw, self.ev_dtw, self.ev_cd,
                        self.ev_A, self.ev_sel, self.ev_halo0]
        def load_x0(jj):
            ssl = jj % 2
            S["x0_ev"][ssl] = self.dma(
                "sp",
                x0[ssl][:, 0:c.TT * c.Dm].rearrange("p (t d) -> p t d",
                                                    t=c.TT),
                self.dram["x_seq"][jj * c.C:(jj + 1) * c.C, :].rearrange(
                    "(t p) d -> p t d", p=P),
                [S["x0_free"][ssl]], "q_x0" + ("a" if ssl == 0 else "b"))
        if j == 0:
            load_x0(0)
        if j + 1 < c.NCH:
            load_x0(j + 1)

        ev_xnT = None
        for tt in range(c.TT):
            xt = x0[sl][:, tt * c.Dm:(tt + 1) * c.Dm]
            deps = [S["x0_ev"][sl], S["lnst_busy"]]
            nst = cdiv(c.Dm, 512)
            for si in range(nst):
                w = min(512, c.Dm - si * 512)
                self.dve(lambda h, xt=xt, si=si, w=w: h.bn_stats(
                    M["lnst"][:, si * 6:si * 6 + 6],
                    xt[:, si * 512:si * 512 + w]), deps=deps)
            self.drain("dve")
            self.dve(lambda h, nst=nst: h.bn_aggr(
                M["lnst"][:, 12:14], M["lnst"][:, 0:6 * nst]))
            self.drain("dve")
            ev_v = self.dve(lambda h: h.tensor_scalar(
                M["lnst"][:, 14:15], M["lnst"][:, 13:14], 1.0 + c.EPS,
                c.EPS * c.EPS, Op.mult, Op.add))
            ev = self.act(lambda h: h.activation(
                M["lnst"][:, 14:15], M["lnst"][:, 14:15], AF.Ln),
                deps=[ev_v])
            self.drain("act")
            ev = self.act(lambda h: h.activation(
                M["lnst"][:, 14:15], M["lnst"][:, 14:15], AF.Exp,
                scale=-0.5))
            ev = self.dve(lambda h: h.tensor_tensor(
                M["lnst"][:, 15:16], M["lnst"][:, 12:13],
                M["lnst"][:, 14:15], Op.mult), deps=[ev])
            self.drain("dve")
            ev = self.dve(lambda h: h.tensor_scalar_mul(
                M["lnst"][:, 15:16], M["lnst"][:, 15:16], -1.0))
            self.drain("act")
            ev_xn = self.act(lambda h, xt=xt: h.activation(
                M["xn_b"][:, 0:c.Dm], xt, AF.Identity,
                bias=M["lnst"][:, 15:16], scale=M["lnst"][:, 14:15]),
                deps=[ev, S["tr_prev"]])
            S["lnst_busy"] = ev_xn
            for dmb in range(c.DMB):
                evt = self.pe(lambda h, dmb=dmb: h.transpose(
                    self.ps_tr[:], M["xn_b"][:, dmb * P:(dmb + 1) * P],
                    self.ident_b[:]),
                    deps=[ev_xn, self.ev_id, S["tr_prev"]])
                ev_xnT = self.act(lambda h, dmb=dmb, tt=tt:
                    h.activation(
                        M["xnT"][:, dmb * c.C + tt * P:
                                 dmb * c.C + tt * P + P],
                        self.ps_tr[:], AF.Identity,
                        scale=self.nw_sb[:, dmb:dmb + 1]), deps=[evt])
                S["tr_prev"] = ev_xnT
            yield
        S["x0_free"][sl] = ev_xnT

        ev_xi_ev = []
        ev_z = None
        doff = sl * c.DBLK * c.C
        for colg in range(c.CBLK + c.DBLK):
            ws = S["wslab"] % 6
            S["wslab"] += 1
            so = ws * c.DMB * P
            ld = self.dma(
                "pool",
                M["w_in_ring"][:, so:so + c.DMB * P].rearrange(
                    "p (b cc) -> p b cc", b=c.DMB),
                self.dram["w_in_tiled"][colg * c.DMB * P:
                                        (colg + 1) * c.DMB * P, :]
                .rearrange("(b p) cc -> p b cc", p=P),
                [S["wslab_free"][ws]], "q_wi" + str(ws))
            bank = [self.ps_mm, self.ps_h1][colg % 2]
            pairs = []
            for dmb in range(c.DMB):
                pairs.append((
                    lambda so=so, dmb=dmb:
                    M["w_in_ring"][:, so + dmb * P:so + dmb * P + P],
                    lambda dmb=dmb: M["xnT"][:, dmb * c.C:
                                             dmb * c.C + c.C]))
            evm = self.mm_chain(
                bank[:, 0:c.C], pairs,
                deps=[ld, ev_xnT, S["bank_free"][colg % 2]] + wait_weights)
            S["wslab_free"][ws] = evm
            if colg < c.CBLK:
                ev = self.act(lambda h, colg=colg, bank=bank:
                    h.activation(
                        M["xi"][:, colg * W + c.CONV - 1:
                                colg * W + c.CONV - 1 + c.C],
                        bank[:, 0:c.C], AF.Copy),
                    deps=[evm, S["xi_tail"]])
                ev_xi_ev.append(ev)
            else:
                zb = colg - c.CBLK
                ev = self.act(lambda h, zb=zb, bank=bank, doff=doff:
                    h.activation(
                        M["z_b"][:, doff + zb * c.C:doff + (zb + 1) * c.C],
                        bank[:, 0:c.C], AF.Silu), deps=[evm])
                ev_z = ev
            S["bank_free"][colg % 2] = ev
            if colg % 2 == 1:
                yield
        S["prep_z"] = ev_z

        ev_xcb = []
        ev_xcf = {}
        for cblk in range(c.CBLK):
            bank = [self.ps_mm, self.ps_h1][cblk % 2]
            pairs = []
            for k in range(c.CONV):
                o = (cblk * c.CONV + k) * P
                xo = cblk * W + k
                pairs.append((lambda o=o: M["conv_diag"][:, o:o + P],
                              lambda xo=xo: M["xi"][:, xo:xo + c.C]))
            evm = self.mm_chain(bank[:, 0:c.C], pairs,
                                deps=[ev_xi_ev[cblk], ev_xi_ev[-1],
                                      S["bank_free"][cblk % 2]])
            ev1 = self.act(lambda h, cblk=cblk, bank=bank:
                h.activation(M["xcb"][:, cblk * c.C:(cblk + 1) * c.C],
                             bank[:, 0:c.C], AF.Silu,
                             bias=self.convb_sb[:, cblk:cblk + 1]),
                deps=[evm, getattr(self, "ev_dbg_xc", None)])
            ev_xcb.append(ev1)
            if cblk < c.DBLK:
                ev2 = self.act(lambda h, cblk=cblk, bank=bank, doff=doff:
                    h.activation(
                        M["xcf"][:, doff + cblk * c.C:
                                 doff + (cblk + 1) * c.C],
                        bank[:, 0:c.C], AF.Silu,
                        bias=self.convb_sb[:, cblk:cblk + 1]),
                    deps=[evm, S["xcf_free"] if "xcf_free" in S else None])
                ev_xcf[cblk] = ev2
                S["bank_free"][cblk % 2] = ev2
            else:
                S["bank_free"][cblk % 2] = ev1
            if cblk % 2 == 1:
                yield
        ev_tail = None
        for cblk in range(c.CBLK):
            o = cblk * W
            ev_tail = self.dve(lambda h, o=o: h.tensor_copy(
                M["xi"][:, o:o + c.CONV - 1],
                M["xi"][:, o + c.C:o + c.C + c.CONV - 1]),
                deps=[ev_xcb[-1], ev_xi_ev[-1]])
        S["xi_tail"] = ev_tail

        bank = self.ps_mm
        pairs = []
        for cblk in range(c.CBLK):
            pairs.append((
                lambda cblk=cblk: M["xprojT"][:, cblk * c.RKC:
                                              (cblk + 1) * c.RKC],
                lambda cblk=cblk: M["xcb"][:, cblk * c.C:
                                           (cblk + 1) * c.C]))
        evm = self.mm_chain(bank[0:c.RKC, 0:c.C], pairs,
                            deps=[ev_xcb[-1], S["bank_free"][0]])
        ev_dt = self.act(lambda h, bank=bank: h.activation(
            M["dblb"][0:c.RK, 0:c.C], bank[0:c.RK, 0:c.C], AF.Copy),
            deps=[evm])
        boff = sl * 3 * c.C
        ev_bcev = self.act(lambda h, bank=bank, boff=boff: h.activation(
            M["bc16"][c.RK:c.RK + 2 * c.NST, boff:boff + c.C],
            bank[c.RK:c.RK + 2 * c.NST, 0:c.C], AF.Copy),
            deps=[evm, S["dup_free"][sl]])
        ev_bc1 = self.dma(
            "pool", M["bc16"][0:2 * c.NST, boff + c.C:boff + 2 * c.C],
            M["bc16"][c.RK:c.RK + 2 * c.NST, boff:boff + c.C],
            [ev_bcev], "q_bc" + ("a" if sl == 0 else "b"))
        ev_bc2 = self.dma(
            "pool", M["bc16"][0:2 * c.NST, boff + 2 * c.C:boff + 3 * c.C],
            M["bc16"][c.RK:c.RK + 2 * c.NST, boff:boff + c.C],
            [ev_bcev], "q_bc" + ("a" if sl == 0 else "b"))
        S["prep_bc"] = (ev_bc1, ev_bc2)
        S["bank_free"][0] = ev_bcev

        ev_dl = []
        for dblk in range(c.DBLK):
            bank = [self.ps_mm, self.ps_h1][dblk % 2]
            evm = self.mm_chain(
                bank[:, 0:c.C],
                [(lambda dblk=dblk: M["dtprojT"][0:c.RK,
                                                 dblk * P:(dblk + 1) * P],
                  lambda: M["dblb"][0:c.RK, 0:c.C])],
                deps=[ev_dt, S["bank_free"][dblk % 2]])
            eva = self.act(lambda h, dblk=dblk, bank=bank: h.activation(
                M["sp_tmp"][:, dblk * c.C:(dblk + 1) * c.C],
                bank[:, 0:c.C], AF.Abs,
                bias=self.dtb_sb[:, dblk:dblk + 1]),
                deps=[evm, S["sp_free"]])
            evr = self.act(lambda h, dblk=dblk, bank=bank, doff=doff:
                h.activation(
                    M["delta"][:, doff + dblk * c.C:
                               doff + (dblk + 1) * c.C],
                    bank[:, 0:c.C], AF.Relu,
                    bias=self.dtb_sb[:, dblk:dblk + 1]),
                deps=[evm, getattr(self, "ev_dbg_dl", None)])
            ev_dl.append((eva, evr))
            S["bank_free"][dblk % 2] = evr
            if dblk % 2 == 1:
                yield
        self.drain("act")
        eve = None
        for dblk in range(c.DBLK):
            eve = self.act(lambda h, dblk=dblk: h.activation(
                M["sp_tmp"][:, dblk * c.C:(dblk + 1) * c.C],
                M["sp_tmp"][:, dblk * c.C:(dblk + 1) * c.C],
                AF.Exp, scale=-1.0), deps=[ev_dl[dblk][0]])
        self.drain("act")
        evl = None
        for dblk in range(c.DBLK):
            evl = self.act(lambda h, dblk=dblk: h.activation(
                M["sp_tmp"][:, dblk * c.C:(dblk + 1) * c.C],
                M["sp_tmp"][:, dblk * c.C:(dblk + 1) * c.C],
                AF.Ln, bias=1.0), deps=[eve])
        ev_delta = None
        for dblk in range(c.DBLK):
            ev_delta = self.dve(lambda h, dblk=dblk, doff=doff:
                h.tensor_add(
                    M["delta"][:, doff + dblk * c.C:
                               doff + (dblk + 1) * c.C],
                    M["delta"][:, doff + dblk * c.C:
                               doff + (dblk + 1) * c.C],
                    M["sp_tmp"][:, dblk * c.C:(dblk + 1) * c.C]),
                deps=[evl, ev_dl[dblk][1]])
        S["sp_free"] = ev_delta
        self.drain("dve")
        ev_du = None
        for dblk in range(c.DBLK):
            ev_du = self.dve(lambda h, dblk=dblk, doff=doff: h.tensor_mul(
                M["du"][:, doff + dblk * c.C:doff + (dblk + 1) * c.C],
                M["delta"][:, doff + dblk * c.C:doff + (dblk + 1) * c.C],
                M["xcf"][:, doff + dblk * c.C:doff + (dblk + 1) * c.C]),
                deps=[ev_delta, ev_xcf.get(dblk)])
        self.drain("dve")
        yield
        S["prep"][sl] = dict(delta=ev_delta, du=ev_du, xcf=ev_xcf,
                             z=ev_z, bc=S["prep_bc"], xcb=ev_xcb[-1])

        if c.debug:
            self.ev_dbg_xc = self.dma(
                "pool", self.dram["dbg_xc"][j, :, :],
                M["xcb"][:, :], [ev_xcb[-1]], "q_dbgp")
            self.ev_dbg_dl = self.dma(
                "sp", self.dram["dbg_delta"][j, :, :],
                M["delta"][:, doff:doff + c.DBLK * c.C], [ev_delta],
                "q_dbg1")

    def emit_scan(self, j):
        c, M, S = self.cfg, self.M, self._m
        sl = j % 2
        doff = sl * c.DBLK * c.C
        boff = sl * 3 * c.C
        pr = S["prep"][sl]
        dA = [M["dA0"], M["dA1"]]
        h_b = [M["h_b0"], M["h_b1"]]
        nbt = c.DBLK // c.DBB
        DBC = c.DBB * c.C
        ev_y = None
        ev_hs_new = None
        bcdep = pr["bc"]
        rep_banks = [(self.ps_bc[0], self.ps_bc[1]),
                     (self.ps_h3, self.ps_f[1])]
        hs_prev_map = S["hs_evs"]
        S["hs_evs"] = {}
        for n in range(c.NST):
            npar = n % 2
            bB, bC = rep_banks[npar]
            evB = self.pe(lambda h, n=n, bB=bB, boff=boff: h.matmul(
                bB[:, 0:2 * c.C],
                self.sel_b[:, n * P:(n + 1) * P],
                M["bc16"][0:2 * c.NST, boff + c.C:boff + 3 * c.C],
                start=True, stop=True),
                deps=[bcdep[0], bcdep[1], S["bc_free"][npar],
                      self.ev_sel])
            evC = self.pe(lambda h, n=n, bC=bC, boff=boff: h.matmul(
                bC[:, 0:2 * c.C],
                self.sel_b[:, (c.NST + n) * P:(c.NST + n + 1) * P],
                M["bc16"][0:2 * c.NST, boff + c.C:boff + 3 * c.C],
                start=True, stop=True), deps=[])
            S["selmm_last"][sl] = evC
            for bb in range(nbt):
                par = (n * nbt + bb) % 2
                ev_dA = None
                for q in range(c.DBB):
                    dblk = bb * c.DBB + q
                    ev_dA = self.act(lambda h, q=q, dblk=dblk, n=n,
                                     par=par, doff=doff: h.activation(
                        dA[par][:, q * c.C:(q + 1) * c.C],
                        M["delta"][:, doff + dblk * c.C:
                                   doff + (dblk + 1) * c.C],
                        AF.Exp,
                        scale=self.A_sb[:, dblk * c.NST + n:
                                        dblk * c.NST + n + 1]),
                        deps=[pr["delta"], S["dA_free"][par]])
                evd = self.dve(lambda h, bb=bb, doff=doff, bB=bB:
                    h.tensor_mul(
                        M["dBu"][:, 0:DBC],
                        M["du"][:, doff + bb * DBC:doff + (bb + 1) * DBC],
                        bB[:, 0:DBC]),
                    deps=[pr["du"], evC, S["h_free"][par],
                          S["dBu_busy"]])
                evs = None
                for q in range(c.DBB):
                    dblk = bb * c.DBB + q
                    col = dblk * c.NST + n
                    evs = self.dve(lambda h, q=q, col=col, par=par:
                        h.tensor_tensor_scan(
                            h_b[par][:, q * c.C:(q + 1) * c.C],
                            dA[par][:, q * c.C:(q + 1) * c.C],
                            M["dBu"][:, q * c.C:(q + 1) * c.C],
                            self.hstate[:, col:col + 1],
                            Op.mult, Op.add),
                        deps=[ev_dA, evd,
                              hs_prev_map.get((n, bb), self.ev_h0)])
                S["dA_free"][par] = evs
                S["dBu_busy"] = evs
                col0 = bb * c.DBB * c.NST + n
                ev_hs_new = self.act(lambda h, par=par, col0=col0:
                    h.activation(
                        self.hstate[:, col0:col0 + (c.DBB - 1) * c.NST
                                    + 1:c.NST],
                        h_b[par][:, c.C - 1:c.DBB * c.C:c.C], AF.Copy),
                    deps=[evs])
                S["hs_evs"][(n, bb)] = ev_hs_new
                if n == 0:
                    ev_y = self.dve(lambda h, bb=bb, bC=bC, par=par:
                        h.tensor_mul(
                            M["y_acc"][:, bb * DBC:(bb + 1) * DBC],
                            h_b[par][:, 0:DBC], bC[:, 0:DBC]),
                        deps=[evs, getattr(self, "ev_dbg_y", None)])
                else:
                    ev_ym = self.dve(lambda h, bC=bC, par=par:
                        h.tensor_mul(
                            M["ymul"][:, 0:DBC], h_b[par][:, 0:DBC],
                            bC[:, 0:DBC]), deps=[evs])
                    ev_y = self.dve(lambda h, bb=bb: h.tensor_add(
                        M["y_acc"][:, bb * DBC:(bb + 1) * DBC],
                        M["y_acc"][:, bb * DBC:(bb + 1) * DBC],
                        M["ymul"][:, 0:DBC]), deps=[ev_ym])
                S["h_free"][par] = ev_y
            S["bc_free"][npar] = ev_y
            yield
        S["dup_free"][sl] = S["selmm_last"][sl]

        zs_busy = [None, None]
        evg1 = None
        evg2 = None
        ev_yg = None
        for dblk in range(c.DBLK):
            zsl = dblk % 2
            ev_zs = self.act(lambda h, dblk=dblk, doff=doff, zsl=zsl:
                h.activation(
                    M["dBu"][:, zsl * c.C:(zsl + 1) * c.C],
                    M["z_b"][:, doff + dblk * c.C:doff + (dblk + 1) * c.C],
                    AF.Silu),
                deps=[pr["z"], ev_y, zs_busy[zsl]])
            evg1 = self.dve(lambda h, dblk=dblk, doff=doff, zsl=zsl:
                h.tensor_scalar(
                    M["ymul"][:, zsl * c.C:(zsl + 1) * c.C],
                    M["xcf"][:, doff + dblk * c.C:
                             doff + (dblk + 1) * c.C],
                    self.dp_sb[:, dblk:dblk + 1], None, Op.mult),
                deps=[ev_y])
            evg2 = self.dve(lambda h, dblk=dblk, zsl=zsl: h.tensor_add(
                M["y_acc"][:, dblk * c.C:(dblk + 1) * c.C],
                M["y_acc"][:, dblk * c.C:(dblk + 1) * c.C],
                M["ymul"][:, zsl * c.C:(zsl + 1) * c.C]), deps=[evg1])
            ev_yg = self.dve(lambda h, dblk=dblk, zsl=zsl: h.tensor_mul(
                M["yg"][:, dblk * c.C:(dblk + 1) * c.C],
                M["y_acc"][:, dblk * c.C:(dblk + 1) * c.C],
                M["dBu"][:, zsl * c.C:(zsl + 1) * c.C]),
                deps=[evg2, ev_zs, S["opmm_last"]])
            zs_busy[zsl] = ev_yg
            S["dBu_busy"] = ev_yg
            if dblk % 2 == 1:
                yield
        S["xcf_free"] = evg1
        ev_yg_last = ev_yg

        if c.debug:
            self.ev_dbg_y = self.dma(
                "sp", self.dram["dbg_y"][j, :, :],
                M["y_acc"][:, :], [evg2], "q_dbg2")

        ev_opev = []
        for tt in range(c.TT):
            for half in range(cdiv(c.Dm, 512)):
                wcols = min(512, c.Dm - half * 512)
                bank = self.ps_f[0]
                pairs = []
                for cblk in range(c.DBLK):
                    pairs.append((
                        lambda cblk=cblk, tt=tt:
                        M["yg"][:, cblk * c.C + tt * P:
                                cblk * c.C + tt * P + P],
                        lambda cblk=cblk, half=half, wcols=wcols:
                        M["opT"][:, cblk * c.Dm + half * 512:
                                 cblk * c.Dm + half * 512 + wcols]))
                evm = self.mm_chain(bank[:, 0:wcols], pairs,
                                    deps=[ev_yg_last, S["pf_free"]])
                S["opmm_last"] = evm
                evo = self.act(lambda h, tt=tt, half=half, wcols=wcols,
                               bank=bank: h.activation(
                    M["opev"][:, tt * c.Dm + half * 512:
                              tt * c.Dm + half * 512 + wcols],
                    bank[:, 0:wcols], AF.Copy),
                    deps=[evm, S["partial_last"]])
                S["pf_free"] = evo
                ev_opev.append(evo)
            yield
        rb = (j % 2) * c.TP + (j // 2) * c.C
        S["partial_last"] = self.dma(
            "sp",
            self.dram["rs_in"][rb:rb + c.C, :].rearrange(
                "(t p) d -> p t d", p=P),
            M["opev"][:, 0:c.TT * c.Dm].rearrange("p (t d) -> p t d",
                                                  t=c.TT),
            [ev_opev[-1]], "q_rs")

    def emit_post_ffn(self, rs_done):
        c, F = self.cfg, self.F
        self._flnst_busy = None
        mamba_done = [rs_done]
        for qq in ("q_dbg1", "q_dbg2", "q_dbgp"):
            if self.counts.get(qq, 0):
                mamba_done.append((qq, self.counts[qq]))
        ev_ffn_prev_group = None
        ev_out_last = None
        ev_tr_prev = None
        ev_w2_free = None
        ev_hq_free = None
        slab = 0
        ev_slab_free = [None, None, None, None]
        self._si_busy = {}
        self._h_busy = {}
        self._psf_busy = {}
        n_out = 0
        for g in range(c.NG):
            ev_xn2T = None
            ev_gate_done = None
            ev_step_free = ev_ffn_prev_group
            for ps in range(c.SPG):
                row = g * c.GT + ps * c.C
                ld_cb = self.dma(
                    "sp", F["comb"][:, :].rearrange("p (t d) -> p t d", t=c.TT),
                    self.dram["comb_d"][row:row + c.C, :].rearrange(
                        "(t p) d -> p t d", p=P),
                    mamba_done + [ev_step_free], "q_cb")
                ld_xp = self.dma(
                    "sp", F["xp"][:, :].rearrange("p (t d) -> p t d", t=c.TT),
                    self.dram["x_post"][row:row + c.C, :].rearrange(
                        "(t p) d -> p t d", p=P),
                    mamba_done + [ev_step_free], "q_cb")

                def ln_scale_bias(src_ap, soff, extra_deps):
                    extra_deps = list(extra_deps) + [self._flnst_busy]
                    """bn stats on src -> flnst[soff]=g, flnst[soff+1]=-m*g"""
                    nst = cdiv(c.Dm, 512)
                    for si in range(nst):
                        w = min(512, c.Dm - si * 512)
                        self.dve(lambda h, si=si, w=w, src_ap=src_ap:
                                 h.bn_stats(F["flnst"][:, si * 6:si * 6 + 6],
                                            src_ap[:, si * 512:si * 512 + w]),
                                 deps=extra_deps)
                    self.drain("dve")
                    self.dve(lambda h, nst=nst: h.bn_aggr(
                        F["flnst"][:, 12:14], F["flnst"][:, 0:6 * nst]))
                    self.drain("dve")
                    ev_v = self.dve(lambda h: h.tensor_scalar_add(
                        F["flnst"][:, 11:12], F["flnst"][:, 13:14], c.EPS))
                    ev = self.act(lambda h, soff=soff: h.activation(
                        F["flnst"][:, soff:soff + 1], F["flnst"][:, 11:12],
                        AF.Ln), deps=[ev_v])
                    self.drain("act")
                    ev = self.act(lambda h, soff=soff: h.activation(
                        F["flnst"][:, soff:soff + 1],
                        F["flnst"][:, soff:soff + 1], AF.Exp, scale=-0.5))
                    ev = self.dve(lambda h, soff=soff: h.tensor_mul(
                        F["flnst"][:, soff + 1:soff + 2],
                        F["flnst"][:, 12:13], F["flnst"][:, soff:soff + 1]),
                        deps=[ev])
                    self.drain("dve")
                    ev = self.dve(lambda h, soff=soff: h.tensor_scalar_mul(
                        F["flnst"][:, soff + 1:soff + 2],
                        F["flnst"][:, soff + 1:soff + 2], -1.0))
                    self.drain("act")
                    return ev

                ev_ab = None
                ev_x1 = None
                for tt in range(c.TT):
                    xpt = F["xp"][:, tt * c.Dm:(tt + 1) * c.Dm]
                    cbt = F["comb"][:, tt * c.Dm:(tt + 1) * c.Dm]
                    ev = ln_scale_bias(xpt, 14, [ld_xp, ev_step_free])
                    ev_t1 = self.act(lambda h, xpt=xpt, tt=tt: h.activation(
                        F["t1b"][:, tt * c.Dm:(tt + 1) * c.Dm], xpt, AF.Identity,
                        bias=F["flnst"][:, 15:16],
                        scale=F["flnst"][:, 14:15]), deps=[ev])
                    ev_ab = self.dve(lambda h, tt=tt, cbt=cbt: h.tensor_add(
                        F["ab"][:, tt * c.Dm:(tt + 1) * c.Dm],
                        F["t1b"][:, tt * c.Dm:(tt + 1) * c.Dm], cbt),
                        deps=[ev_t1, ld_cb])
                    abt = F["ab"][:, tt * c.Dm:(tt + 1) * c.Dm]
                    ev = ln_scale_bias(abt, 14, [ev_ab])
                    ev_t2 = self.act(lambda h, abt=abt, tt=tt: h.activation(
                        F["t1b"][:, tt * c.Dm:(tt + 1) * c.Dm], abt, AF.Identity,
                        bias=F["flnst"][:, 15:16],
                        scale=F["flnst"][:, 14:15]), deps=[ev, ev_ab])
                    ev_x1 = self.dve(lambda h, tt=tt, xpt=xpt: h.tensor_add(
                        F["x1b"][:, tt * c.Dm:(tt + 1) * c.Dm],
                        F["t1b"][:, tt * c.Dm:(tt + 1) * c.Dm], xpt),
                        deps=[ev_t2] + list(getattr(self, "_x1_guard", [])))
                    x1t = F["x1b"][:, tt * c.Dm:(tt + 1) * c.Dm]
                    ev = ln_scale_bias(x1t, 14, [ev_x1])
                    ev_xn2 = self.act(lambda h, x1t=x1t, tt=tt: h.activation(
                        F["xn2"][:, tt * c.Dm:(tt + 1) * c.Dm], x1t, AF.Identity,
                        bias=F["flnst"][:, 15:16],
                        scale=F["flnst"][:, 14:15]), deps=[ev, ev_x1])
                    for dmb in range(c.DMB):
                        evt = self.pe(lambda h, dmb=dmb, tt=tt: h.transpose(
                            self.ps_tr[:],
                            F["xn2"][:, tt * c.Dm + dmb * P:
                                     tt * c.Dm + dmb * P + P],
                            self.ident_b[:]),
                            deps=[ev_xn2, ev_tr_prev, self.ev_id,
                                  ev_ffn_prev_group])
                        ev_xn2T = self.act(lambda h, dmb=dmb, tt=tt, ps=ps:
                            h.activation(
                                F["xn2T"][:, dmb * c.GT + ps * c.C + tt * P:
                                          dmb * c.GT + ps * c.C + tt * P
                                          + P],
                                self.ps_tr[:], AF.Copy), deps=[evt])
                        ev_tr_prev = ev_xn2T
                ev_sp = self.dma(
                    "sp", self.dram["x1_spill"][row:row + c.C, :].rearrange(
                        "(t p) d -> p t d", p=P),
                    F["x1b"][:, :].rearrange("p (t d) -> p t d", t=c.TT),
                    [ev_x1], "q_x1")
                self._x1_guard = [ev_sp]
                if c.debug:
                    self._x1_guard.append(self.dma(
                        "sp", self.dram["dbg_x1"][row:row + c.C, :]
                        .rearrange("(t p) d -> p t d", p=P),
                        F["x1b"][:, :].rearrange("p (t d) -> p t d", t=c.TT),
                        [ev_x1], "q_dbg"))

                for tt in range(c.TT):
                    ti = (g * c.GT + ps * c.C) // P + tt
                    pairs = []
                    for dmb in range(c.DMB):
                        pairs.append((
                            lambda dmb=dmb, ps=ps, tt=tt:
                            F["xn2T"][:, dmb * c.GT + ps * c.C + tt * P:
                                      dmb * c.GT + ps * c.C + tt * P + P],
                            lambda dmb=dmb: self.rw_sb[:, dmb * c.E:
                                                       (dmb + 1) * c.E]))
                    evm = self.mm_chain(self.ps_mm[:, 0:c.E], pairs,
                                        deps=[ev_xn2T, self.ev_rw,
                                              ev_gate_done])
                    evl = self.dve(lambda h, tt=tt: h.tensor_add(
                        F["lg"][:, tt * c.E:(tt + 1) * c.E],
                        self.ps_mm[:, 0:c.E], self.rb_bc[:, 0:c.E]),
                        deps=[evm, self.ev_rb])
                    self.drain("dve")
                    ev1 = self.dve(lambda h, tt=tt: h.tensor_reduce(
                        F["gcl"][:, tt * 8:tt * 8 + 1],
                        F["lg"][:, tt * c.E:(tt + 1) * c.E], AX.X, Op.max))
                    self.drain("dve")
                    ev2 = self.dve(lambda h, tt=tt: h.tensor_scalar(
                        F["lgm"][:, tt * c.E:(tt + 1) * c.E],
                        F["lg"][:, tt * c.E:(tt + 1) * c.E],
                        F["gcl"][:, tt * 8:tt * 8 + 1], None, Op.is_ge))
                    self.drain("dve")
                    ev3 = self.dve(lambda h, tt=tt: h.scalar_tensor_tensor(
                        F["lgm"][:, tt * c.E:(tt + 1) * c.E],
                        F["lgm"][:, tt * c.E:(tt + 1) * c.E], -1e30,
                        F["lg"][:, tt * c.E:(tt + 1) * c.E],
                        Op.mult, Op.add))
                    self.drain("dve")
                    ev4 = self.dve(lambda h, tt=tt: h.tensor_reduce(
                        F["gcl"][:, tt * 8 + 1:tt * 8 + 2],
                        F["lgm"][:, tt * c.E:(tt + 1) * c.E], AX.X, Op.max))
                    self.drain("dve")
                    ev5 = self.dve(lambda h, tt=tt: h.tensor_scalar_mul(
                        F["gcl"][:, tt * 8 + 2:tt * 8 + 3],
                        F["gcl"][:, tt * 8:tt * 8 + 1], -1.0))
                    self.drain("dve")
                    ev6 = self.dve(lambda h, tt=tt: h.tensor_tensor(
                        F["gcl"][:, tt * 8 + 3:tt * 8 + 4],
                        F["gcl"][:, tt * 8 + 1:tt * 8 + 2],
                        F["gcl"][:, tt * 8 + 2:tt * 8 + 3], Op.add))
                    self.drain("dve")
                    ev7 = self.act(lambda h, tt=tt: h.activation(
                        F["lgn"][:, tt * c.E:(tt + 1) * c.E],
                        F["lg"][:, tt * c.E:(tt + 1) * c.E], AF.Exp,
                        bias=F["gcl"][:, tt * 8 + 2:tt * 8 + 3]),
                        deps=[ev5])
                    ev8 = self.act(lambda h, tt=tt: h.activation(
                        F["gcl"][:, tt * 8 + 4:tt * 8 + 5],
                        F["gcl"][:, tt * 8 + 3:tt * 8 + 4], AF.Exp),
                        deps=[ev6])
                    self.drain("act")
                    ev9 = self.act(lambda h, tt=tt: h.activation(
                        F["gcl"][:, tt * 8 + 5:tt * 8 + 6],
                        F["gcl"][:, tt * 8 + 4:tt * 8 + 5], AF.Ln,
                        bias=1.0), deps=[ev8])
                    self.drain("act")
                    ev10 = self.act(lambda h, tt=tt: h.activation(
                        F["gcl"][:, tt * 8 + 5:tt * 8 + 6],
                        F["gcl"][:, tt * 8 + 5:tt * 8 + 6], AF.Exp,
                        scale=-1.0))
                    ev11 = self.dve(lambda h, tt=tt: h.tensor_scalar(
                        F["lgm"][:, tt * c.E:(tt + 1) * c.E],
                        F["lg"][:, tt * c.E:(tt + 1) * c.E],
                        F["gcl"][:, tt * 8 + 1:tt * 8 + 2], None, Op.is_ge),
                        deps=[ev4])
                    self.drain("dve")
                    ev12 = self.dve(lambda h, tt=tt: h.tensor_mul(
                        F["lgn"][:, tt * c.E:(tt + 1) * c.E],
                        F["lgn"][:, tt * c.E:(tt + 1) * c.E],
                        F["lgm"][:, tt * c.E:(tt + 1) * c.E]),
                        deps=[ev7])
                    self.drain("dve")
                    ev_gate_done = self.dve(lambda h, tt=tt, ti=ti:
                        h.tensor_scalar(
                            self.gates_all[:, ti * c.E:(ti + 1) * c.E],
                            F["lgn"][:, tt * c.E:(tt + 1) * c.E],
                            F["gcl"][:, tt * 8 + 5:tt * 8 + 6], None,
                            Op.mult), deps=[ev10])
                ev_step_free = ev_gate_done

            if c.debug:
                ti0 = g * c.GT // P
                nti = c.GT // P
                self.dma("sp", self.dram["dbg_gates"][ti0:ti0 + nti, :, :].rearrange("t p e -> p t e"),
                         self.gates_all[:, ti0 * c.E:(ti0 + nti) * c.E]
                         .rearrange("p (t e) -> p t e", t=nti),
                         [ev_gate_done], "q_dbgg")

            ev_facc = None
            for e in range(c.E):
                for fq in range(c.NFQ):
                    first = (e == 0 and fq == 0)
                    ev_w2 = self.dma(
                        "pool",
                        F["w2q"][:, :].rearrange("p (fb o) -> p fb o",
                                                 fb=c.FB),
                        self.dram["w2"][e, fq * c.FQ:(fq + 1) * c.FQ, :]
                        .rearrange("(fb p) o -> p fb o", p=P),
                        [ev_w2_free], "q_w2")
                    ev_h_last = None
                    for fb in range(c.FB):
                        hpar = fb % 2
                        ps_a = [self.ps_h1, self.ps_bc[0]][hpar]
                        ps_b = [self.ps_h3, self.ps_bc[1]][hpar]
                        sl4 = slab % 4
                        slab += 1
                        fcol = fq * c.FQ + fb * P
                        so = sl4 * 2 * c.DMB * P
                        ld1 = self.dma(
                            "pool",
                            F["w13s"][:, so:so + c.DMB * P].rearrange(
                                "p (b f) -> p b f", b=c.DMB),
                            self.dram["w1"][e, :, fcol:fcol + P].rearrange(
                                "(b p) f -> p b f", p=P),
                            [ev_slab_free[sl4]], "q_w" + str(sl4))
                        ld3 = self.dma(
                            "pool",
                            F["w13s"][:, so + c.DMB * P:
                                      so + 2 * c.DMB * P].rearrange(
                                "p (b f) -> p b f", b=c.DMB),
                            self.dram["w3"][e, :, fcol:fcol + P].rearrange(
                                "(b p) f -> p b f", p=P),
                            [], "q_w" + str(sl4))
                        for th in range(cdiv(c.GT, 512)):
                            tw = min(512, c.GT - th * 512)
                            pairs1, pairs3 = [], []
                            for dmb in range(c.DMB):
                                pairs1.append((
                                    lambda so=so, dmb=dmb:
                                    F["w13s"][:, so + dmb * P:
                                              so + dmb * P + P],
                                    lambda dmb=dmb, th=th, tw=tw:
                                    F["xn2T"][:, dmb * c.GT + th * 512:
                                              dmb * c.GT + th * 512 + tw]))
                                pairs3.append((
                                    lambda so=so, dmb=dmb:
                                    F["w13s"][:, so + c.DMB * P + dmb * P:
                                              so + c.DMB * P + dmb * P + P],
                                    lambda dmb=dmb, th=th, tw=tw:
                                    F["xn2T"][:, dmb * c.GT + th * 512:
                                              dmb * c.GT + th * 512 + tw]))
                            evm1 = self.mm_chain(
                                ps_a[:, 0:tw], pairs1,
                                deps=[(ld3 if th == 0 else None),
                                      ev_xn2T, self._si_busy.get(hpar)])
                            evm3 = self.mm_chain(
                                ps_b[:, 0:tw], pairs3,
                                deps=[(ld3 if th == 0 else None),
                                      self._h_busy.get(hpar)])
                            ev_si = self.act(lambda h, tw=tw, hpar=hpar,
                                             ps_a=ps_a: h.activation(
                                F["silu_t"][:, hpar * 512:hpar * 512 + tw],
                                ps_a[:, 0:tw],
                                AF.Silu), deps=[evm1])
                            self._si_busy[hpar] = ev_si
                            ev_h_last = self.dve(
                                lambda h, fb=fb, th=th, tw=tw, hpar=hpar,
                                ps_b=ps_b: h.tensor_mul(
                                    F["h_q"][:, fb * c.GT + th * 512:
                                             fb * c.GT + th * 512 + tw],
                                    F["silu_t"][:, hpar * 512:
                                                hpar * 512 + tw],
                                    ps_b[:, 0:tw]),
                                deps=[ev_si, evm3, ev_hq_free])
                            self._h_busy[hpar] = ev_h_last
                        ev_slab_free[sl4] = ev_h_last
                    for tt in range(c.GT // P):
                        ti = g * (c.GT // P) + tt
                        for half in range(cdiv(c.Dm, 512)):
                            wcols = min(512, c.Dm - half * 512)
                            bank = self.ps_f[half % 2]
                            pairs = []
                            for fb in range(c.FB):
                                pairs.append((
                                    lambda fb=fb, tt=tt:
                                    F["h_q"][:, fb * c.GT + tt * P:
                                             fb * c.GT + tt * P + P],
                                    lambda fb=fb, half=half, wcols=wcols:
                                    F["w2q"][:, fb * c.Dm + half * 512:
                                             fb * c.Dm + half * 512
                                             + wcols]))
                            evm = self.mm_chain(
                                bank[:, 0:wcols], pairs,
                                deps=[ev_h_last, ev_w2,
                                      self._psf_busy.get(half % 2)])
                            go = (g * (c.GT // P) + tt) * c.E + e
                            if first:
                                ev_facc = self._psf_busy[half % 2] = self.dve(
                                    lambda h, tt=tt, half=half, wcols=wcols,
                                    bank=bank, go=go: h.tensor_scalar(
                                        F["f_acc"][:, tt * c.Dm + half * 512:
                                                   tt * c.Dm + half * 512
                                                   + wcols],
                                        bank[:, 0:wcols],
                                        self.gates_all[:, go:go + 1], None,
                                        Op.mult),
                                    deps=[evm, ev_gate_done,
                                          ev_ffn_prev_group])
                            else:
                                ev_facc = self._psf_busy[half % 2] = self.dve(
                                    lambda h, tt=tt, half=half, wcols=wcols,
                                    bank=bank, go=go:
                                    h.scalar_tensor_tensor(
                                        F["f_acc"][:, tt * c.Dm + half * 512:
                                                   tt * c.Dm + half * 512
                                                   + wcols],
                                        bank[:, 0:wcols],
                                        self.gates_all[:, go:go + 1],
                                        F["f_acc"][:, tt * c.Dm + half * 512:
                                                   tt * c.Dm + half * 512
                                                   + wcols],
                                        Op.mult, Op.add),
                                    deps=[evm, ev_gate_done])
                            self.drain("dve")
                    ev_hq_free = ev_facc
                    ev_w2_free = ev_facc

            ev_last = None
            for ps in range(c.SPG):
                row = g * c.GT + ps * c.C
                ld_x1 = self.dma(
                    "sp", F["x1b"][:, :].rearrange("p (t d) -> p t d", t=c.TT),
                    self.dram["x1_spill"][row:row + c.C, :].rearrange(
                        "(t p) d -> p t d", p=P),
                    [("q_x1", 16 * (g + 1) * c.SPG), ev_facc, ev_last]
                    + ([("q_dbg", self.counts.get("q_dbg", 0))]
                       if c.debug else []),
                    "q_x1r")
                for tt in range(c.TT):
                    gtt = ps * c.TT + tt
                    fat = F["f_acc"][:, gtt * c.Dm:(gtt + 1) * c.Dm]
                    nst = cdiv(c.Dm, 512)
                    for si in range(nst):
                        w = min(512, c.Dm - si * 512)
                        self.dve(lambda h, si=si, w=w, fat=fat: h.bn_stats(
                            F["flnst"][:, si * 6:si * 6 + 6],
                            fat[:, si * 512:si * 512 + w]),
                            deps=[ev_facc, self._flnst_busy])
                    self.drain("dve")
                    self.dve(lambda h, nst=nst: h.bn_aggr(
                        F["flnst"][:, 12:14], F["flnst"][:, 0:6 * nst]))
                    self.drain("dve")
                    ev_v4 = self.dve(lambda h: h.tensor_scalar_add(
                        F["flnst"][:, 11:12], F["flnst"][:, 13:14], c.EPS))
                    ev = self.act(lambda h: h.activation(
                        F["flnst"][:, 14:15], F["flnst"][:, 11:12], AF.Ln),
                        deps=[ev_v4])
                    self.drain("act")
                    ev = self.act(lambda h: h.activation(
                        F["flnst"][:, 14:15], F["flnst"][:, 14:15], AF.Exp,
                        scale=-0.5))
                    ev = self.dve(lambda h: h.tensor_mul(
                        F["flnst"][:, 15:16], F["flnst"][:, 12:13],
                        F["flnst"][:, 14:15]), deps=[ev])
                    self.drain("dve")
                    ev = self.dve(lambda h: h.tensor_scalar_mul(
                        F["flnst"][:, 15:16], F["flnst"][:, 15:16], -1.0))
                    ev_t4 = self.act(lambda h, fat=fat, tt=tt: h.activation(
                        F["fout"][:, tt * c.Dm:(tt + 1) * c.Dm], fat,
                        AF.Identity, bias=F["flnst"][:, 15:16],
                        scale=F["flnst"][:, 14:15]),
                        deps=[ev, getattr(self, "_fout_free", None)])
                    self._flnst_busy = ev_t4
                    ev_last = self.dve(lambda h, tt=tt: h.tensor_add(
                        F["fout"][:, tt * c.Dm:(tt + 1) * c.Dm],
                        F["fout"][:, tt * c.Dm:(tt + 1) * c.Dm],
                        F["x1b"][:, tt * c.Dm:(tt + 1) * c.Dm]),
                        deps=[ev_t4, ld_x1])
                ev_out = self.dma(
                    "sp", self.dram["out_post"][row:row + c.C, :].rearrange(
                        "(t p) d -> p t d", p=P),
                    F["fout"][:, :].rearrange("p (t d) -> p t d", t=c.TT),
                    [ev_last], "q_out")
                self._fout_free = ev_out
                n_out += 1
            ev_ffn_prev_group = ev_out
        self.n_out_dmas = n_out

    def emit_final_wait(self):
        c = self.cfg
        deps = [("q_out", 16 * self.n_out_dmas)]
        if c.debug:
            for qq in ("q_dbg", "q_dbg1", "q_dbg2", "q_dbgp", "q_dbgg"):
                if self.counts.get(qq, 0):
                    deps.append((qq, self.counts[qq]))
        self.op("pool", lambda h: None, deps=deps)


_NC_CACHE = {}


def _get_nc(debug=False):
    key = ("full", debug)
    if key not in _NC_CACHE:
        _NC_CACHE[key] = Builder(CFG(debug=debug)).build()
    return _NC_CACHE[key]


def _tile_win(w_in, perm, lo, hi, cfg):
    DMB = cfg.DMB
    w_cat_T = np.concatenate([w_in[perm, :].T,
                              w_in[cfg.DI + lo:cfg.DI + hi, :].T], axis=1)
    ncol = w_cat_T.shape[1] // P
    t = w_cat_T.reshape(DMB, P, ncol, P).transpose(2, 0, 1, 3)
    return np.ascontiguousarray(t.reshape(ncol * DMB * P, P))


def make_in_maps(inputs, cfg):
    x = np.ascontiguousarray(inputs["L0_attn_x"], dtype=np.float32)
    B, L, Dm = x.shape
    DI, DH = cfg.DI, cfg.DH
    w_in = np.asarray(inputs["L0_attn_in_proj_w"], np.float32)
    in_maps = []
    for c in range(8):
        b, D = c // 2, c % 2
        lo, hi = D * DH, (D + 1) * DH
        perm = np.concatenate([np.arange(lo, hi),
                               np.arange(0, lo),
                               np.arange(hi, DI)])
        xb = x[b]
        x_post = np.ascontiguousarray(
            xb.reshape(cfg.NCH, cfg.C, Dm)[D::2].reshape(cfg.TP, Dm))
        m = {
            "x_seq": np.ascontiguousarray(xb),
            "x_post": x_post,
            "w_in_tiled": _tile_win(w_in, perm, lo, hi, cfg),
            "conv_w": np.ascontiguousarray(
                np.asarray(inputs["L0_attn_conv_w"], np.float32)[perm]),
            "conv_b": np.ascontiguousarray(
                np.asarray(inputs["L0_attn_conv_b"], np.float32)[perm]),
            "x_proj_T": np.ascontiguousarray(
                np.asarray(inputs["L0_attn_x_proj_w"], np.float32)[:, perm].T),
            "dt_w_T": np.ascontiguousarray(
                np.asarray(inputs["L0_attn_dt_proj_w"],
                           np.float32)[lo:hi].T),
            "dt_b": np.ascontiguousarray(
                np.asarray(inputs["L0_attn_dt_proj_b"], np.float32)[lo:hi]),
            "A_log_h": np.ascontiguousarray(
                np.asarray(inputs["L0_attn_A_log"], np.float32)[lo:hi]),
            "Dp_h": np.ascontiguousarray(
                np.asarray(inputs["L0_attn_Dp"], np.float32)[lo:hi]),
            "op_w_T": np.ascontiguousarray(
                np.asarray(inputs["L0_attn_out_proj_w"],
                           np.float32)[:, lo:hi].T),
            "norm_w": np.asarray(inputs["L0_attn_norm_w"], np.float32),
            "router_w": np.asarray(inputs["L0_ffn_router_w"], np.float32),
            "router_b": np.asarray(inputs["L0_ffn_router_b"], np.float32),
            "w1": np.asarray(inputs["L0_ffn_w1"], np.float32),
            "w3": np.asarray(inputs["L0_ffn_w3"], np.float32),
            "w2": np.asarray(inputs["L0_ffn_w2"], np.float32),
        }
        in_maps.append(m)
    return in_maps


def kernel(**inputs):
    cfg = CFG()
    nc = _get_nc()
    in_maps = make_in_maps(inputs, cfg)
    res = run_bass_kernel_spmd(nc, in_maps, core_ids=list(range(8)))
    x = np.asarray(inputs["L0_attn_x"])
    B, L, Dm = x.shape
    out = np.empty((B, L, Dm), np.float32)
    for c in range(8):
        b, D = c // 2, c % 2
        op = res.results[c]["out_post"].reshape(cfg.TP // cfg.C, cfg.C, Dm)
        out.reshape(B, cfg.NCH, cfg.C, Dm)[b, D::2] = op
    return out


# version 30
# speedup vs baseline: 1.1755x; 1.0158x over previous
import numpy as np
import concourse.bass as bass
import concourse.mybir as mybir
from concourse.alu_op_type import AluOpType as Op
from concourse.bass_utils import run_bass_kernel_spmd

F32 = mybir.dt.float32
BF16 = mybir.dt.bfloat16
AF = mybir.ActivationFunctionType
AX = mybir.AxisListType
P = 128
ENGINES = ("sp", "act", "dve", "pe", "pool")
RG_PAIRS = [[0, 1], [2, 3], [4, 5], [6, 7]]


class CFG:
    def __init__(self, **kw):
        self.Dm = 1024; self.DI = 2048; self.NST = 16; self.RK = 64
        self.CONV = 4; self.F = 4096; self.E = 4; self.L = 4096
        self.C = 256
        self.GT = 1024
        self.FQ = 1024
        self.DBB = 2
        self.EPS = 1e-5
        self.debug = False
        for k, v in kw.items():
            setattr(self, k, v)
        self.DH = self.DI // 2
        self.NCH = self.L // self.C
        self.TP = self.L // 2
        self.TT = self.C // P
        self.DBLK = self.DH // P
        self.CBLK = self.DI // P
        self.DMB = self.Dm // P
        self.NG = self.TP // self.GT
        self.SPG = self.GT // self.C
        self.NFQ = self.F // self.FQ
        self.FB = self.FQ // P
        self.RKC = self.RK + 2 * self.NST
        self.DBB = min(self.DBB, self.DBLK)
        assert self.C % P == 0 and self.DH % P == 0 and self.Dm % P == 0
        assert self.TP % self.GT == 0 and self.GT % self.C == 0
        assert self.F % self.FQ == 0 and self.FQ % P == 0
        assert self.DBLK % self.DBB == 0
        assert self.RK <= P and self.NST * 2 + self.RK <= P
        assert self.NCH % 2 == 0


def cdiv(a, b):
    return (a + b - 1) // b


class Builder:
    def __init__(self, cfg: CFG):
        self.cfg = cfg
        self.nc = bass.Bass("TRN2", target_bir_lowering=False, num_devices=8)
        self.streams = {e: [] for e in ENGINES}
        self.counts = {}
        self.last_wait = {}
        self.sem_names = set()
        self.dram = {}

    def ev(self, sem, n=1):
        self.sem_names.add(sem)
        v = self.counts.get(sem, 0) + n
        self.counts[sem] = v
        return (sem, v)

    def op(self, eng, fn, deps=(), inc=None, inc_n=1):
        e = self.ev(inc, inc_n) if inc else None
        agg = {}
        for d in deps:
            if d is None:
                continue
            sem, val = d
            if val > 0:
                agg[sem] = max(agg.get(sem, 0), val)
        waits = []
        for sem, val in agg.items():
            key = (eng, sem)
            if self.last_wait.get(key, 0) < val:
                self.last_wait[key] = val
                waits.append((sem, val))

        def run(h, sems):
            for sem, val in waits:
                h.wait_ge(sems[sem], val)
            inst = fn(h)
            if e is not None and inst is not None:
                inst.then_inc(sems[e[0]], inc_n)

        self.streams[eng].append(run)
        return e

    def drain(self, eng):
        self.streams[eng].append(lambda h, sems: h.drain())

    def dma(self, eng, out_ap, in_ap, deps, qsem):
        return self.op(eng, lambda h: h.dma_start(out=out_ap, in_=in_ap),
                       deps=deps, inc=qsem, inc_n=16)

    def act(self, fn, deps=(), inc="s_act"):
        return self.op("act", fn, deps=deps, inc=inc)

    def dve(self, fn, deps=(), inc="s_dve"):
        return self.op("dve", fn, deps=deps, inc=inc)

    def pe(self, fn, deps=(), inc="s_pe"):
        return self.op("pe", fn, deps=deps, inc=inc)

    def pool(self, fn, deps=(), inc="s_pool"):
        return self.op("pool", fn, deps=deps, inc=inc)

    def mm_chain(self, bank_ap, pairs, deps, inc="s_pe"):
        n = len(pairs)
        ev = None
        for i, (st, mv) in enumerate(pairs):
            ev = self.pe(lambda h, st=st, mv=mv, i=i:
                         h.matmul(bank_ap, st(), mv(), start=(i == 0),
                                  stop=(i == n - 1)),
                         deps=deps if i == 0 else (),
                         inc=inc if i == n - 1 else None)
        return ev

    def din(self, name, shape, dtype=F32):
        self.dram[name] = self.nc.dram_tensor(
            name, list(shape), dtype, kind="ExternalInput").ap()

    def dout(self, name, shape, dtype=F32):
        self.dram[name] = self.nc.dram_tensor(
            name, list(shape), dtype, kind="ExternalOutput").ap()

    def dint(self, name, shape, dtype=F32):
        self.dram[name] = self.nc.dram_tensor(
            name, list(shape), dtype, kind="Internal").ap()

    def build(self):
        from contextlib import ExitStack
        c = self.cfg
        self.din("x_seq", (c.L, c.Dm))
        self.din("x_post", (c.TP, c.Dm))
        self.din("w_in_tiled", ((c.CBLK + c.DH // P) * (c.Dm // P) * P, P))
        self.din("conv_w", (c.DI, c.CONV))
        self.din("conv_b", (c.DI,))
        self.din("x_proj_T", (c.DI, c.RKC))
        self.din("dt_w_T", (c.RK, c.DH))
        self.din("dt_b", (c.DH,))
        self.din("A_log_h", (c.DH, c.NST))
        self.din("Dp_h", (c.DH,))
        self.din("op_w_T", (c.DH, c.Dm))
        self.din("norm_w", (c.Dm,))
        self.din("router_w", (c.Dm, c.E))
        self.din("router_b", (c.E,))
        self.din("w1", (c.E, c.Dm, c.F))
        self.din("w3", (c.E, c.Dm, c.F))
        self.din("w2", (c.E, c.F, c.Dm))
        self.dint("rs_in", (c.L, c.Dm))
        self.dint("comb_d", (c.TP, c.Dm))
        self.dint("x1_spill", (c.TP, c.Dm))
        self.dout("out_post", (c.TP, c.Dm))
        if c.debug:
            self.dout("dbg_xc", (c.NCH, P, c.CBLK * c.C))
            self.dout("dbg_delta", (c.NCH, P, c.DBLK * c.C))
            self.dout("dbg_y", (c.NCH, P, c.DBLK * c.C))
            self.dout("dbg_x1", (c.TP, c.Dm))
            self.dout("dbg_gates", (c.TP // P, P, c.E))

        with ExitStack() as es:
            self.es = es
            self.alloc_sbuf()
            self.emit_preload()
            partials_done = self.emit_mamba()
            rs_done = self.pool(
                lambda h: h.collective_compute(
                    "ReduceScatter", mybir.AluOpType.add, RG_PAIRS,
                    [self.dram["rs_in"][:, :]], [self.dram["comb_d"][:, :]]),
                deps=[partials_done], inc="s_pool")
            self.emit_post_ffn(rs_done)
            self.emit_final_wait()
            sems = {}
            for name in sorted(self.sem_names):
                sems[name] = es.enter_context(self.nc.semaphore(name))
            block = es.enter_context(self.nc.Block())
            for attr, eng in [("sync", "sp"), ("scalar", "act"),
                              ("vector", "dve"), ("tensor", "pe"),
                              ("gpsimd", "pool")]:
                def replay(h, eng=eng):
                    for fn in self.streams[eng]:
                        fn(h, sems)
                getattr(block, attr)(replay)
        return self.nc

    def sb(self, name, shape, dtype):
        return self.es.enter_context(self.nc.sbuf_tensor(name, shape, dtype))

    def psum_t(self, name, shape, dtype=F32):
        return self.es.enter_context(self.nc.psum_tensor(name, shape, dtype))

    def alloc_sbuf(self):
        c = self.cfg
        nb = mybir.dt.size
        self.ident_b = self.sb("ident_b", [P, P], BF16)
        self.iota_f = self.sb("iota_f", [P, P], F32)
        self.ones_f = self.sb("ones_f", [1, P], F32)
        self.sel_b = self.sb("sel_b", [2 * c.NST, 2 * c.NST * P], BF16)
        self.A_sb = self.sb("A_sb", [P, c.DBLK * c.NST], F32)
        self.convw_f = self.sb("convw_f", [P, c.CBLK * c.CONV], F32)
        self.convb_sb = self.sb("convb_sb", [P, c.CBLK], F32)
        self.dtb_sb = self.sb("dtb_sb", [P, c.DBLK], F32)
        self.dp_sb = self.sb("dp_sb", [P, c.DBLK], F32)
        self.nw_sb = self.sb("nw_sb", [P, c.DMB], F32)
        self.rw_sb = self.sb("rw_sb", [P, c.DMB * c.E], BF16)
        self.rb_bc = self.sb("rb_bc", [P, c.E], F32)
        self.hstate = self.sb("hstate", [P, c.DBLK * c.NST], F32)
        self.gates_all = self.sb("gates_all", [P, (c.TP // P) * c.E], F32)

        m_shapes = dict(
            x0_a=([P, c.TT * c.Dm], F32), x0_b=([P, c.TT * c.Dm], F32),
            xn_b=([P, c.Dm], BF16),
            xnT=([P, c.DMB * c.C], BF16),
            xi=([P, c.CBLK * (c.C + c.CONV - 1)], BF16),
            xcb=([P, c.CBLK * c.C], BF16),
            xcf=([P, 2 * c.DBLK * c.C], F32),
            z_b=([P, 2 * c.DBLK * c.C], BF16),
            delta=([P, 2 * c.DBLK * c.C], F32),
            sp_tmp=([P, c.DBLK * c.C], F32),
            du=([P, 2 * c.DBLK * c.C], F32),
            dA0=([P, c.DBB * c.C], F32), dA1=([P, c.DBB * c.C], F32),
            dBu=([P, c.DBB * c.C], F32),
            h_b0=([P, c.DBB * c.C], F32), h_b1=([P, c.DBB * c.C], F32),
            ymul=([P, c.DBB * c.C], F32),
            y_acc=([P, c.DBLK * c.C], F32),
            yg=([P, c.DBLK * c.C], BF16),
            dblb=([c.RK, c.C], BF16),
            bc16=([P, 2 * 3 * c.C], BF16),
            opev=([P, c.TT * c.Dm], F32),
            lnst=([P, 16], F32),
            w_in_ring=([P, 6 * c.DMB * P], BF16),
            opT=([P, c.DBLK * c.Dm], BF16),
            conv_diag=([P, c.CBLK * c.CONV * P], BF16),
            xprojT=([P, c.CBLK * c.RKC], BF16),
            dtprojT=([c.RK, c.DBLK * P], BF16),
        )
        f_shapes = dict(
            comb=([P, c.TT * c.Dm], F32), xp=([P, c.TT * c.Dm], F32),
            t1b=([P, c.TT * c.Dm], F32), ab=([P, c.TT * c.Dm], F32),
            x1b=([P, c.TT * c.Dm], F32), xn2=([P, c.TT * c.Dm], BF16),
            silu_t=([P, 2 * 512], F32),
            lg=([P, c.TT * c.E], F32), lgm=([P, c.TT * c.E], F32),
            lgn=([P, c.TT * c.E], F32),
            gcl=([P, 8 * c.TT], F32),
            fout=([P, c.TT * c.Dm], F32),
            flnst=([P, 16], F32),
            xn2T=([P, c.DMB * c.GT], BF16),
            f_acc=([P, (c.GT // P) * c.Dm], F32),
            h_q=([P, c.FB * c.GT], BF16),
            w2q=([P, c.FB * c.Dm], BF16),
            w13s=([P, 4 * 2 * c.DMB * P], BF16),
        )

        def region(shapes):
            tot = 0
            for shape, dt in shapes.values():
                tot = (tot + 31) // 32 * 32
                tot += int(np.prod(shape[1:])) * nb(dt)
            return tot

        arena_bytes = max(region(m_shapes), region(f_shapes))
        arena = self.sb("arena", [P, arena_bytes], mybir.dt.uint8)
        base = self.nc.lookup_mloc(arena).addr

        def place(prefix, shapes):
            out, cur = {}, 0
            for name, (shape, dt) in shapes.items():
                cur = (cur + 31) // 32 * 32
                h = self.nc.virtual_sbuf_tensor(prefix + name, shape, dt)
                self.nc._tensor(h.name, h.shape, h.dtype, type="SB")
                m = self.nc.lookup_mloc(h)
                m.allocated = True
                m.addr = base + cur
                cur += int(np.prod(shape[1:])) * nb(dt)
                out[name] = h
            return out

        self.M = place("m_", m_shapes)
        self.F = place("f_", f_shapes)
        self.ps_mm = self.psum_t("ps_mm", [P, max(c.C, 512)])
        self.ps_tr = self.psum_t("ps_tr", [P, P], BF16)
        self.ps_bc = [self.psum_t("ps_bc0", [P, 2 * c.C]),
                      self.psum_t("ps_bc1", [P, 2 * c.C])]
        self.ps_h1 = self.psum_t("ps_h1", [P, 512])
        self.ps_h3 = self.psum_t("ps_h3", [P, 512])
        self.ps_f = [self.psum_t("ps_f0", [P, 512]),
                     self.psum_t("ps_f1", [P, 512])]

    def emit_preload(self):
        c, M = self.cfg, self.M
        ev_iota = self.pool(lambda h: h.iota(
            self.iota_f[:], [[-1, P]], base=0, channel_multiplier=1,
            allow_small_or_imprecise_dtypes=True))
        self.ev_id = self.dve(lambda h: h.tensor_single_scalar(
            self.ident_b[:], self.iota_f[:], 0.0, Op.is_equal),
            deps=[ev_iota])
        self.ev_ones = self.dve(lambda h: h.memset(self.ones_f[:], 1.0))
        NS2 = 2 * c.NST
        ev_sel = None
        for n in range(NS2):
            evi = self.pool(lambda h, n=n: h.iota(
                self.iota_f[0:NS2, 0:P], [[0, P]], base=-n,
                channel_multiplier=1,
                allow_small_or_imprecise_dtypes=True),
                deps=[ev_sel if ev_sel else self.ev_id])
            ev_sel = self.dve(lambda h, n=n: h.tensor_single_scalar(
                self.sel_b[:, n * P:(n + 1) * P],
                self.iota_f[0:NS2, 0:P], 0.0, Op.is_equal), deps=[evi])
        self.ev_sel = ev_sel

        def load_cols(dst, src, nblk):
            in_ap = self.dram[src].rearrange("(b p) -> p b", p=P)
            return self.op(
                "sp", lambda h, d=dst, i=in_ap: h.dma_start(
                    out=d[:, 0:nblk], in_=i, allow_slow_non_contiguous=True),
                deps=[], inc="q_pre", inc_n=16)

        pre = [load_cols(self.convb_sb, "conv_b", c.CBLK),
               load_cols(self.dtb_sb, "dt_b", c.DBLK),
               load_cols(self.dp_sb, "Dp_h", c.DBLK),
               load_cols(self.nw_sb, "norm_w", c.DMB),
               self.dma("sp",
                        self.A_sb[:, 0:c.DBLK * c.NST].rearrange(
                            "p (b n) -> p b n", b=c.DBLK),
                        self.dram["A_log_h"].rearrange(
                            "(b p) n -> p b n", p=P), [], "q_pre"),
               self.dma("sp",
                        self.convw_f[:, 0:c.CBLK * c.CONV].rearrange(
                            "p (b k) -> p b k", b=c.CBLK),
                        self.dram["conv_w"].rearrange(
                            "(b p) k -> p b k", p=P), [], "q_pre"),
               self.dma("sp", self.rb_bc[0:1, 0:c.E],
                        self.dram["router_b"].rearrange("(o e) -> o e", o=1),
                        [], "q_pre")]
        self.pre = pre
        ev = self.act(lambda h: h.activation(
            self.A_sb[:, 0:c.DBLK * c.NST], self.A_sb[:, 0:c.DBLK * c.NST],
            AF.Exp), deps=pre)
        self.ev_A = self.dve(lambda h: h.tensor_scalar_mul(
            self.A_sb[:, 0:c.DBLK * c.NST], self.A_sb[:, 0:c.DBLK * c.NST],
            -1.0), deps=[ev])
        ev = self.pe(lambda h: h.matmul(
            self.ps_mm[:, 0:c.E], self.ones_f[:], self.rb_bc[0:1, 0:c.E],
            start=True, stop=True), deps=pre + [self.ev_ones])
        self.ev_rb = self.act(lambda h: h.activation(
            self.rb_bc[:, 0:c.E], self.ps_mm[:, 0:c.E], AF.Copy), deps=[ev])
        self.ev_rw = self.dma("pool",
                              self.rw_sb[:, 0:c.DMB * c.E].rearrange(
                                  "p (b e) -> p b e", b=c.DMB),
                              self.dram["router_w"].rearrange(
                                  "(b p) e -> p b e", p=P), [], "q_pool")
        ev_cd = None
        for cblk in range(c.CBLK):
            for k in range(c.CONV):
                o = (cblk * c.CONV + k) * P
                col = cblk * c.CONV + k
                ev_cd = self.dve(lambda h, o=o, col=col:
                    h.tensor_scalar_mul(M["conv_diag"][:, o:o + P],
                                        self.ident_b[:],
                                        self.convw_f[:, col:col + 1]),
                    deps=[self.ev_id] + pre)
        self.ev_cd = ev_cd
        self.ev_h0 = self.dve(lambda h: h.memset(
            self.hstate[:, 0:c.DBLK * c.NST], 0.0))
        self.ev_halo0 = None
        for cblk in range(c.CBLK):
            o = cblk * (c.C + c.CONV - 1)
            self.ev_halo0 = self.dve(lambda h, o=o: h.memset(
                M["xi"][:, o:o + c.CONV - 1], 0.0))

        def wload3(dst_name, src, nb_):
            return self.dma("pool",
                            M[dst_name][:, :].rearrange("p (b o) -> p b o",
                                                        b=nb_),
                            self.dram[src].rearrange("(b p) o -> p b o",
                                                     p=P),
                            (), "q_pool")

        self.ev_opw = wload3("opT", "op_w_T", c.DBLK)
        self.ev_xpw = wload3("xprojT", "x_proj_T", c.CBLK)
        self.ev_dtw = self.dma(
            "pool", M["dtprojT"][0:c.RK, :], self.dram["dt_w_T"][:, :],
            [], "q_pool")

    def emit_mamba(self):
        c = self.cfg
        self._m = dict(
            x0_ev=[None, None], x0_free=[None, None],
            xi_tail=None, lnst_busy=None, tr_prev=None,
            bank_free={0: None, 1: None},
            wslab_free=[None] * 6, wslab_ev=[None] * 6, wslab=0,
            bc_free=[None, None], dA_free=[None, None], h_free=[None, None],
            hs_prev=self.ev_h0, pf_free=None, partial_last=None,
            opmm_last=None, sp_free=None, dBu_busy=None, hs_evs={},
            selmm_last=[None, None], dup_free=[None, None],
            prep=[None, None],
        )
        DONE = object()
        for j in range(c.NCH + 1):
            pg = self.emit_prep(j) if j < c.NCH else None
            sg = self.emit_scan(j - 1) if j >= 1 else None
            done_p, done_s = pg is None, sg is None
            while not (done_p and done_s):
                if not done_s:
                    done_s = next(sg, DONE) is DONE
                for _ in range(3):
                    if not done_p:
                        done_p = next(pg, DONE) is DONE
        return ("q_rs", 16 * c.NCH)

    def emit_prep(self, j):
        c, M, S = self.cfg, self.M, self._m
        x0 = [M["x0_a"], M["x0_b"]]
        sl = j % 2
        W = c.C + c.CONV - 1
        wait_weights = [self.ev_opw, self.ev_xpw, self.ev_dtw, self.ev_cd,
                        self.ev_A, self.ev_sel, self.ev_halo0]
        def load_x0(jj):
            ssl = jj % 2
            S["x0_ev"][ssl] = self.dma(
                "sp",
                x0[ssl][:, 0:c.TT * c.Dm].rearrange("p (t d) -> p t d",
                                                    t=c.TT),
                self.dram["x_seq"][jj * c.C:(jj + 1) * c.C, :].rearrange(
                    "(t p) d -> p t d", p=P),
                [S["x0_free"][ssl]], "q_x0" + ("a" if ssl == 0 else "b"))
        if j == 0:
            load_x0(0)
        if j + 1 < c.NCH:
            load_x0(j + 1)

        ev_xnT = None
        for tt in range(c.TT):
            xt = x0[sl][:, tt * c.Dm:(tt + 1) * c.Dm]
            deps = [S["x0_ev"][sl], S["lnst_busy"]]
            nst = cdiv(c.Dm, 512)
            ev_bs = None
            for si in range(nst):
                w = min(512, c.Dm - si * 512)
                ev_bs = self.dve(lambda h, xt=xt, si=si, w=w: h.bn_stats(
                    M["lnst"][:, si * 6:si * 6 + 6],
                    xt[:, si * 512:si * 512 + w]), deps=deps)
            ev_ag = self.dve(lambda h, nst=nst: h.bn_aggr(
                M["lnst"][:, 12:14], M["lnst"][:, 0:6 * nst]),
                deps=[ev_bs])
            ev_v = self.dve(lambda h: h.tensor_scalar(
                M["lnst"][:, 14:15], M["lnst"][:, 13:14], 1.0 + c.EPS,
                c.EPS * c.EPS, Op.mult, Op.add), deps=[ev_ag])
            ev_ln = self.act(lambda h: h.activation(
                M["lnst"][:, 14:15], M["lnst"][:, 14:15], AF.Ln),
                deps=[ev_v])
            ev_ex = self.act(lambda h: h.activation(
                M["lnst"][:, 14:15], M["lnst"][:, 14:15], AF.Exp,
                scale=-0.5), deps=[ev_ln])
            ev_mu = self.dve(lambda h: h.tensor_tensor(
                M["lnst"][:, 15:16], M["lnst"][:, 12:13],
                M["lnst"][:, 14:15], Op.mult), deps=[ev_ex])
            ev = self.dve(lambda h: h.tensor_scalar_mul(
                M["lnst"][:, 15:16], M["lnst"][:, 15:16], -1.0),
                deps=[ev_mu])
            ev_xn = self.act(lambda h, xt=xt: h.activation(
                M["xn_b"][:, 0:c.Dm], xt, AF.Identity,
                bias=M["lnst"][:, 15:16], scale=M["lnst"][:, 14:15]),
                deps=[ev, ev_ex, S["tr_prev"]])
            S["lnst_busy"] = ev_xn
            for dmb in range(c.DMB):
                evt = self.pe(lambda h, dmb=dmb: h.transpose(
                    self.ps_tr[:], M["xn_b"][:, dmb * P:(dmb + 1) * P],
                    self.ident_b[:]),
                    deps=[ev_xn, self.ev_id, S["tr_prev"]])
                ev_xnT = self.act(lambda h, dmb=dmb, tt=tt:
                    h.activation(
                        M["xnT"][:, dmb * c.C + tt * P:
                                 dmb * c.C + tt * P + P],
                        self.ps_tr[:], AF.Identity,
                        scale=self.nw_sb[:, dmb:dmb + 1]), deps=[evt])
                S["tr_prev"] = ev_xnT
            yield
        S["x0_free"][sl] = ev_xnT

        ev_xi_ev = []
        ev_z = None
        doff = sl * c.DBLK * c.C
        for colg in range(c.CBLK + c.DBLK):
            ws = S["wslab"] % 6
            S["wslab"] += 1
            so = ws * c.DMB * P
            ld = self.dma(
                "pool",
                M["w_in_ring"][:, so:so + c.DMB * P].rearrange(
                    "p (b cc) -> p b cc", b=c.DMB),
                self.dram["w_in_tiled"][colg * c.DMB * P:
                                        (colg + 1) * c.DMB * P, :]
                .rearrange("(b p) cc -> p b cc", p=P),
                [S["wslab_free"][ws]], "q_wi" + str(ws))
            bank = [self.ps_mm, self.ps_h1][colg % 2]
            pairs = []
            for dmb in range(c.DMB):
                pairs.append((
                    lambda so=so, dmb=dmb:
                    M["w_in_ring"][:, so + dmb * P:so + dmb * P + P],
                    lambda dmb=dmb: M["xnT"][:, dmb * c.C:
                                             dmb * c.C + c.C]))
            evm = self.mm_chain(
                bank[:, 0:c.C], pairs,
                deps=[ld, ev_xnT, S["bank_free"][colg % 2]] + wait_weights)
            S["wslab_free"][ws] = evm
            if colg < c.CBLK:
                ev = self.act(lambda h, colg=colg, bank=bank:
                    h.activation(
                        M["xi"][:, colg * W + c.CONV - 1:
                                colg * W + c.CONV - 1 + c.C],
                        bank[:, 0:c.C], AF.Copy),
                    deps=[evm, S["xi_tail"]])
                ev_xi_ev.append(ev)
            else:
                zb = colg - c.CBLK
                ev = self.act(lambda h, zb=zb, bank=bank, doff=doff:
                    h.activation(
                        M["z_b"][:, doff + zb * c.C:doff + (zb + 1) * c.C],
                        bank[:, 0:c.C], AF.Silu), deps=[evm])
                ev_z = ev
            S["bank_free"][colg % 2] = ev
            if colg % 2 == 1:
                yield
        S["prep_z"] = ev_z

        ev_xcb = []
        ev_xcf = {}
        for cblk in range(c.CBLK):
            bank = [self.ps_mm, self.ps_h1][cblk % 2]
            pairs = []
            for k in range(c.CONV):
                o = (cblk * c.CONV + k) * P
                xo = cblk * W + k
                pairs.append((lambda o=o: M["conv_diag"][:, o:o + P],
                              lambda xo=xo: M["xi"][:, xo:xo + c.C]))
            evm = self.mm_chain(bank[:, 0:c.C], pairs,
                                deps=[ev_xi_ev[cblk], ev_xi_ev[-1],
                                      S["bank_free"][cblk % 2]])
            ev1 = self.act(lambda h, cblk=cblk, bank=bank:
                h.activation(M["xcb"][:, cblk * c.C:(cblk + 1) * c.C],
                             bank[:, 0:c.C], AF.Silu,
                             bias=self.convb_sb[:, cblk:cblk + 1]),
                deps=[evm, getattr(self, "ev_dbg_xc", None)])
            ev_xcb.append(ev1)
            if cblk < c.DBLK:
                ev2 = self.act(lambda h, cblk=cblk, bank=bank, doff=doff:
                    h.activation(
                        M["xcf"][:, doff + cblk * c.C:
                                 doff + (cblk + 1) * c.C],
                        bank[:, 0:c.C], AF.Silu,
                        bias=self.convb_sb[:, cblk:cblk + 1]),
                    deps=[evm, S["xcf_free"] if "xcf_free" in S else None])
                ev_xcf[cblk] = ev2
                S["bank_free"][cblk % 2] = ev2
            else:
                S["bank_free"][cblk % 2] = ev1
            if cblk % 2 == 1:
                yield
        ev_tail = None
        for cblk in range(c.CBLK):
            o = cblk * W
            ev_tail = self.dve(lambda h, o=o: h.tensor_copy(
                M["xi"][:, o:o + c.CONV - 1],
                M["xi"][:, o + c.C:o + c.C + c.CONV - 1]),
                deps=[ev_xcb[-1], ev_xi_ev[-1]])
        S["xi_tail"] = ev_tail

        bank = self.ps_mm
        pairs = []
        for cblk in range(c.CBLK):
            pairs.append((
                lambda cblk=cblk: M["xprojT"][:, cblk * c.RKC:
                                              (cblk + 1) * c.RKC],
                lambda cblk=cblk: M["xcb"][:, cblk * c.C:
                                           (cblk + 1) * c.C]))
        evm = self.mm_chain(bank[0:c.RKC, 0:c.C], pairs,
                            deps=[ev_xcb[-1], S["bank_free"][0]])
        ev_dt = self.act(lambda h, bank=bank: h.activation(
            M["dblb"][0:c.RK, 0:c.C], bank[0:c.RK, 0:c.C], AF.Copy),
            deps=[evm])
        boff = sl * 3 * c.C
        ev_bcev = self.act(lambda h, bank=bank, boff=boff: h.activation(
            M["bc16"][c.RK:c.RK + 2 * c.NST, boff:boff + c.C],
            bank[c.RK:c.RK + 2 * c.NST, 0:c.C], AF.Copy),
            deps=[evm, S["dup_free"][sl]])
        ev_bc1 = self.dma(
            "pool", M["bc16"][0:2 * c.NST, boff + c.C:boff + 2 * c.C],
            M["bc16"][c.RK:c.RK + 2 * c.NST, boff:boff + c.C],
            [ev_bcev], "q_bc" + ("a" if sl == 0 else "b"))
        ev_bc2 = self.dma(
            "pool", M["bc16"][0:2 * c.NST, boff + 2 * c.C:boff + 3 * c.C],
            M["bc16"][c.RK:c.RK + 2 * c.NST, boff:boff + c.C],
            [ev_bcev], "q_bc" + ("a" if sl == 0 else "b"))
        S["prep_bc"] = (ev_bc1, ev_bc2)
        S["bank_free"][0] = ev_bcev

        ev_dl = []
        for dblk in range(c.DBLK):
            bank = [self.ps_mm, self.ps_h1][dblk % 2]
            evm = self.mm_chain(
                bank[:, 0:c.C],
                [(lambda dblk=dblk: M["dtprojT"][0:c.RK,
                                                 dblk * P:(dblk + 1) * P],
                  lambda: M["dblb"][0:c.RK, 0:c.C])],
                deps=[ev_dt, S["bank_free"][dblk % 2]])
            eva = self.act(lambda h, dblk=dblk, bank=bank: h.activation(
                M["sp_tmp"][:, dblk * c.C:(dblk + 1) * c.C],
                bank[:, 0:c.C], AF.Abs,
                bias=self.dtb_sb[:, dblk:dblk + 1]),
                deps=[evm, S["sp_free"]])
            evr = self.act(lambda h, dblk=dblk, bank=bank, doff=doff:
                h.activation(
                    M["delta"][:, doff + dblk * c.C:
                               doff + (dblk + 1) * c.C],
                    bank[:, 0:c.C], AF.Relu,
                    bias=self.dtb_sb[:, dblk:dblk + 1]),
                deps=[evm, getattr(self, "ev_dbg_dl", None)])
            ev_dl.append((eva, evr))
            S["bank_free"][dblk % 2] = evr
            if dblk % 2 == 1:
                yield
        eve = None
        for dblk in range(c.DBLK):
            eve = self.act(lambda h, dblk=dblk: h.activation(
                M["sp_tmp"][:, dblk * c.C:(dblk + 1) * c.C],
                M["sp_tmp"][:, dblk * c.C:(dblk + 1) * c.C],
                AF.Exp, scale=-1.0), deps=[ev_dl[dblk][0]])
        evl = None
        for dblk in range(c.DBLK):
            evl = self.act(lambda h, dblk=dblk: h.activation(
                M["sp_tmp"][:, dblk * c.C:(dblk + 1) * c.C],
                M["sp_tmp"][:, dblk * c.C:(dblk + 1) * c.C],
                AF.Ln, bias=1.0), deps=[eve])
        ev_delta = None
        dl_evs = []
        for dblk in range(c.DBLK):
            ev_delta = self.dve(lambda h, dblk=dblk, doff=doff:
                h.tensor_add(
                    M["delta"][:, doff + dblk * c.C:
                               doff + (dblk + 1) * c.C],
                    M["delta"][:, doff + dblk * c.C:
                               doff + (dblk + 1) * c.C],
                    M["sp_tmp"][:, dblk * c.C:(dblk + 1) * c.C]),
                deps=[evl, ev_dl[dblk][1]])
            dl_evs.append(ev_delta)
        S["sp_free"] = ev_delta
        ev_du = None
        for dblk in range(c.DBLK):
            ev_du = self.dve(lambda h, dblk=dblk, doff=doff: h.tensor_mul(
                M["du"][:, doff + dblk * c.C:doff + (dblk + 1) * c.C],
                M["delta"][:, doff + dblk * c.C:doff + (dblk + 1) * c.C],
                M["xcf"][:, doff + dblk * c.C:doff + (dblk + 1) * c.C]),
                deps=[dl_evs[dblk], ev_xcf.get(dblk)])
        yield
        S["prep"][sl] = dict(delta=ev_delta, du=ev_du, xcf=ev_xcf,
                             z=ev_z, bc=S["prep_bc"], xcb=ev_xcb[-1])

        if c.debug:
            self.ev_dbg_xc = self.dma(
                "pool", self.dram["dbg_xc"][j, :, :],
                M["xcb"][:, :], [ev_xcb[-1]], "q_dbgp")
            self.ev_dbg_dl = self.dma(
                "sp", self.dram["dbg_delta"][j, :, :],
                M["delta"][:, doff:doff + c.DBLK * c.C], [ev_delta],
                "q_dbg1")

    def emit_scan(self, j):
        c, M, S = self.cfg, self.M, self._m
        sl = j % 2
        doff = sl * c.DBLK * c.C
        boff = sl * 3 * c.C
        pr = S["prep"][sl]
        dA = [M["dA0"], M["dA1"]]
        h_b = [M["h_b0"], M["h_b1"]]
        nbt = c.DBLK // c.DBB
        DBC = c.DBB * c.C
        ev_y = None
        ev_hs_new = None
        bcdep = pr["bc"]
        rep_banks = [(self.ps_bc[0], self.ps_bc[1]),
                     (self.ps_h3, self.ps_f[1])]
        hs_prev_map = S["hs_evs"]
        S["hs_evs"] = {}
        for n in range(c.NST):
            npar = n % 2
            bB, bC = rep_banks[npar]
            evB = self.pe(lambda h, n=n, bB=bB, boff=boff: h.matmul(
                bB[:, 0:2 * c.C],
                self.sel_b[:, n * P:(n + 1) * P],
                M["bc16"][0:2 * c.NST, boff + c.C:boff + 3 * c.C],
                start=True, stop=True),
                deps=[bcdep[0], bcdep[1], S["bc_free"][npar],
                      self.ev_sel])
            evC = self.pe(lambda h, n=n, bC=bC, boff=boff: h.matmul(
                bC[:, 0:2 * c.C],
                self.sel_b[:, (c.NST + n) * P:(c.NST + n + 1) * P],
                M["bc16"][0:2 * c.NST, boff + c.C:boff + 3 * c.C],
                start=True, stop=True), deps=[])
            S["selmm_last"][sl] = evC
            for bb in range(nbt):
                par = (n * nbt + bb) % 2
                ev_dA = None
                for q in range(c.DBB):
                    dblk = bb * c.DBB + q
                    ev_dA = self.act(lambda h, q=q, dblk=dblk, n=n,
                                     par=par, doff=doff: h.activation(
                        dA[par][:, q * c.C:(q + 1) * c.C],
                        M["delta"][:, doff + dblk * c.C:
                                   doff + (dblk + 1) * c.C],
                        AF.Exp,
                        scale=self.A_sb[:, dblk * c.NST + n:
                                        dblk * c.NST + n + 1]),
                        deps=[pr["delta"], S["dA_free"][par]])
                evd = self.dve(lambda h, bb=bb, doff=doff, bB=bB:
                    h.tensor_mul(
                        M["dBu"][:, 0:DBC],
                        M["du"][:, doff + bb * DBC:doff + (bb + 1) * DBC],
                        bB[:, 0:DBC]),
                    deps=[pr["du"], evC, S["h_free"][par],
                          S["dBu_busy"]])
                evs = None
                for q in range(c.DBB):
                    dblk = bb * c.DBB + q
                    col = dblk * c.NST + n
                    evs = self.dve(lambda h, q=q, col=col, par=par:
                        h.tensor_tensor_scan(
                            h_b[par][:, q * c.C:(q + 1) * c.C],
                            dA[par][:, q * c.C:(q + 1) * c.C],
                            M["dBu"][:, q * c.C:(q + 1) * c.C],
                            self.hstate[:, col:col + 1],
                            Op.mult, Op.add),
                        deps=[ev_dA, evd,
                              hs_prev_map.get((n, bb), self.ev_h0)])
                S["dA_free"][par] = evs
                S["dBu_busy"] = evs
                col0 = bb * c.DBB * c.NST + n
                ev_hs_new = self.act(lambda h, par=par, col0=col0:
                    h.activation(
                        self.hstate[:, col0:col0 + (c.DBB - 1) * c.NST
                                    + 1:c.NST],
                        h_b[par][:, c.C - 1:c.DBB * c.C:c.C], AF.Copy),
                    deps=[evs])
                S["hs_evs"][(n, bb)] = ev_hs_new
                if n == 0:
                    ev_y = self.dve(lambda h, bb=bb, bC=bC, par=par:
                        h.tensor_mul(
                            M["y_acc"][:, bb * DBC:(bb + 1) * DBC],
                            h_b[par][:, 0:DBC], bC[:, 0:DBC]),
                        deps=[evs, getattr(self, "ev_dbg_y", None)])
                else:
                    ev_ym = self.dve(lambda h, bC=bC, par=par:
                        h.tensor_mul(
                            M["ymul"][:, 0:DBC], h_b[par][:, 0:DBC],
                            bC[:, 0:DBC]), deps=[evs])
                    ev_y = self.dve(lambda h, bb=bb: h.tensor_add(
                        M["y_acc"][:, bb * DBC:(bb + 1) * DBC],
                        M["y_acc"][:, bb * DBC:(bb + 1) * DBC],
                        M["ymul"][:, 0:DBC]), deps=[ev_ym])
                S["h_free"][par] = ev_y
            S["bc_free"][npar] = ev_y
            yield
        S["dup_free"][sl] = S["selmm_last"][sl]

        zs_busy = [None, None]
        evg1 = None
        evg2 = None
        ev_yg = None
        for dblk in range(c.DBLK):
            zsl = dblk % 2
            ev_zs = self.act(lambda h, dblk=dblk, doff=doff, zsl=zsl:
                h.activation(
                    M["dBu"][:, zsl * c.C:(zsl + 1) * c.C],
                    M["z_b"][:, doff + dblk * c.C:doff + (dblk + 1) * c.C],
                    AF.Silu),
                deps=[pr["z"], ev_y, zs_busy[zsl]])
            evg1 = self.dve(lambda h, dblk=dblk, doff=doff, zsl=zsl:
                h.tensor_scalar(
                    M["ymul"][:, zsl * c.C:(zsl + 1) * c.C],
                    M["xcf"][:, doff + dblk * c.C:
                             doff + (dblk + 1) * c.C],
                    self.dp_sb[:, dblk:dblk + 1], None, Op.mult),
                deps=[ev_y])
            evg2 = self.dve(lambda h, dblk=dblk, zsl=zsl: h.tensor_add(
                M["y_acc"][:, dblk * c.C:(dblk + 1) * c.C],
                M["y_acc"][:, dblk * c.C:(dblk + 1) * c.C],
                M["ymul"][:, zsl * c.C:(zsl + 1) * c.C]), deps=[evg1])
            ev_yg = self.dve(lambda h, dblk=dblk, zsl=zsl: h.tensor_mul(
                M["yg"][:, dblk * c.C:(dblk + 1) * c.C],
                M["y_acc"][:, dblk * c.C:(dblk + 1) * c.C],
                M["dBu"][:, zsl * c.C:(zsl + 1) * c.C]),
                deps=[evg2, ev_zs, S["opmm_last"]])
            zs_busy[zsl] = ev_yg
            S["dBu_busy"] = ev_yg
            if dblk % 2 == 1:
                yield
        S["xcf_free"] = evg1
        ev_yg_last = ev_yg

        if c.debug:
            self.ev_dbg_y = self.dma(
                "sp", self.dram["dbg_y"][j, :, :],
                M["y_acc"][:, :], [evg2], "q_dbg2")

        ev_opev = []
        for tt in range(c.TT):
            for half in range(cdiv(c.Dm, 512)):
                wcols = min(512, c.Dm - half * 512)
                bank = self.ps_f[0]
                pairs = []
                for cblk in range(c.DBLK):
                    pairs.append((
                        lambda cblk=cblk, tt=tt:
                        M["yg"][:, cblk * c.C + tt * P:
                                cblk * c.C + tt * P + P],
                        lambda cblk=cblk, half=half, wcols=wcols:
                        M["opT"][:, cblk * c.Dm + half * 512:
                                 cblk * c.Dm + half * 512 + wcols]))
                evm = self.mm_chain(bank[:, 0:wcols], pairs,
                                    deps=[ev_yg_last, S["pf_free"]])
                S["opmm_last"] = evm
                evo = self.act(lambda h, tt=tt, half=half, wcols=wcols,
                               bank=bank: h.activation(
                    M["opev"][:, tt * c.Dm + half * 512:
                              tt * c.Dm + half * 512 + wcols],
                    bank[:, 0:wcols], AF.Copy),
                    deps=[evm, S["partial_last"]])
                S["pf_free"] = evo
                ev_opev.append(evo)
            yield
        rb = (j % 2) * c.TP + (j // 2) * c.C
        S["partial_last"] = self.dma(
            "sp",
            self.dram["rs_in"][rb:rb + c.C, :].rearrange(
                "(t p) d -> p t d", p=P),
            M["opev"][:, 0:c.TT * c.Dm].rearrange("p (t d) -> p t d",
                                                  t=c.TT),
            [ev_opev[-1]], "q_rs")

    def emit_post_ffn(self, rs_done):
        c, F = self.cfg, self.F
        self._flnst_busy = None
        mamba_done = [rs_done]
        for qq in ("q_dbg1", "q_dbg2", "q_dbgp"):
            if self.counts.get(qq, 0):
                mamba_done.append((qq, self.counts[qq]))
        ev_ffn_prev_group = None
        facc_ev = {}
        ev_out_last = None
        ev_tr_prev = None
        ev_w2_free = None
        ev_hq_free = None
        slab = 0
        ev_slab_free = [None, None, None, None]
        self._si_busy = {}
        self._h_busy = {}
        self._psf_busy = {}
        n_out = 0
        for g in range(c.NG):
            ev_xn2T = None
            ev_gate_done = None
            ev_step_free = ev_ffn_prev_group
            for ps in range(c.SPG):
                row = g * c.GT + ps * c.C
                ld_cb = self.dma(
                    "sp", F["comb"][:, :].rearrange("p (t d) -> p t d", t=c.TT),
                    self.dram["comb_d"][row:row + c.C, :].rearrange(
                        "(t p) d -> p t d", p=P),
                    mamba_done + [ev_step_free], "q_cb")
                ld_xp = self.dma(
                    "sp", F["xp"][:, :].rearrange("p (t d) -> p t d", t=c.TT),
                    self.dram["x_post"][row:row + c.C, :].rearrange(
                        "(t p) d -> p t d", p=P),
                    mamba_done + [ev_step_free], "q_cb")

                def ln_scale_bias(src_ap, soff, extra_deps):
                    extra_deps = list(extra_deps) + [self._flnst_busy]
                    """bn stats on src -> flnst[soff]=g, flnst[soff+1]=-m*g"""
                    nst = cdiv(c.Dm, 512)
                    ev_bs = None
                    for si in range(nst):
                        w = min(512, c.Dm - si * 512)
                        ev_bs = self.dve(
                            lambda h, si=si, w=w, src_ap=src_ap:
                            h.bn_stats(F["flnst"][:, si * 6:si * 6 + 6],
                                       src_ap[:, si * 512:si * 512 + w]),
                            deps=extra_deps)
                    ev_ag = self.dve(lambda h, nst=nst: h.bn_aggr(
                        F["flnst"][:, 12:14], F["flnst"][:, 0:6 * nst]),
                        deps=[ev_bs])
                    ev_v = self.dve(lambda h: h.tensor_scalar_add(
                        F["flnst"][:, 11:12], F["flnst"][:, 13:14], c.EPS),
                        deps=[ev_ag])
                    ev_ln = self.act(lambda h, soff=soff: h.activation(
                        F["flnst"][:, soff:soff + 1], F["flnst"][:, 11:12],
                        AF.Ln), deps=[ev_v])
                    ev_ex = self.act(lambda h, soff=soff: h.activation(
                        F["flnst"][:, soff:soff + 1],
                        F["flnst"][:, soff:soff + 1], AF.Exp, scale=-0.5),
                        deps=[ev_ln])
                    ev_mu = self.dve(lambda h, soff=soff: h.tensor_mul(
                        F["flnst"][:, soff + 1:soff + 2],
                        F["flnst"][:, 12:13], F["flnst"][:, soff:soff + 1]),
                        deps=[ev_ex])
                    ev = self.dve(lambda h, soff=soff: h.tensor_scalar_mul(
                        F["flnst"][:, soff + 1:soff + 2],
                        F["flnst"][:, soff + 1:soff + 2], -1.0),
                        deps=[ev_mu])
                    return (ev, ev_ex)

                ev_ab = None
                ev_x1 = None
                for tt in range(c.TT):
                    xpt = F["xp"][:, tt * c.Dm:(tt + 1) * c.Dm]
                    cbt = F["comb"][:, tt * c.Dm:(tt + 1) * c.Dm]
                    ev, ev_x = ln_scale_bias(xpt, 14, [ld_xp, ev_step_free])
                    ev_t1 = self.act(lambda h, xpt=xpt, tt=tt: h.activation(
                        F["t1b"][:, tt * c.Dm:(tt + 1) * c.Dm], xpt, AF.Identity,
                        bias=F["flnst"][:, 15:16],
                        scale=F["flnst"][:, 14:15]), deps=[ev, ev_x])
                    ev_ab = self.dve(lambda h, tt=tt, cbt=cbt: h.tensor_add(
                        F["ab"][:, tt * c.Dm:(tt + 1) * c.Dm],
                        F["t1b"][:, tt * c.Dm:(tt + 1) * c.Dm], cbt),
                        deps=[ev_t1, ld_cb])
                    abt = F["ab"][:, tt * c.Dm:(tt + 1) * c.Dm]
                    ev, ev_x = ln_scale_bias(abt, 14, [ev_ab])
                    ev_t2 = self.act(lambda h, abt=abt, tt=tt: h.activation(
                        F["t1b"][:, tt * c.Dm:(tt + 1) * c.Dm], abt, AF.Identity,
                        bias=F["flnst"][:, 15:16],
                        scale=F["flnst"][:, 14:15]), deps=[ev, ev_x, ev_ab])
                    ev_x1 = self.dve(lambda h, tt=tt, xpt=xpt: h.tensor_add(
                        F["x1b"][:, tt * c.Dm:(tt + 1) * c.Dm],
                        F["t1b"][:, tt * c.Dm:(tt + 1) * c.Dm], xpt),
                        deps=[ev_t2] + list(getattr(self, "_x1_guard", [])))
                    x1t = F["x1b"][:, tt * c.Dm:(tt + 1) * c.Dm]
                    ev, ev_x = ln_scale_bias(x1t, 14, [ev_x1])
                    ev_xn2 = self.act(lambda h, x1t=x1t, tt=tt: h.activation(
                        F["xn2"][:, tt * c.Dm:(tt + 1) * c.Dm], x1t, AF.Identity,
                        bias=F["flnst"][:, 15:16],
                        scale=F["flnst"][:, 14:15]), deps=[ev, ev_x, ev_x1])
                    for dmb in range(c.DMB):
                        evt = self.pe(lambda h, dmb=dmb, tt=tt: h.transpose(
                            self.ps_tr[:],
                            F["xn2"][:, tt * c.Dm + dmb * P:
                                     tt * c.Dm + dmb * P + P],
                            self.ident_b[:]),
                            deps=[ev_xn2, ev_tr_prev, self.ev_id,
                                  ev_ffn_prev_group])
                        ev_xn2T = self.act(lambda h, dmb=dmb, tt=tt, ps=ps:
                            h.activation(
                                F["xn2T"][:, dmb * c.GT + ps * c.C + tt * P:
                                          dmb * c.GT + ps * c.C + tt * P
                                          + P],
                                self.ps_tr[:], AF.Copy), deps=[evt])
                        ev_tr_prev = ev_xn2T
                ev_sp = self.dma(
                    "sp", self.dram["x1_spill"][row:row + c.C, :].rearrange(
                        "(t p) d -> p t d", p=P),
                    F["x1b"][:, :].rearrange("p (t d) -> p t d", t=c.TT),
                    [ev_x1], "q_x1")
                self._x1_guard = [ev_sp]
                if c.debug:
                    self._x1_guard.append(self.dma(
                        "sp", self.dram["dbg_x1"][row:row + c.C, :]
                        .rearrange("(t p) d -> p t d", p=P),
                        F["x1b"][:, :].rearrange("p (t d) -> p t d", t=c.TT),
                        [ev_x1], "q_dbg"))

                for tt in range(c.TT):
                    ti = (g * c.GT + ps * c.C) // P + tt
                    pairs = []
                    for dmb in range(c.DMB):
                        pairs.append((
                            lambda dmb=dmb, ps=ps, tt=tt:
                            F["xn2T"][:, dmb * c.GT + ps * c.C + tt * P:
                                      dmb * c.GT + ps * c.C + tt * P + P],
                            lambda dmb=dmb: self.rw_sb[:, dmb * c.E:
                                                       (dmb + 1) * c.E]))
                    evm = self.mm_chain(self.ps_mm[:, 0:c.E], pairs,
                                        deps=[ev_xn2T, self.ev_rw,
                                              ev_gate_done])
                    evl = self.dve(lambda h, tt=tt: h.tensor_add(
                        F["lg"][:, tt * c.E:(tt + 1) * c.E],
                        self.ps_mm[:, 0:c.E], self.rb_bc[:, 0:c.E]),
                        deps=[evm, self.ev_rb])
                    ev1 = self.dve(lambda h, tt=tt: h.tensor_reduce(
                        F["gcl"][:, tt * 8:tt * 8 + 1],
                        F["lg"][:, tt * c.E:(tt + 1) * c.E], AX.X, Op.max),
                        deps=[evl])
                    ev2 = self.dve(lambda h, tt=tt: h.tensor_scalar(
                        F["lgm"][:, tt * c.E:(tt + 1) * c.E],
                        F["lg"][:, tt * c.E:(tt + 1) * c.E],
                        F["gcl"][:, tt * 8:tt * 8 + 1], None, Op.is_ge),
                        deps=[ev1])
                    ev3 = self.dve(lambda h, tt=tt: h.scalar_tensor_tensor(
                        F["lgm"][:, tt * c.E:(tt + 1) * c.E],
                        F["lgm"][:, tt * c.E:(tt + 1) * c.E], -1e30,
                        F["lg"][:, tt * c.E:(tt + 1) * c.E],
                        Op.mult, Op.add), deps=[ev2])
                    ev4 = self.dve(lambda h, tt=tt: h.tensor_reduce(
                        F["gcl"][:, tt * 8 + 1:tt * 8 + 2],
                        F["lgm"][:, tt * c.E:(tt + 1) * c.E], AX.X, Op.max),
                        deps=[ev3])
                    ev5 = self.dve(lambda h, tt=tt: h.tensor_scalar_mul(
                        F["gcl"][:, tt * 8 + 2:tt * 8 + 3],
                        F["gcl"][:, tt * 8:tt * 8 + 1], -1.0), deps=[ev1])
                    ev6 = self.dve(lambda h, tt=tt: h.tensor_tensor(
                        F["gcl"][:, tt * 8 + 3:tt * 8 + 4],
                        F["gcl"][:, tt * 8 + 1:tt * 8 + 2],
                        F["gcl"][:, tt * 8 + 2:tt * 8 + 3], Op.add),
                        deps=[ev4, ev5])
                    ev7 = self.act(lambda h, tt=tt: h.activation(
                        F["lgn"][:, tt * c.E:(tt + 1) * c.E],
                        F["lg"][:, tt * c.E:(tt + 1) * c.E], AF.Exp,
                        bias=F["gcl"][:, tt * 8 + 2:tt * 8 + 3]),
                        deps=[ev5])
                    ev8 = self.act(lambda h, tt=tt: h.activation(
                        F["gcl"][:, tt * 8 + 4:tt * 8 + 5],
                        F["gcl"][:, tt * 8 + 3:tt * 8 + 4], AF.Exp),
                        deps=[ev6])
                    ev9 = self.act(lambda h, tt=tt: h.activation(
                        F["gcl"][:, tt * 8 + 5:tt * 8 + 6],
                        F["gcl"][:, tt * 8 + 4:tt * 8 + 5], AF.Ln,
                        bias=1.0), deps=[ev8])
                    ev10 = self.act(lambda h, tt=tt: h.activation(
                        F["gcl"][:, tt * 8 + 5:tt * 8 + 6],
                        F["gcl"][:, tt * 8 + 5:tt * 8 + 6], AF.Exp,
                        scale=-1.0), deps=[ev9])
                    ev11 = self.dve(lambda h, tt=tt: h.tensor_scalar(
                        F["lgm"][:, tt * c.E:(tt + 1) * c.E],
                        F["lg"][:, tt * c.E:(tt + 1) * c.E],
                        F["gcl"][:, tt * 8 + 1:tt * 8 + 2], None, Op.is_ge),
                        deps=[ev4])
                    ev12 = self.dve(lambda h, tt=tt: h.tensor_mul(
                        F["lgn"][:, tt * c.E:(tt + 1) * c.E],
                        F["lgn"][:, tt * c.E:(tt + 1) * c.E],
                        F["lgm"][:, tt * c.E:(tt + 1) * c.E]),
                        deps=[ev7, ev11])
                    ev_gate_done = self.dve(lambda h, tt=tt, ti=ti:
                        h.tensor_scalar(
                            self.gates_all[:, ti * c.E:(ti + 1) * c.E],
                            F["lgn"][:, tt * c.E:(tt + 1) * c.E],
                            F["gcl"][:, tt * 8 + 5:tt * 8 + 6], None,
                            Op.mult), deps=[ev10, ev12])
                ev_step_free = ev_gate_done

            if c.debug:
                ti0 = g * c.GT // P
                nti = c.GT // P
                self.dma("sp", self.dram["dbg_gates"][ti0:ti0 + nti, :, :].rearrange("t p e -> p t e"),
                         self.gates_all[:, ti0 * c.E:(ti0 + nti) * c.E]
                         .rearrange("p (t e) -> p t e", t=nti),
                         [ev_gate_done], "q_dbgg")

            ev_facc = None
            for e in range(c.E):
                for fq in range(c.NFQ):
                    first = (e == 0 and fq == 0)
                    ev_w2 = self.dma(
                        "pool",
                        F["w2q"][:, :].rearrange("p (fb o) -> p fb o",
                                                 fb=c.FB),
                        self.dram["w2"][e, fq * c.FQ:(fq + 1) * c.FQ, :]
                        .rearrange("(fb p) o -> p fb o", p=P),
                        [ev_w2_free], "q_w2")
                    ev_h_last = None
                    for fb in range(c.FB):
                        hpar = fb % 2
                        ps_a = [self.ps_h1, self.ps_bc[0]][hpar]
                        ps_b = [self.ps_h3, self.ps_bc[1]][hpar]
                        sl4 = slab % 4
                        slab += 1
                        fcol = fq * c.FQ + fb * P
                        so = sl4 * 2 * c.DMB * P
                        ld1 = self.dma(
                            "pool",
                            F["w13s"][:, so:so + c.DMB * P].rearrange(
                                "p (b f) -> p b f", b=c.DMB),
                            self.dram["w1"][e, :, fcol:fcol + P].rearrange(
                                "(b p) f -> p b f", p=P),
                            [ev_slab_free[sl4]], "q_w" + str(sl4))
                        ld3 = self.dma(
                            "pool",
                            F["w13s"][:, so + c.DMB * P:
                                      so + 2 * c.DMB * P].rearrange(
                                "p (b f) -> p b f", b=c.DMB),
                            self.dram["w3"][e, :, fcol:fcol + P].rearrange(
                                "(b p) f -> p b f", p=P),
                            [], "q_w" + str(sl4))
                        for th in range(cdiv(c.GT, 512)):
                            tw = min(512, c.GT - th * 512)
                            pairs1, pairs3 = [], []
                            for dmb in range(c.DMB):
                                pairs1.append((
                                    lambda so=so, dmb=dmb:
                                    F["w13s"][:, so + dmb * P:
                                              so + dmb * P + P],
                                    lambda dmb=dmb, th=th, tw=tw:
                                    F["xn2T"][:, dmb * c.GT + th * 512:
                                              dmb * c.GT + th * 512 + tw]))
                                pairs3.append((
                                    lambda so=so, dmb=dmb:
                                    F["w13s"][:, so + c.DMB * P + dmb * P:
                                              so + c.DMB * P + dmb * P + P],
                                    lambda dmb=dmb, th=th, tw=tw:
                                    F["xn2T"][:, dmb * c.GT + th * 512:
                                              dmb * c.GT + th * 512 + tw]))
                            evm1 = self.mm_chain(
                                ps_a[:, 0:tw], pairs1,
                                deps=[(ld3 if th == 0 else None),
                                      ev_xn2T, self._si_busy.get(hpar)])
                            evm3 = self.mm_chain(
                                ps_b[:, 0:tw], pairs3,
                                deps=[(ld3 if th == 0 else None),
                                      self._h_busy.get(hpar)])
                            ev_si = self.act(lambda h, tw=tw, hpar=hpar,
                                             ps_a=ps_a: h.activation(
                                F["silu_t"][:, hpar * 512:hpar * 512 + tw],
                                ps_a[:, 0:tw],
                                AF.Silu), deps=[evm1])
                            self._si_busy[hpar] = ev_si
                            ev_h_last = self.dve(
                                lambda h, fb=fb, th=th, tw=tw, hpar=hpar,
                                ps_b=ps_b: h.tensor_mul(
                                    F["h_q"][:, fb * c.GT + th * 512:
                                             fb * c.GT + th * 512 + tw],
                                    F["silu_t"][:, hpar * 512:
                                                hpar * 512 + tw],
                                    ps_b[:, 0:tw]),
                                deps=[ev_si, evm3, ev_hq_free])
                            self._h_busy[hpar] = ev_h_last
                        ev_slab_free[sl4] = ev_h_last
                    for tt in range(c.GT // P):
                        ti = g * (c.GT // P) + tt
                        for half in range(cdiv(c.Dm, 512)):
                            wcols = min(512, c.Dm - half * 512)
                            bank = self.ps_f[half % 2]
                            pairs = []
                            for fb in range(c.FB):
                                pairs.append((
                                    lambda fb=fb, tt=tt:
                                    F["h_q"][:, fb * c.GT + tt * P:
                                             fb * c.GT + tt * P + P],
                                    lambda fb=fb, half=half, wcols=wcols:
                                    F["w2q"][:, fb * c.Dm + half * 512:
                                             fb * c.Dm + half * 512
                                             + wcols]))
                            evm = self.mm_chain(
                                bank[:, 0:wcols], pairs,
                                deps=[ev_h_last, ev_w2,
                                      self._psf_busy.get(half % 2)])
                            go = (g * (c.GT // P) + tt) * c.E + e
                            if first:
                                ev_facc = self._psf_busy[half % 2] = self.dve(
                                    lambda h, tt=tt, half=half, wcols=wcols,
                                    bank=bank, go=go: h.tensor_scalar(
                                        F["f_acc"][:, tt * c.Dm + half * 512:
                                                   tt * c.Dm + half * 512
                                                   + wcols],
                                        bank[:, 0:wcols],
                                        self.gates_all[:, go:go + 1], None,
                                        Op.mult),
                                    deps=[evm, ev_gate_done,
                                          ev_ffn_prev_group])
                            else:
                                ev_facc = self._psf_busy[half % 2] = self.dve(
                                    lambda h, tt=tt, half=half, wcols=wcols,
                                    bank=bank, go=go:
                                    h.scalar_tensor_tensor(
                                        F["f_acc"][:, tt * c.Dm + half * 512:
                                                   tt * c.Dm + half * 512
                                                   + wcols],
                                        bank[:, 0:wcols],
                                        self.gates_all[:, go:go + 1],
                                        F["f_acc"][:, tt * c.Dm + half * 512:
                                                   tt * c.Dm + half * 512
                                                   + wcols],
                                        Op.mult, Op.add),
                                    deps=[evm, ev_gate_done,
                                          facc_ev.get((tt, half))])
                            facc_ev[(tt, half)] = ev_facc
                    ev_hq_free = ev_facc
                    ev_w2_free = ev_facc

            ev_last = None
            for ps in range(c.SPG):
                row = g * c.GT + ps * c.C
                ld_x1 = self.dma(
                    "sp", F["x1b"][:, :].rearrange("p (t d) -> p t d", t=c.TT),
                    self.dram["x1_spill"][row:row + c.C, :].rearrange(
                        "(t p) d -> p t d", p=P),
                    [("q_x1", 16 * (g + 1) * c.SPG), ev_facc, ev_last]
                    + ([("q_dbg", self.counts.get("q_dbg", 0))]
                       if c.debug else []),
                    "q_x1r")
                for tt in range(c.TT):
                    gtt = ps * c.TT + tt
                    fat = F["f_acc"][:, gtt * c.Dm:(gtt + 1) * c.Dm]
                    nst = cdiv(c.Dm, 512)
                    ev_bs4 = None
                    for si in range(nst):
                        w = min(512, c.Dm - si * 512)
                        ev_bs4 = self.dve(
                            lambda h, si=si, w=w, fat=fat: h.bn_stats(
                                F["flnst"][:, si * 6:si * 6 + 6],
                                fat[:, si * 512:si * 512 + w]),
                            deps=[ev_facc, self._flnst_busy])
                    ev_ag4 = self.dve(lambda h, nst=nst: h.bn_aggr(
                        F["flnst"][:, 12:14], F["flnst"][:, 0:6 * nst]),
                        deps=[ev_bs4])
                    ev_v4 = self.dve(lambda h: h.tensor_scalar_add(
                        F["flnst"][:, 11:12], F["flnst"][:, 13:14], c.EPS),
                        deps=[ev_ag4])
                    ev_l4 = self.act(lambda h: h.activation(
                        F["flnst"][:, 14:15], F["flnst"][:, 11:12], AF.Ln),
                        deps=[ev_v4])
                    ev_x4 = self.act(lambda h: h.activation(
                        F["flnst"][:, 14:15], F["flnst"][:, 14:15], AF.Exp,
                        scale=-0.5), deps=[ev_l4])
                    ev_m4 = self.dve(lambda h: h.tensor_mul(
                        F["flnst"][:, 15:16], F["flnst"][:, 12:13],
                        F["flnst"][:, 14:15]), deps=[ev_x4])
                    ev = self.dve(lambda h: h.tensor_scalar_mul(
                        F["flnst"][:, 15:16], F["flnst"][:, 15:16], -1.0),
                        deps=[ev_m4])
                    ev_t4 = self.act(lambda h, fat=fat, tt=tt: h.activation(
                        F["fout"][:, tt * c.Dm:(tt + 1) * c.Dm], fat,
                        AF.Identity, bias=F["flnst"][:, 15:16],
                        scale=F["flnst"][:, 14:15]),
                        deps=[ev, ev_x4, getattr(self, "_fout_free", None)])
                    self._flnst_busy = ev_t4
                    ev_last = self.dve(lambda h, tt=tt: h.tensor_add(
                        F["fout"][:, tt * c.Dm:(tt + 1) * c.Dm],
                        F["fout"][:, tt * c.Dm:(tt + 1) * c.Dm],
                        F["x1b"][:, tt * c.Dm:(tt + 1) * c.Dm]),
                        deps=[ev_t4, ld_x1])
                ev_out = self.dma(
                    "sp", self.dram["out_post"][row:row + c.C, :].rearrange(
                        "(t p) d -> p t d", p=P),
                    F["fout"][:, :].rearrange("p (t d) -> p t d", t=c.TT),
                    [ev_last], "q_out")
                self._fout_free = ev_out
                n_out += 1
            ev_ffn_prev_group = ev_out
        self.n_out_dmas = n_out

    def emit_final_wait(self):
        c = self.cfg
        deps = [("q_out", 16 * self.n_out_dmas)]
        if c.debug:
            for qq in ("q_dbg", "q_dbg1", "q_dbg2", "q_dbgp", "q_dbgg"):
                if self.counts.get(qq, 0):
                    deps.append((qq, self.counts[qq]))
        self.op("pool", lambda h: None, deps=deps)


_NC_CACHE = {}


def _get_nc(debug=False):
    key = ("full", debug)
    if key not in _NC_CACHE:
        _NC_CACHE[key] = Builder(CFG(debug=debug)).build()
    return _NC_CACHE[key]


def _tile_win(w_in, perm, lo, hi, cfg):
    DMB = cfg.DMB
    w_cat_T = np.concatenate([w_in[perm, :].T,
                              w_in[cfg.DI + lo:cfg.DI + hi, :].T], axis=1)
    ncol = w_cat_T.shape[1] // P
    t = w_cat_T.reshape(DMB, P, ncol, P).transpose(2, 0, 1, 3)
    return np.ascontiguousarray(t.reshape(ncol * DMB * P, P))


def make_in_maps(inputs, cfg):
    x = np.ascontiguousarray(inputs["L0_attn_x"], dtype=np.float32)
    B, L, Dm = x.shape
    DI, DH = cfg.DI, cfg.DH
    w_in = np.asarray(inputs["L0_attn_in_proj_w"], np.float32)
    in_maps = []
    for c in range(8):
        b, D = c // 2, c % 2
        lo, hi = D * DH, (D + 1) * DH
        perm = np.concatenate([np.arange(lo, hi),
                               np.arange(0, lo),
                               np.arange(hi, DI)])
        xb = x[b]
        x_post = np.ascontiguousarray(
            xb.reshape(cfg.NCH, cfg.C, Dm)[D::2].reshape(cfg.TP, Dm))
        m = {
            "x_seq": np.ascontiguousarray(xb),
            "x_post": x_post,
            "w_in_tiled": _tile_win(w_in, perm, lo, hi, cfg),
            "conv_w": np.ascontiguousarray(
                np.asarray(inputs["L0_attn_conv_w"], np.float32)[perm]),
            "conv_b": np.ascontiguousarray(
                np.asarray(inputs["L0_attn_conv_b"], np.float32)[perm]),
            "x_proj_T": np.ascontiguousarray(
                np.asarray(inputs["L0_attn_x_proj_w"], np.float32)[:, perm].T),
            "dt_w_T": np.ascontiguousarray(
                np.asarray(inputs["L0_attn_dt_proj_w"],
                           np.float32)[lo:hi].T),
            "dt_b": np.ascontiguousarray(
                np.asarray(inputs["L0_attn_dt_proj_b"], np.float32)[lo:hi]),
            "A_log_h": np.ascontiguousarray(
                np.asarray(inputs["L0_attn_A_log"], np.float32)[lo:hi]),
            "Dp_h": np.ascontiguousarray(
                np.asarray(inputs["L0_attn_Dp"], np.float32)[lo:hi]),
            "op_w_T": np.ascontiguousarray(
                np.asarray(inputs["L0_attn_out_proj_w"],
                           np.float32)[:, lo:hi].T),
            "norm_w": np.asarray(inputs["L0_attn_norm_w"], np.float32),
            "router_w": np.asarray(inputs["L0_ffn_router_w"], np.float32),
            "router_b": np.asarray(inputs["L0_ffn_router_b"], np.float32),
            "w1": np.asarray(inputs["L0_ffn_w1"], np.float32),
            "w3": np.asarray(inputs["L0_ffn_w3"], np.float32),
            "w2": np.asarray(inputs["L0_ffn_w2"], np.float32),
        }
        in_maps.append(m)
    return in_maps


def kernel(**inputs):
    cfg = CFG()
    nc = _get_nc()
    in_maps = make_in_maps(inputs, cfg)
    res = run_bass_kernel_spmd(nc, in_maps, core_ids=list(range(8)))
    x = np.asarray(inputs["L0_attn_x"])
    B, L, Dm = x.shape
    out = np.empty((B, L, Dm), np.float32)
    for c in range(8):
        b, D = c // 2, c % 2
        op = res.results[c]["out_post"].reshape(cfg.TP // cfg.C, cfg.C, Dm)
        out.reshape(B, cfg.NCH, cfg.C, Dm)[b, D::2] = op
    return out


# version 31
# speedup vs baseline: 1.1816x; 1.0051x over previous
import numpy as np
import concourse.bass as bass
import concourse.mybir as mybir
from concourse.alu_op_type import AluOpType as Op
from concourse.bass_utils import run_bass_kernel_spmd

F32 = mybir.dt.float32
BF16 = mybir.dt.bfloat16
AF = mybir.ActivationFunctionType
AX = mybir.AxisListType
P = 128
ENGINES = ("sp", "act", "dve", "pe", "pool")
RG_PAIRS = [[0, 1], [2, 3], [4, 5], [6, 7]]


class CFG:
    def __init__(self, **kw):
        self.Dm = 1024; self.DI = 2048; self.NST = 16; self.RK = 64
        self.CONV = 4; self.F = 4096; self.E = 4; self.L = 4096
        self.C = 256
        self.GT = 1024
        self.FQ = 1024
        self.DBB = 2
        self.EPS = 1e-5
        self.debug = False
        for k, v in kw.items():
            setattr(self, k, v)
        self.DH = self.DI // 2
        self.NCH = self.L // self.C
        self.TP = self.L // 2
        self.TT = self.C // P
        self.DBLK = self.DH // P
        self.CBLK = self.DI // P
        self.DMB = self.Dm // P
        self.NG = self.TP // self.GT
        self.SPG = self.GT // self.C
        self.NFQ = self.F // self.FQ
        self.FB = self.FQ // P
        self.RKC = self.RK + 2 * self.NST
        self.DBB = min(self.DBB, self.DBLK)
        assert self.C % P == 0 and self.DH % P == 0 and self.Dm % P == 0
        assert self.TP % self.GT == 0 and self.GT % self.C == 0
        assert self.F % self.FQ == 0 and self.FQ % P == 0
        assert self.DBLK % self.DBB == 0
        assert self.RK <= P and self.NST * 2 + self.RK <= P
        assert self.NCH % 2 == 0


def cdiv(a, b):
    return (a + b - 1) // b


class Builder:
    def __init__(self, cfg: CFG):
        self.cfg = cfg
        self.nc = bass.Bass("TRN2", target_bir_lowering=False, num_devices=8)
        self.streams = {e: [] for e in ENGINES}
        self.counts = {}
        self.last_wait = {}
        self.sem_names = set()
        self.dram = {}

    def ev(self, sem, n=1):
        self.sem_names.add(sem)
        v = self.counts.get(sem, 0) + n
        self.counts[sem] = v
        return (sem, v)

    def op(self, eng, fn, deps=(), inc=None, inc_n=1):
        e = self.ev(inc, inc_n) if inc else None
        agg = {}
        for d in deps:
            if d is None:
                continue
            sem, val = d
            if val > 0:
                agg[sem] = max(agg.get(sem, 0), val)
        waits = []
        for sem, val in agg.items():
            key = (eng, sem)
            if self.last_wait.get(key, 0) < val:
                self.last_wait[key] = val
                waits.append((sem, val))

        def run(h, sems):
            for sem, val in waits:
                h.wait_ge(sems[sem], val)
            inst = fn(h)
            if e is not None and inst is not None:
                inst.then_inc(sems[e[0]], inc_n)

        self.streams[eng].append(run)
        return e

    def drain(self, eng):
        self.streams[eng].append(lambda h, sems: h.drain())

    def dma(self, eng, out_ap, in_ap, deps, qsem):
        return self.op(eng, lambda h: h.dma_start(out=out_ap, in_=in_ap),
                       deps=deps, inc=qsem, inc_n=16)

    def act(self, fn, deps=(), inc="s_act"):
        return self.op("act", fn, deps=deps, inc=inc)

    def dve(self, fn, deps=(), inc="s_dve"):
        return self.op("dve", fn, deps=deps, inc=inc)

    def pe(self, fn, deps=(), inc="s_pe"):
        return self.op("pe", fn, deps=deps, inc=inc)

    def pool(self, fn, deps=(), inc="s_pool"):
        return self.op("pool", fn, deps=deps, inc=inc)

    def mm_chain(self, bank_ap, pairs, deps, inc="s_pe"):
        n = len(pairs)
        ev = None
        for i, (st, mv) in enumerate(pairs):
            ev = self.pe(lambda h, st=st, mv=mv, i=i:
                         h.matmul(bank_ap, st(), mv(), start=(i == 0),
                                  stop=(i == n - 1)),
                         deps=deps if i == 0 else (),
                         inc=inc if i == n - 1 else None)
        return ev

    def din(self, name, shape, dtype=F32):
        self.dram[name] = self.nc.dram_tensor(
            name, list(shape), dtype, kind="ExternalInput").ap()

    def dout(self, name, shape, dtype=F32):
        self.dram[name] = self.nc.dram_tensor(
            name, list(shape), dtype, kind="ExternalOutput").ap()

    def dint(self, name, shape, dtype=F32):
        self.dram[name] = self.nc.dram_tensor(
            name, list(shape), dtype, kind="Internal").ap()

    def build(self):
        from contextlib import ExitStack
        c = self.cfg
        self.din("x_seq", (c.L, c.Dm))
        self.din("x_post", (c.TP, c.Dm))
        self.din("w_in_tiled", ((c.CBLK + c.DH // P) * (c.Dm // P) * P, P))
        self.din("conv_w", (c.DI, c.CONV))
        self.din("conv_b", (c.DI,))
        self.din("x_proj_T", (c.DI, c.RKC))
        self.din("dt_w_T", (c.RK, c.DH))
        self.din("dt_b", (c.DH,))
        self.din("A_log_h", (c.DH, c.NST))
        self.din("Dp_h", (c.DH,))
        self.din("op_w_T", (c.DH, c.Dm))
        self.din("norm_w", (c.Dm,))
        self.din("router_w", (c.Dm, c.E))
        self.din("router_b", (c.E,))
        self.din("w1", (c.E, c.Dm, c.F))
        self.din("w3", (c.E, c.Dm, c.F))
        self.din("w2", (c.E, c.F, c.Dm))
        self.dint("rs_in", (c.L, c.Dm))
        self.dint("comb_d", (c.TP, c.Dm))
        self.dint("x1_spill", (c.TP, c.Dm))
        self.dout("out_post", (c.TP, c.Dm))
        if c.debug:
            self.dout("dbg_xc", (c.NCH, P, c.CBLK * c.C))
            self.dout("dbg_delta", (c.NCH, P, c.DBLK * c.C))
            self.dout("dbg_y", (c.NCH, P, c.DBLK * c.C))
            self.dout("dbg_x1", (c.TP, c.Dm))
            self.dout("dbg_gates", (c.TP // P, P, c.E))

        with ExitStack() as es:
            self.es = es
            self.alloc_sbuf()
            self.emit_preload()
            partials_done = self.emit_mamba()
            rs_done = self.pool(
                lambda h: h.collective_compute(
                    "ReduceScatter", mybir.AluOpType.add, RG_PAIRS,
                    [self.dram["rs_in"][:, :]], [self.dram["comb_d"][:, :]]),
                deps=[partials_done], inc="s_pool")
            self.emit_post_ffn(rs_done)
            self.emit_final_wait()
            sems = {}
            for name in sorted(self.sem_names):
                sems[name] = es.enter_context(self.nc.semaphore(name))
            block = es.enter_context(self.nc.Block())
            for attr, eng in [("sync", "sp"), ("scalar", "act"),
                              ("vector", "dve"), ("tensor", "pe"),
                              ("gpsimd", "pool")]:
                def replay(h, eng=eng):
                    for fn in self.streams[eng]:
                        fn(h, sems)
                getattr(block, attr)(replay)
        return self.nc

    def sb(self, name, shape, dtype):
        return self.es.enter_context(self.nc.sbuf_tensor(name, shape, dtype))

    def psum_t(self, name, shape, dtype=F32):
        return self.es.enter_context(self.nc.psum_tensor(name, shape, dtype))

    def alloc_sbuf(self):
        c = self.cfg
        nb = mybir.dt.size
        self.ident_b = self.sb("ident_b", [P, P], BF16)
        self.iota_f = self.sb("iota_f", [P, P], F32)
        self.ones_f = self.sb("ones_f", [1, P], F32)
        self.sel_b = self.sb("sel_b", [2 * c.NST, 2 * c.NST * P], BF16)
        self.A_sb = self.sb("A_sb", [P, c.DBLK * c.NST], F32)
        self.convw_f = self.sb("convw_f", [P, c.CBLK * c.CONV], F32)
        self.convb_sb = self.sb("convb_sb", [P, c.CBLK], F32)
        self.dtb_sb = self.sb("dtb_sb", [P, c.DBLK], F32)
        self.dp_sb = self.sb("dp_sb", [P, c.DBLK], F32)
        self.nw_sb = self.sb("nw_sb", [P, c.DMB], F32)
        self.rw_sb = self.sb("rw_sb", [P, c.DMB * c.E], BF16)
        self.rb_bc = self.sb("rb_bc", [P, c.E], F32)
        self.hstate = self.sb("hstate", [P, c.DBLK * c.NST], F32)
        self.gates_all = self.sb("gates_all", [P, (c.TP // P) * c.E], F32)

        m_shapes = dict(
            x0_a=([P, c.TT * c.Dm], F32), x0_b=([P, c.TT * c.Dm], F32),
            xn_b=([P, c.Dm], BF16),
            xnT=([P, c.DMB * c.C], BF16),
            xi=([P, c.CBLK * (c.C + c.CONV - 1)], BF16),
            xcb=([P, c.CBLK * c.C], BF16),
            xcf=([P, 2 * c.DBLK * c.C], F32),
            z_b=([P, 2 * c.DBLK * c.C], BF16),
            delta=([P, 2 * c.DBLK * c.C], F32),
            sp_tmp=([P, c.DBLK * c.C], F32),
            du=([P, 2 * c.DBLK * c.C], F32),
            dA0=([P, c.DBB * c.C], F32), dA1=([P, c.DBB * c.C], F32),
            dBu=([P, c.DBB * c.C], F32),
            h_b0=([P, c.DBB * c.C], F32), h_b1=([P, c.DBB * c.C], F32),
            ymul=([P, c.DBB * c.C], F32),
            y_acc=([P, c.DBLK * c.C], F32),
            yg=([P, c.DBLK * c.C], BF16),
            dblb=([c.RK, c.C], BF16),
            bc16=([P, 2 * 3 * c.C], BF16),
            opev=([P, c.TT * c.Dm], F32),
            lnst=([P, 16], F32),
            w_in_ring=([P, 6 * c.DMB * P], BF16),
            opT=([P, c.DBLK * c.Dm], BF16),
            conv_diag=([P, c.CBLK * c.CONV * P], BF16),
            xprojT=([P, c.CBLK * c.RKC], BF16),
            dtprojT=([c.RK, c.DBLK * P], BF16),
        )
        f_shapes = dict(
            comb=([P, c.TT * c.Dm], F32), xp=([P, c.TT * c.Dm], F32),
            t1b=([P, c.TT * c.Dm], F32), ab=([P, c.TT * c.Dm], F32),
            x1b=([P, c.TT * c.Dm], F32), xn2=([P, c.TT * c.Dm], BF16),
            silu_t=([P, 2 * 512], F32),
            lg=([P, c.TT * c.E], F32), lgm=([P, c.TT * c.E], F32),
            lgn=([P, c.TT * c.E], F32),
            gcl=([P, 8 * c.TT], F32),
            fout=([P, c.TT * c.Dm], F32),
            flnst=([P, 16], F32),
            xn2T=([P, c.DMB * c.GT], BF16),
            f_acc=([P, (c.GT // P) * c.Dm], F32),
            h_q=([P, c.FB * c.GT], BF16),
            w2q=([P, c.FB * c.Dm], BF16),
            w13s=([P, 4 * 2 * c.DMB * P], BF16),
        )

        def region(shapes):
            tot = 0
            for shape, dt in shapes.values():
                tot = (tot + 31) // 32 * 32
                tot += int(np.prod(shape[1:])) * nb(dt)
            return tot

        arena_bytes = max(region(m_shapes), region(f_shapes))
        arena = self.sb("arena", [P, arena_bytes], mybir.dt.uint8)
        base = self.nc.lookup_mloc(arena).addr

        def place(prefix, shapes):
            out, cur = {}, 0
            for name, (shape, dt) in shapes.items():
                cur = (cur + 31) // 32 * 32
                h = self.nc.virtual_sbuf_tensor(prefix + name, shape, dt)
                self.nc._tensor(h.name, h.shape, h.dtype, type="SB")
                m = self.nc.lookup_mloc(h)
                m.allocated = True
                m.addr = base + cur
                cur += int(np.prod(shape[1:])) * nb(dt)
                out[name] = h
            return out

        self.M = place("m_", m_shapes)
        self.F = place("f_", f_shapes)
        self.ps_mm = self.psum_t("ps_mm", [P, max(c.C, 512)])
        self.ps_tr = self.psum_t("ps_tr", [P, P], BF16)
        self.ps_bc = [self.psum_t("ps_bc0", [P, 2 * c.C]),
                      self.psum_t("ps_bc1", [P, 2 * c.C])]
        self.ps_h1 = self.psum_t("ps_h1", [P, 512])
        self.ps_h3 = self.psum_t("ps_h3", [P, 512])
        self.ps_f = [self.psum_t("ps_f0", [P, 512]),
                     self.psum_t("ps_f1", [P, 512])]

    def emit_preload(self):
        c, M = self.cfg, self.M
        ev_iota = self.pool(lambda h: h.iota(
            self.iota_f[:], [[-1, P]], base=0, channel_multiplier=1,
            allow_small_or_imprecise_dtypes=True))
        self.ev_id = self.dve(lambda h: h.tensor_single_scalar(
            self.ident_b[:], self.iota_f[:], 0.0, Op.is_equal),
            deps=[ev_iota])
        self.ev_ones = self.dve(lambda h: h.memset(self.ones_f[:], 1.0))
        NS2 = 2 * c.NST
        ev_sel = None
        for n in range(NS2):
            evi = self.pool(lambda h, n=n: h.iota(
                self.iota_f[0:NS2, 0:P], [[0, P]], base=-n,
                channel_multiplier=1,
                allow_small_or_imprecise_dtypes=True),
                deps=[ev_sel if ev_sel else self.ev_id])
            ev_sel = self.dve(lambda h, n=n: h.tensor_single_scalar(
                self.sel_b[:, n * P:(n + 1) * P],
                self.iota_f[0:NS2, 0:P], 0.0, Op.is_equal), deps=[evi])
        self.ev_sel = ev_sel

        def load_cols(dst, src, nblk):
            in_ap = self.dram[src].rearrange("(b p) -> p b", p=P)
            return self.op(
                "sp", lambda h, d=dst, i=in_ap: h.dma_start(
                    out=d[:, 0:nblk], in_=i, allow_slow_non_contiguous=True),
                deps=[], inc="q_pre", inc_n=16)

        pre = [load_cols(self.convb_sb, "conv_b", c.CBLK),
               load_cols(self.dtb_sb, "dt_b", c.DBLK),
               load_cols(self.dp_sb, "Dp_h", c.DBLK),
               load_cols(self.nw_sb, "norm_w", c.DMB),
               self.dma("sp",
                        self.A_sb[:, 0:c.DBLK * c.NST].rearrange(
                            "p (b n) -> p b n", b=c.DBLK),
                        self.dram["A_log_h"].rearrange(
                            "(b p) n -> p b n", p=P), [], "q_pre"),
               self.dma("sp",
                        self.convw_f[:, 0:c.CBLK * c.CONV].rearrange(
                            "p (b k) -> p b k", b=c.CBLK),
                        self.dram["conv_w"].rearrange(
                            "(b p) k -> p b k", p=P), [], "q_pre"),
               self.dma("sp", self.rb_bc[0:1, 0:c.E],
                        self.dram["router_b"].rearrange("(o e) -> o e", o=1),
                        [], "q_pre")]
        self.pre = pre
        ev = self.act(lambda h: h.activation(
            self.A_sb[:, 0:c.DBLK * c.NST], self.A_sb[:, 0:c.DBLK * c.NST],
            AF.Exp), deps=pre)
        self.ev_A = self.dve(lambda h: h.tensor_scalar_mul(
            self.A_sb[:, 0:c.DBLK * c.NST], self.A_sb[:, 0:c.DBLK * c.NST],
            -1.0), deps=[ev])
        ev = self.pe(lambda h: h.matmul(
            self.ps_mm[:, 0:c.E], self.ones_f[:], self.rb_bc[0:1, 0:c.E],
            start=True, stop=True), deps=pre + [self.ev_ones])
        self.ev_rb = self.act(lambda h: h.activation(
            self.rb_bc[:, 0:c.E], self.ps_mm[:, 0:c.E], AF.Copy), deps=[ev])
        self.ev_rw = self.dma("pool",
                              self.rw_sb[:, 0:c.DMB * c.E].rearrange(
                                  "p (b e) -> p b e", b=c.DMB),
                              self.dram["router_w"].rearrange(
                                  "(b p) e -> p b e", p=P), [], "q_pool")
        ev_cd = None
        for cblk in range(c.CBLK):
            for k in range(c.CONV):
                o = (cblk * c.CONV + k) * P
                col = cblk * c.CONV + k
                ev_cd = self.dve(lambda h, o=o, col=col:
                    h.tensor_scalar_mul(M["conv_diag"][:, o:o + P],
                                        self.ident_b[:],
                                        self.convw_f[:, col:col + 1]),
                    deps=[self.ev_id] + pre)
        self.ev_cd = ev_cd
        self.ev_h0 = self.dve(lambda h: h.memset(
            self.hstate[:, 0:c.DBLK * c.NST], 0.0))
        self.ev_halo0 = None
        for cblk in range(c.CBLK):
            o = cblk * (c.C + c.CONV - 1)
            self.ev_halo0 = self.dve(lambda h, o=o: h.memset(
                M["xi"][:, o:o + c.CONV - 1], 0.0))

        def wload3(dst_name, src, nb_):
            return self.dma("pool",
                            M[dst_name][:, :].rearrange("p (b o) -> p b o",
                                                        b=nb_),
                            self.dram[src].rearrange("(b p) o -> p b o",
                                                     p=P),
                            (), "q_pool")

        self.ev_opw = wload3("opT", "op_w_T", c.DBLK)
        self.ev_xpw = wload3("xprojT", "x_proj_T", c.CBLK)
        self.ev_dtw = self.dma(
            "pool", M["dtprojT"][0:c.RK, :], self.dram["dt_w_T"][:, :],
            [], "q_pool")

    def emit_mamba(self):
        c = self.cfg
        self._m = dict(
            x0_ev=[None, None], x0_free=[None, None],
            xi_tail=None, lnst_busy=None, tr_prev=None,
            bank_free={0: None, 1: None},
            wslab_free=[None] * 6, wslab_ev=[None] * 6, wslab=0,
            bc_free=[None, None], dA_free=[None, None], h_free=[None, None],
            hs_prev=self.ev_h0, pf_free=None, partial_last=None,
            opmm_last=None, sp_free=None, dBu_busy=None, hs_evs={},
            selmm_last=[None, None], dup_free=[None, None],
            prep=[None, None],
        )
        DONE = object()
        for j in range(c.NCH + 1):
            pg = self.emit_prep(j) if j < c.NCH else None
            sg = self.emit_scan(j - 1) if j >= 1 else None
            done_p, done_s = pg is None, sg is None
            while not (done_p and done_s):
                if not done_s:
                    done_s = next(sg, DONE) is DONE
                for _ in range(3):
                    if not done_p:
                        done_p = next(pg, DONE) is DONE
        return ("q_rs", 16 * c.NCH)

    def emit_prep(self, j):
        c, M, S = self.cfg, self.M, self._m
        x0 = [M["x0_a"], M["x0_b"]]
        sl = j % 2
        W = c.C + c.CONV - 1
        wait_weights = [self.ev_opw, self.ev_xpw, self.ev_dtw, self.ev_cd,
                        self.ev_A, self.ev_sel, self.ev_halo0]
        def load_x0(jj):
            ssl = jj % 2
            S["x0_ev"][ssl] = self.dma(
                "sp",
                x0[ssl][:, 0:c.TT * c.Dm].rearrange("p (t d) -> p t d",
                                                    t=c.TT),
                self.dram["x_seq"][jj * c.C:(jj + 1) * c.C, :].rearrange(
                    "(t p) d -> p t d", p=P),
                [S["x0_free"][ssl]], "q_x0" + ("a" if ssl == 0 else "b"))
        if j == 0:
            load_x0(0)
        if j + 1 < c.NCH:
            load_x0(j + 1)

        ev_xnT = None
        for tt in range(c.TT):
            xt = x0[sl][:, tt * c.Dm:(tt + 1) * c.Dm]
            deps = [S["x0_ev"][sl], S["lnst_busy"]]
            nst = cdiv(c.Dm, 512)
            ev_bs = None
            for si in range(nst):
                w = min(512, c.Dm - si * 512)
                ev_bs = self.dve(lambda h, xt=xt, si=si, w=w: h.bn_stats(
                    M["lnst"][:, si * 6:si * 6 + 6],
                    xt[:, si * 512:si * 512 + w]), deps=deps)
            ev_ag = self.dve(lambda h, nst=nst: h.bn_aggr(
                M["lnst"][:, 12:14], M["lnst"][:, 0:6 * nst]),
                deps=[ev_bs])
            ev_v = self.dve(lambda h: h.tensor_scalar(
                M["lnst"][:, 14:15], M["lnst"][:, 13:14], 1.0 + c.EPS,
                c.EPS * c.EPS, Op.mult, Op.add), deps=[ev_ag])
            ev_ln = self.act(lambda h: h.activation(
                M["lnst"][:, 14:15], M["lnst"][:, 14:15], AF.Ln),
                deps=[ev_v])
            ev_ex = self.act(lambda h: h.activation(
                M["lnst"][:, 14:15], M["lnst"][:, 14:15], AF.Exp,
                scale=-0.5), deps=[ev_ln])
            ev_mu = self.dve(lambda h: h.tensor_tensor(
                M["lnst"][:, 15:16], M["lnst"][:, 12:13],
                M["lnst"][:, 14:15], Op.mult), deps=[ev_ex])
            ev = self.dve(lambda h: h.tensor_scalar_mul(
                M["lnst"][:, 15:16], M["lnst"][:, 15:16], -1.0),
                deps=[ev_mu])
            ev_xn = self.act(lambda h, xt=xt: h.activation(
                M["xn_b"][:, 0:c.Dm], xt, AF.Identity,
                bias=M["lnst"][:, 15:16], scale=M["lnst"][:, 14:15]),
                deps=[ev, ev_ex, S["tr_prev"]])
            S["lnst_busy"] = ev_xn
            for dmb in range(c.DMB):
                evt = self.pe(lambda h, dmb=dmb: h.transpose(
                    self.ps_tr[:], M["xn_b"][:, dmb * P:(dmb + 1) * P],
                    self.ident_b[:]),
                    deps=[ev_xn, self.ev_id, S["tr_prev"]])
                ev_xnT = self.act(lambda h, dmb=dmb, tt=tt:
                    h.activation(
                        M["xnT"][:, dmb * c.C + tt * P:
                                 dmb * c.C + tt * P + P],
                        self.ps_tr[:], AF.Identity,
                        scale=self.nw_sb[:, dmb:dmb + 1]), deps=[evt])
                S["tr_prev"] = ev_xnT
            yield
        S["x0_free"][sl] = ev_xnT

        ev_xi_ev = []
        ev_z = None
        doff = sl * c.DBLK * c.C
        for colg in range(c.CBLK + c.DBLK):
            ws = S["wslab"] % 6
            S["wslab"] += 1
            so = ws * c.DMB * P
            ld = self.dma(
                "pool",
                M["w_in_ring"][:, so:so + c.DMB * P].rearrange(
                    "p (b cc) -> p b cc", b=c.DMB),
                self.dram["w_in_tiled"][colg * c.DMB * P:
                                        (colg + 1) * c.DMB * P, :]
                .rearrange("(b p) cc -> p b cc", p=P),
                [S["wslab_free"][ws]], "q_wi" + str(ws))
            bank = [self.ps_mm, self.ps_h1][colg % 2]
            pairs = []
            for dmb in range(c.DMB):
                pairs.append((
                    lambda so=so, dmb=dmb:
                    M["w_in_ring"][:, so + dmb * P:so + dmb * P + P],
                    lambda dmb=dmb: M["xnT"][:, dmb * c.C:
                                             dmb * c.C + c.C]))
            evm = self.mm_chain(
                bank[:, 0:c.C], pairs,
                deps=[ld, ev_xnT, S["bank_free"][colg % 2]] + wait_weights)
            S["wslab_free"][ws] = evm
            if colg < c.CBLK:
                ev = self.act(lambda h, colg=colg, bank=bank:
                    h.activation(
                        M["xi"][:, colg * W + c.CONV - 1:
                                colg * W + c.CONV - 1 + c.C],
                        bank[:, 0:c.C], AF.Copy),
                    deps=[evm, S["xi_tail"]])
                ev_xi_ev.append(ev)
            else:
                zb = colg - c.CBLK
                ev = self.act(lambda h, zb=zb, bank=bank, doff=doff:
                    h.activation(
                        M["z_b"][:, doff + zb * c.C:doff + (zb + 1) * c.C],
                        bank[:, 0:c.C], AF.Silu), deps=[evm])
                ev_z = ev
            S["bank_free"][colg % 2] = ev
            if colg % 2 == 1:
                yield
        S["prep_z"] = ev_z

        ev_xcb = []
        ev_xcf = {}
        for cblk in range(c.CBLK):
            bank = [self.ps_mm, self.ps_h1][cblk % 2]
            pairs = []
            for k in range(c.CONV):
                o = (cblk * c.CONV + k) * P
                xo = cblk * W + k
                pairs.append((lambda o=o: M["conv_diag"][:, o:o + P],
                              lambda xo=xo: M["xi"][:, xo:xo + c.C]))
            evm = self.mm_chain(bank[:, 0:c.C], pairs,
                                deps=[ev_xi_ev[cblk], ev_xi_ev[-1],
                                      S["bank_free"][cblk % 2]])
            ev1 = self.act(lambda h, cblk=cblk, bank=bank:
                h.activation(M["xcb"][:, cblk * c.C:(cblk + 1) * c.C],
                             bank[:, 0:c.C], AF.Silu,
                             bias=self.convb_sb[:, cblk:cblk + 1]),
                deps=[evm, getattr(self, "ev_dbg_xc", None)])
            ev_xcb.append(ev1)
            if cblk < c.DBLK:
                ev2 = self.act(lambda h, cblk=cblk, bank=bank, doff=doff:
                    h.activation(
                        M["xcf"][:, doff + cblk * c.C:
                                 doff + (cblk + 1) * c.C],
                        bank[:, 0:c.C], AF.Silu,
                        bias=self.convb_sb[:, cblk:cblk + 1]),
                    deps=[evm, S["xcf_free"] if "xcf_free" in S else None])
                ev_xcf[cblk] = ev2
                S["bank_free"][cblk % 2] = ev2
            else:
                S["bank_free"][cblk % 2] = ev1
            if cblk % 2 == 1:
                yield
        ev_tail = None
        for cblk in range(c.CBLK):
            o = cblk * W
            ev_tail = self.dve(lambda h, o=o: h.tensor_copy(
                M["xi"][:, o:o + c.CONV - 1],
                M["xi"][:, o + c.C:o + c.C + c.CONV - 1]),
                deps=[ev_xcb[-1], ev_xi_ev[-1]])
        S["xi_tail"] = ev_tail

        bank = self.ps_mm
        pairs = []
        for cblk in range(c.CBLK):
            pairs.append((
                lambda cblk=cblk: M["xprojT"][:, cblk * c.RKC:
                                              (cblk + 1) * c.RKC],
                lambda cblk=cblk: M["xcb"][:, cblk * c.C:
                                           (cblk + 1) * c.C]))
        evm = self.mm_chain(bank[0:c.RKC, 0:c.C], pairs,
                            deps=[ev_xcb[-1], S["bank_free"][0]])
        ev_dt = self.act(lambda h, bank=bank: h.activation(
            M["dblb"][0:c.RK, 0:c.C], bank[0:c.RK, 0:c.C], AF.Copy),
            deps=[evm])
        boff = sl * 3 * c.C
        ev_bcev = self.act(lambda h, bank=bank, boff=boff: h.activation(
            M["bc16"][c.RK:c.RK + 2 * c.NST, boff:boff + c.C],
            bank[c.RK:c.RK + 2 * c.NST, 0:c.C], AF.Copy),
            deps=[evm, S["dup_free"][sl]])
        ev_bc1 = self.dma(
            "pool", M["bc16"][0:2 * c.NST, boff + c.C:boff + 2 * c.C],
            M["bc16"][c.RK:c.RK + 2 * c.NST, boff:boff + c.C],
            [ev_bcev], "q_bc" + ("a" if sl == 0 else "b"))
        ev_bc2 = self.dma(
            "pool", M["bc16"][0:2 * c.NST, boff + 2 * c.C:boff + 3 * c.C],
            M["bc16"][c.RK:c.RK + 2 * c.NST, boff:boff + c.C],
            [ev_bcev], "q_bc" + ("a" if sl == 0 else "b"))
        S["prep_bc"] = (ev_bc1, ev_bc2)
        S["bank_free"][0] = ev_bcev

        ev_dl = []
        for dblk in range(c.DBLK):
            bank = [self.ps_mm, self.ps_h1][dblk % 2]
            evm = self.mm_chain(
                bank[:, 0:c.C],
                [(lambda dblk=dblk: M["dtprojT"][0:c.RK,
                                                 dblk * P:(dblk + 1) * P],
                  lambda: M["dblb"][0:c.RK, 0:c.C])],
                deps=[ev_dt, S["bank_free"][dblk % 2]])
            eva = self.act(lambda h, dblk=dblk, bank=bank: h.activation(
                M["sp_tmp"][:, dblk * c.C:(dblk + 1) * c.C],
                bank[:, 0:c.C], AF.Abs,
                bias=self.dtb_sb[:, dblk:dblk + 1]),
                deps=[evm, S["sp_free"]])
            evr = self.act(lambda h, dblk=dblk, bank=bank, doff=doff:
                h.activation(
                    M["delta"][:, doff + dblk * c.C:
                               doff + (dblk + 1) * c.C],
                    bank[:, 0:c.C], AF.Relu,
                    bias=self.dtb_sb[:, dblk:dblk + 1]),
                deps=[evm, getattr(self, "ev_dbg_dl", None)])
            ev_dl.append((eva, evr))
            S["bank_free"][dblk % 2] = evr
            if dblk % 2 == 1:
                yield
        eve = None
        for dblk in range(c.DBLK):
            eve = self.act(lambda h, dblk=dblk: h.activation(
                M["sp_tmp"][:, dblk * c.C:(dblk + 1) * c.C],
                M["sp_tmp"][:, dblk * c.C:(dblk + 1) * c.C],
                AF.Exp, scale=-1.0), deps=[ev_dl[dblk][0]])
        evl = None
        for dblk in range(c.DBLK):
            evl = self.act(lambda h, dblk=dblk: h.activation(
                M["sp_tmp"][:, dblk * c.C:(dblk + 1) * c.C],
                M["sp_tmp"][:, dblk * c.C:(dblk + 1) * c.C],
                AF.Ln, bias=1.0), deps=[eve])
        ev_delta = None
        dl_evs = []
        for dblk in range(c.DBLK):
            ev_delta = self.dve(lambda h, dblk=dblk, doff=doff:
                h.tensor_add(
                    M["delta"][:, doff + dblk * c.C:
                               doff + (dblk + 1) * c.C],
                    M["delta"][:, doff + dblk * c.C:
                               doff + (dblk + 1) * c.C],
                    M["sp_tmp"][:, dblk * c.C:(dblk + 1) * c.C]),
                deps=[evl, ev_dl[dblk][1]])
            dl_evs.append(ev_delta)
        S["sp_free"] = ev_delta
        ev_du = None
        for dblk in range(c.DBLK):
            ev_du = self.dve(lambda h, dblk=dblk, doff=doff: h.tensor_mul(
                M["du"][:, doff + dblk * c.C:doff + (dblk + 1) * c.C],
                M["delta"][:, doff + dblk * c.C:doff + (dblk + 1) * c.C],
                M["xcf"][:, doff + dblk * c.C:doff + (dblk + 1) * c.C]),
                deps=[dl_evs[dblk], ev_xcf.get(dblk)])
        yield
        S["prep"][sl] = dict(delta=ev_delta, du=ev_du, xcf=ev_xcf,
                             z=ev_z, bc=S["prep_bc"], xcb=ev_xcb[-1])

        if c.debug:
            self.ev_dbg_xc = self.dma(
                "pool", self.dram["dbg_xc"][j, :, :],
                M["xcb"][:, :], [ev_xcb[-1]], "q_dbgp")
            self.ev_dbg_dl = self.dma(
                "sp", self.dram["dbg_delta"][j, :, :],
                M["delta"][:, doff:doff + c.DBLK * c.C], [ev_delta],
                "q_dbg1")

    def emit_scan(self, j):
        c, M, S = self.cfg, self.M, self._m
        sl = j % 2
        doff = sl * c.DBLK * c.C
        boff = sl * 3 * c.C
        pr = S["prep"][sl]
        dA = [M["dA0"], M["dA1"]]
        h_b = [M["h_b0"], M["h_b1"]]
        nbt = c.DBLK // c.DBB
        DBC = c.DBB * c.C
        ev_y = None
        ev_hs_new = None
        bcdep = pr["bc"]
        rep_banks = [(self.ps_bc[0], self.ps_bc[1]),
                     (self.ps_h3, self.ps_f[1])]
        hs_prev_map = S["hs_evs"]
        S["hs_evs"] = {}
        sel_q = {}

        def emit_sel(n):
            npar_ = n % 2
            bB_, bC_ = rep_banks[npar_]
            evB_ = self.pe(lambda h, n=n, bB_=bB_, boff=boff: h.matmul(
                bB_[:, 0:2 * c.C],
                self.sel_b[:, n * P:(n + 1) * P],
                M["bc16"][0:2 * c.NST, boff + c.C:boff + 3 * c.C],
                start=True, stop=True),
                deps=[bcdep[0], bcdep[1], S["bc_free"][npar_],
                      self.ev_sel])
            evC_ = self.pe(lambda h, n=n, bC_=bC_, boff=boff: h.matmul(
                bC_[:, 0:2 * c.C],
                self.sel_b[:, (c.NST + n) * P:(c.NST + n + 1) * P],
                M["bc16"][0:2 * c.NST, boff + c.C:boff + 3 * c.C],
                start=True, stop=True), deps=[])
            sel_q[n] = (evB_, evC_)

        emit_sel(0)
        for n in range(c.NST):
            npar = n % 2
            bB, bC = rep_banks[npar]
            evB, evC = sel_q.pop(n)
            if n + 1 < c.NST:
                emit_sel(n + 1)
            S["selmm_last"][sl] = evC
            for bb in range(nbt):
                par = (n * nbt + bb) % 2
                ev_dA = None
                for q in range(c.DBB):
                    dblk = bb * c.DBB + q
                    ev_dA = self.act(lambda h, q=q, dblk=dblk, n=n,
                                     par=par, doff=doff: h.activation(
                        dA[par][:, q * c.C:(q + 1) * c.C],
                        M["delta"][:, doff + dblk * c.C:
                                   doff + (dblk + 1) * c.C],
                        AF.Exp,
                        scale=self.A_sb[:, dblk * c.NST + n:
                                        dblk * c.NST + n + 1]),
                        deps=[pr["delta"], S["dA_free"][par]])
                evd = self.dve(lambda h, bb=bb, doff=doff, bB=bB:
                    h.tensor_mul(
                        M["dBu"][:, 0:DBC],
                        M["du"][:, doff + bb * DBC:doff + (bb + 1) * DBC],
                        bB[:, 0:DBC]),
                    deps=[pr["du"], evC, S["h_free"][par],
                          S["dBu_busy"]])
                evs = None
                for q in range(c.DBB):
                    dblk = bb * c.DBB + q
                    col = dblk * c.NST + n
                    evs = self.dve(lambda h, q=q, col=col, par=par:
                        h.tensor_tensor_scan(
                            h_b[par][:, q * c.C:(q + 1) * c.C],
                            dA[par][:, q * c.C:(q + 1) * c.C],
                            M["dBu"][:, q * c.C:(q + 1) * c.C],
                            self.hstate[:, col:col + 1],
                            Op.mult, Op.add),
                        deps=[ev_dA, evd,
                              hs_prev_map.get((n, bb), self.ev_h0)])
                S["dA_free"][par] = evs
                S["dBu_busy"] = evs
                col0 = bb * c.DBB * c.NST + n
                ev_hs_new = self.act(lambda h, par=par, col0=col0:
                    h.activation(
                        self.hstate[:, col0:col0 + (c.DBB - 1) * c.NST
                                    + 1:c.NST],
                        h_b[par][:, c.C - 1:c.DBB * c.C:c.C], AF.Copy),
                    deps=[evs])
                S["hs_evs"][(n, bb)] = ev_hs_new
                if n == 0:
                    ev_y = self.dve(lambda h, bb=bb, bC=bC, par=par:
                        h.tensor_mul(
                            M["y_acc"][:, bb * DBC:(bb + 1) * DBC],
                            h_b[par][:, 0:DBC], bC[:, 0:DBC]),
                        deps=[evs, getattr(self, "ev_dbg_y", None)])
                else:
                    ev_ym = self.dve(lambda h, bC=bC, par=par:
                        h.tensor_mul(
                            M["ymul"][:, 0:DBC], h_b[par][:, 0:DBC],
                            bC[:, 0:DBC]), deps=[evs])
                    ev_y = self.dve(lambda h, bb=bb: h.tensor_add(
                        M["y_acc"][:, bb * DBC:(bb + 1) * DBC],
                        M["y_acc"][:, bb * DBC:(bb + 1) * DBC],
                        M["ymul"][:, 0:DBC]), deps=[ev_ym])
                S["h_free"][par] = ev_y
            S["bc_free"][npar] = ev_y
            yield
        S["dup_free"][sl] = S["selmm_last"][sl]

        zs_busy = [None, None]
        evg1 = None
        evg2 = None
        ev_yg = None
        for dblk in range(c.DBLK):
            zsl = dblk % 2
            ev_zs = self.act(lambda h, dblk=dblk, doff=doff, zsl=zsl:
                h.activation(
                    M["dBu"][:, zsl * c.C:(zsl + 1) * c.C],
                    M["z_b"][:, doff + dblk * c.C:doff + (dblk + 1) * c.C],
                    AF.Silu),
                deps=[pr["z"], ev_y, zs_busy[zsl]])
            evg1 = self.dve(lambda h, dblk=dblk, doff=doff, zsl=zsl:
                h.tensor_scalar(
                    M["ymul"][:, zsl * c.C:(zsl + 1) * c.C],
                    M["xcf"][:, doff + dblk * c.C:
                             doff + (dblk + 1) * c.C],
                    self.dp_sb[:, dblk:dblk + 1], None, Op.mult),
                deps=[ev_y])
            evg2 = self.dve(lambda h, dblk=dblk, zsl=zsl: h.tensor_add(
                M["y_acc"][:, dblk * c.C:(dblk + 1) * c.C],
                M["y_acc"][:, dblk * c.C:(dblk + 1) * c.C],
                M["ymul"][:, zsl * c.C:(zsl + 1) * c.C]), deps=[evg1])
            ev_yg = self.dve(lambda h, dblk=dblk, zsl=zsl: h.tensor_mul(
                M["yg"][:, dblk * c.C:(dblk + 1) * c.C],
                M["y_acc"][:, dblk * c.C:(dblk + 1) * c.C],
                M["dBu"][:, zsl * c.C:(zsl + 1) * c.C]),
                deps=[evg2, ev_zs, S["opmm_last"]])
            zs_busy[zsl] = ev_yg
            S["dBu_busy"] = ev_yg
            if dblk % 2 == 1:
                yield
        S["xcf_free"] = evg1
        ev_yg_last = ev_yg

        if c.debug:
            self.ev_dbg_y = self.dma(
                "sp", self.dram["dbg_y"][j, :, :],
                M["y_acc"][:, :], [evg2], "q_dbg2")

        ev_opev = []
        for tt in range(c.TT):
            for half in range(cdiv(c.Dm, 512)):
                wcols = min(512, c.Dm - half * 512)
                bank = self.ps_f[0]
                pairs = []
                for cblk in range(c.DBLK):
                    pairs.append((
                        lambda cblk=cblk, tt=tt:
                        M["yg"][:, cblk * c.C + tt * P:
                                cblk * c.C + tt * P + P],
                        lambda cblk=cblk, half=half, wcols=wcols:
                        M["opT"][:, cblk * c.Dm + half * 512:
                                 cblk * c.Dm + half * 512 + wcols]))
                evm = self.mm_chain(bank[:, 0:wcols], pairs,
                                    deps=[ev_yg_last, S["pf_free"]])
                S["opmm_last"] = evm
                evo = self.act(lambda h, tt=tt, half=half, wcols=wcols,
                               bank=bank: h.activation(
                    M["opev"][:, tt * c.Dm + half * 512:
                              tt * c.Dm + half * 512 + wcols],
                    bank[:, 0:wcols], AF.Copy),
                    deps=[evm, S["partial_last"]])
                S["pf_free"] = evo
                ev_opev.append(evo)
            yield
        rb = (j % 2) * c.TP + (j // 2) * c.C
        S["partial_last"] = self.dma(
            "sp",
            self.dram["rs_in"][rb:rb + c.C, :].rearrange(
                "(t p) d -> p t d", p=P),
            M["opev"][:, 0:c.TT * c.Dm].rearrange("p (t d) -> p t d",
                                                  t=c.TT),
            [ev_opev[-1]], "q_rs")

    def emit_post_ffn(self, rs_done):
        c, F = self.cfg, self.F
        self._flnst_busy = None
        mamba_done = [rs_done]
        for qq in ("q_dbg1", "q_dbg2", "q_dbgp"):
            if self.counts.get(qq, 0):
                mamba_done.append((qq, self.counts[qq]))
        ev_ffn_prev_group = None
        facc_ev = {}
        ev_out_last = None
        ev_tr_prev = None
        ev_w2_free = None
        ev_hq_free = None
        slab = 0
        ev_slab_free = [None, None, None, None]
        self._si_busy = {}
        self._h_busy = {}
        self._psf_busy = {}
        n_out = 0
        for g in range(c.NG):
            ev_xn2T = None
            ev_gate_done = None
            ev_step_free = ev_ffn_prev_group
            for ps in range(c.SPG):
                row = g * c.GT + ps * c.C
                ld_cb = self.dma(
                    "sp", F["comb"][:, :].rearrange("p (t d) -> p t d", t=c.TT),
                    self.dram["comb_d"][row:row + c.C, :].rearrange(
                        "(t p) d -> p t d", p=P),
                    mamba_done + [ev_step_free], "q_cb")
                ld_xp = self.dma(
                    "sp", F["xp"][:, :].rearrange("p (t d) -> p t d", t=c.TT),
                    self.dram["x_post"][row:row + c.C, :].rearrange(
                        "(t p) d -> p t d", p=P),
                    mamba_done + [ev_step_free], "q_cb")

                def ln_scale_bias(src_ap, soff, extra_deps):
                    extra_deps = list(extra_deps) + [self._flnst_busy]
                    """bn stats on src -> flnst[soff]=g, flnst[soff+1]=-m*g"""
                    nst = cdiv(c.Dm, 512)
                    ev_bs = None
                    for si in range(nst):
                        w = min(512, c.Dm - si * 512)
                        ev_bs = self.dve(
                            lambda h, si=si, w=w, src_ap=src_ap:
                            h.bn_stats(F["flnst"][:, si * 6:si * 6 + 6],
                                       src_ap[:, si * 512:si * 512 + w]),
                            deps=extra_deps)
                    ev_ag = self.dve(lambda h, nst=nst: h.bn_aggr(
                        F["flnst"][:, 12:14], F["flnst"][:, 0:6 * nst]),
                        deps=[ev_bs])
                    ev_v = self.dve(lambda h: h.tensor_scalar_add(
                        F["flnst"][:, 11:12], F["flnst"][:, 13:14], c.EPS),
                        deps=[ev_ag])
                    ev_ln = self.act(lambda h, soff=soff: h.activation(
                        F["flnst"][:, soff:soff + 1], F["flnst"][:, 11:12],
                        AF.Ln), deps=[ev_v])
                    ev_ex = self.act(lambda h, soff=soff: h.activation(
                        F["flnst"][:, soff:soff + 1],
                        F["flnst"][:, soff:soff + 1], AF.Exp, scale=-0.5),
                        deps=[ev_ln])
                    ev_mu = self.dve(lambda h, soff=soff: h.tensor_mul(
                        F["flnst"][:, soff + 1:soff + 2],
                        F["flnst"][:, 12:13], F["flnst"][:, soff:soff + 1]),
                        deps=[ev_ex])
                    ev = self.dve(lambda h, soff=soff: h.tensor_scalar_mul(
                        F["flnst"][:, soff + 1:soff + 2],
                        F["flnst"][:, soff + 1:soff + 2], -1.0),
                        deps=[ev_mu])
                    return (ev, ev_ex)

                ev_ab = None
                ev_x1 = None
                for tt in range(c.TT):
                    xpt = F["xp"][:, tt * c.Dm:(tt + 1) * c.Dm]
                    cbt = F["comb"][:, tt * c.Dm:(tt + 1) * c.Dm]
                    ev, ev_x = ln_scale_bias(xpt, 14, [ld_xp, ev_step_free])
                    ev_t1 = self.act(lambda h, xpt=xpt, tt=tt: h.activation(
                        F["t1b"][:, tt * c.Dm:(tt + 1) * c.Dm], xpt, AF.Identity,
                        bias=F["flnst"][:, 15:16],
                        scale=F["flnst"][:, 14:15]), deps=[ev, ev_x])
                    ev_ab = self.dve(lambda h, tt=tt, cbt=cbt: h.tensor_add(
                        F["ab"][:, tt * c.Dm:(tt + 1) * c.Dm],
                        F["t1b"][:, tt * c.Dm:(tt + 1) * c.Dm], cbt),
                        deps=[ev_t1, ld_cb])
                    abt = F["ab"][:, tt * c.Dm:(tt + 1) * c.Dm]
                    ev, ev_x = ln_scale_bias(abt, 14, [ev_ab])
                    ev_t2 = self.act(lambda h, abt=abt, tt=tt: h.activation(
                        F["t1b"][:, tt * c.Dm:(tt + 1) * c.Dm], abt, AF.Identity,
                        bias=F["flnst"][:, 15:16],
                        scale=F["flnst"][:, 14:15]), deps=[ev, ev_x, ev_ab])
                    ev_x1 = self.dve(lambda h, tt=tt, xpt=xpt: h.tensor_add(
                        F["x1b"][:, tt * c.Dm:(tt + 1) * c.Dm],
                        F["t1b"][:, tt * c.Dm:(tt + 1) * c.Dm], xpt),
                        deps=[ev_t2] + list(getattr(self, "_x1_guard", [])))
                    x1t = F["x1b"][:, tt * c.Dm:(tt + 1) * c.Dm]
                    ev, ev_x = ln_scale_bias(x1t, 14, [ev_x1])
                    ev_xn2 = self.act(lambda h, x1t=x1t, tt=tt: h.activation(
                        F["xn2"][:, tt * c.Dm:(tt + 1) * c.Dm], x1t, AF.Identity,
                        bias=F["flnst"][:, 15:16],
                        scale=F["flnst"][:, 14:15]), deps=[ev, ev_x, ev_x1])
                    for dmb in range(c.DMB):
                        evt = self.pe(lambda h, dmb=dmb, tt=tt: h.transpose(
                            self.ps_tr[:],
                            F["xn2"][:, tt * c.Dm + dmb * P:
                                     tt * c.Dm + dmb * P + P],
                            self.ident_b[:]),
                            deps=[ev_xn2, ev_tr_prev, self.ev_id,
                                  ev_ffn_prev_group])
                        ev_xn2T = self.act(lambda h, dmb=dmb, tt=tt, ps=ps:
                            h.activation(
                                F["xn2T"][:, dmb * c.GT + ps * c.C + tt * P:
                                          dmb * c.GT + ps * c.C + tt * P
                                          + P],
                                self.ps_tr[:], AF.Copy), deps=[evt])
                        ev_tr_prev = ev_xn2T
                ev_sp = self.dma(
                    "sp", self.dram["x1_spill"][row:row + c.C, :].rearrange(
                        "(t p) d -> p t d", p=P),
                    F["x1b"][:, :].rearrange("p (t d) -> p t d", t=c.TT),
                    [ev_x1], "q_x1")
                self._x1_guard = [ev_sp]
                if c.debug:
                    self._x1_guard.append(self.dma(
                        "sp", self.dram["dbg_x1"][row:row + c.C, :]
                        .rearrange("(t p) d -> p t d", p=P),
                        F["x1b"][:, :].rearrange("p (t d) -> p t d", t=c.TT),
                        [ev_x1], "q_dbg"))

                for tt in range(c.TT):
                    ti = (g * c.GT + ps * c.C) // P + tt
                    pairs = []
                    for dmb in range(c.DMB):
                        pairs.append((
                            lambda dmb=dmb, ps=ps, tt=tt:
                            F["xn2T"][:, dmb * c.GT + ps * c.C + tt * P:
                                      dmb * c.GT + ps * c.C + tt * P + P],
                            lambda dmb=dmb: self.rw_sb[:, dmb * c.E:
                                                       (dmb + 1) * c.E]))
                    evm = self.mm_chain(self.ps_mm[:, 0:c.E], pairs,
                                        deps=[ev_xn2T, self.ev_rw,
                                              ev_gate_done])
                    evl = self.dve(lambda h, tt=tt: h.tensor_add(
                        F["lg"][:, tt * c.E:(tt + 1) * c.E],
                        self.ps_mm[:, 0:c.E], self.rb_bc[:, 0:c.E]),
                        deps=[evm, self.ev_rb])
                    ev1 = self.dve(lambda h, tt=tt: h.tensor_reduce(
                        F["gcl"][:, tt * 8:tt * 8 + 1],
                        F["lg"][:, tt * c.E:(tt + 1) * c.E], AX.X, Op.max),
                        deps=[evl])
                    ev2 = self.dve(lambda h, tt=tt: h.tensor_scalar(
                        F["lgm"][:, tt * c.E:(tt + 1) * c.E],
                        F["lg"][:, tt * c.E:(tt + 1) * c.E],
                        F["gcl"][:, tt * 8:tt * 8 + 1], None, Op.is_ge),
                        deps=[ev1])
                    ev3 = self.dve(lambda h, tt=tt: h.scalar_tensor_tensor(
                        F["lgm"][:, tt * c.E:(tt + 1) * c.E],
                        F["lgm"][:, tt * c.E:(tt + 1) * c.E], -1e30,
                        F["lg"][:, tt * c.E:(tt + 1) * c.E],
                        Op.mult, Op.add), deps=[ev2])
                    ev4 = self.dve(lambda h, tt=tt: h.tensor_reduce(
                        F["gcl"][:, tt * 8 + 1:tt * 8 + 2],
                        F["lgm"][:, tt * c.E:(tt + 1) * c.E], AX.X, Op.max),
                        deps=[ev3])
                    ev5 = self.dve(lambda h, tt=tt: h.tensor_scalar_mul(
                        F["gcl"][:, tt * 8 + 2:tt * 8 + 3],
                        F["gcl"][:, tt * 8:tt * 8 + 1], -1.0), deps=[ev1])
                    ev6 = self.dve(lambda h, tt=tt: h.tensor_tensor(
                        F["gcl"][:, tt * 8 + 3:tt * 8 + 4],
                        F["gcl"][:, tt * 8 + 1:tt * 8 + 2],
                        F["gcl"][:, tt * 8 + 2:tt * 8 + 3], Op.add),
                        deps=[ev4, ev5])
                    ev7 = self.act(lambda h, tt=tt: h.activation(
                        F["lgn"][:, tt * c.E:(tt + 1) * c.E],
                        F["lg"][:, tt * c.E:(tt + 1) * c.E], AF.Exp,
                        bias=F["gcl"][:, tt * 8 + 2:tt * 8 + 3]),
                        deps=[ev5])
                    ev8 = self.act(lambda h, tt=tt: h.activation(
                        F["gcl"][:, tt * 8 + 4:tt * 8 + 5],
                        F["gcl"][:, tt * 8 + 3:tt * 8 + 4], AF.Exp),
                        deps=[ev6])
                    ev9 = self.act(lambda h, tt=tt: h.activation(
                        F["gcl"][:, tt * 8 + 5:tt * 8 + 6],
                        F["gcl"][:, tt * 8 + 4:tt * 8 + 5], AF.Ln,
                        bias=1.0), deps=[ev8])
                    ev10 = self.act(lambda h, tt=tt: h.activation(
                        F["gcl"][:, tt * 8 + 5:tt * 8 + 6],
                        F["gcl"][:, tt * 8 + 5:tt * 8 + 6], AF.Exp,
                        scale=-1.0), deps=[ev9])
                    ev11 = self.dve(lambda h, tt=tt: h.tensor_scalar(
                        F["lgm"][:, tt * c.E:(tt + 1) * c.E],
                        F["lg"][:, tt * c.E:(tt + 1) * c.E],
                        F["gcl"][:, tt * 8 + 1:tt * 8 + 2], None, Op.is_ge),
                        deps=[ev4])
                    ev12 = self.dve(lambda h, tt=tt: h.tensor_mul(
                        F["lgn"][:, tt * c.E:(tt + 1) * c.E],
                        F["lgn"][:, tt * c.E:(tt + 1) * c.E],
                        F["lgm"][:, tt * c.E:(tt + 1) * c.E]),
                        deps=[ev7, ev11])
                    ev_gate_done = self.dve(lambda h, tt=tt, ti=ti:
                        h.tensor_scalar(
                            self.gates_all[:, ti * c.E:(ti + 1) * c.E],
                            F["lgn"][:, tt * c.E:(tt + 1) * c.E],
                            F["gcl"][:, tt * 8 + 5:tt * 8 + 6], None,
                            Op.mult), deps=[ev10, ev12])
                ev_step_free = ev_gate_done

            if c.debug:
                ti0 = g * c.GT // P
                nti = c.GT // P
                self.dma("sp", self.dram["dbg_gates"][ti0:ti0 + nti, :, :].rearrange("t p e -> p t e"),
                         self.gates_all[:, ti0 * c.E:(ti0 + nti) * c.E]
                         .rearrange("p (t e) -> p t e", t=nti),
                         [ev_gate_done], "q_dbgg")

            ev_facc = None
            for e in range(c.E):
                for fq in range(c.NFQ):
                    first = (e == 0 and fq == 0)
                    ev_w2 = self.dma(
                        "pool",
                        F["w2q"][:, :].rearrange("p (fb o) -> p fb o",
                                                 fb=c.FB),
                        self.dram["w2"][e, fq * c.FQ:(fq + 1) * c.FQ, :]
                        .rearrange("(fb p) o -> p fb o", p=P),
                        [ev_w2_free], "q_w2")
                    ev_h_last = None
                    for fb in range(c.FB):
                        hpar = fb % 2
                        ps_a = [self.ps_h1, self.ps_bc[0]][hpar]
                        ps_b = [self.ps_h3, self.ps_bc[1]][hpar]
                        sl4 = slab % 4
                        slab += 1
                        fcol = fq * c.FQ + fb * P
                        so = sl4 * 2 * c.DMB * P
                        ld1 = self.dma(
                            "pool",
                            F["w13s"][:, so:so + c.DMB * P].rearrange(
                                "p (b f) -> p b f", b=c.DMB),
                            self.dram["w1"][e, :, fcol:fcol + P].rearrange(
                                "(b p) f -> p b f", p=P),
                            [ev_slab_free[sl4]], "q_w" + str(sl4))
                        ld3 = self.dma(
                            "pool",
                            F["w13s"][:, so + c.DMB * P:
                                      so + 2 * c.DMB * P].rearrange(
                                "p (b f) -> p b f", b=c.DMB),
                            self.dram["w3"][e, :, fcol:fcol + P].rearrange(
                                "(b p) f -> p b f", p=P),
                            [], "q_w" + str(sl4))
                        for th in range(cdiv(c.GT, 512)):
                            tw = min(512, c.GT - th * 512)
                            pairs1, pairs3 = [], []
                            for dmb in range(c.DMB):
                                pairs1.append((
                                    lambda so=so, dmb=dmb:
                                    F["w13s"][:, so + dmb * P:
                                              so + dmb * P + P],
                                    lambda dmb=dmb, th=th, tw=tw:
                                    F["xn2T"][:, dmb * c.GT + th * 512:
                                              dmb * c.GT + th * 512 + tw]))
                                pairs3.append((
                                    lambda so=so, dmb=dmb:
                                    F["w13s"][:, so + c.DMB * P + dmb * P:
                                              so + c.DMB * P + dmb * P + P],
                                    lambda dmb=dmb, th=th, tw=tw:
                                    F["xn2T"][:, dmb * c.GT + th * 512:
                                              dmb * c.GT + th * 512 + tw]))
                            evm1 = self.mm_chain(
                                ps_a[:, 0:tw], pairs1,
                                deps=[(ld3 if th == 0 else None),
                                      ev_xn2T, self._si_busy.get(hpar)])
                            evm3 = self.mm_chain(
                                ps_b[:, 0:tw], pairs3,
                                deps=[(ld3 if th == 0 else None),
                                      self._h_busy.get(hpar)])
                            ev_si = self.act(lambda h, tw=tw, hpar=hpar,
                                             ps_a=ps_a: h.activation(
                                F["silu_t"][:, hpar * 512:hpar * 512 + tw],
                                ps_a[:, 0:tw],
                                AF.Silu), deps=[evm1])
                            self._si_busy[hpar] = ev_si
                            ev_h_last = self.dve(
                                lambda h, fb=fb, th=th, tw=tw, hpar=hpar,
                                ps_b=ps_b: h.tensor_mul(
                                    F["h_q"][:, fb * c.GT + th * 512:
                                             fb * c.GT + th * 512 + tw],
                                    F["silu_t"][:, hpar * 512:
                                                hpar * 512 + tw],
                                    ps_b[:, 0:tw]),
                                deps=[ev_si, evm3, ev_hq_free])
                            self._h_busy[hpar] = ev_h_last
                        ev_slab_free[sl4] = ev_h_last
                    for tt in range(c.GT // P):
                        ti = g * (c.GT // P) + tt
                        for half in range(cdiv(c.Dm, 512)):
                            wcols = min(512, c.Dm - half * 512)
                            bank = self.ps_f[half % 2]
                            pairs = []
                            for fb in range(c.FB):
                                pairs.append((
                                    lambda fb=fb, tt=tt:
                                    F["h_q"][:, fb * c.GT + tt * P:
                                             fb * c.GT + tt * P + P],
                                    lambda fb=fb, half=half, wcols=wcols:
                                    F["w2q"][:, fb * c.Dm + half * 512:
                                             fb * c.Dm + half * 512
                                             + wcols]))
                            evm = self.mm_chain(
                                bank[:, 0:wcols], pairs,
                                deps=[ev_h_last, ev_w2,
                                      self._psf_busy.get(half % 2)])
                            go = (g * (c.GT // P) + tt) * c.E + e
                            if first:
                                ev_facc = self._psf_busy[half % 2] = self.dve(
                                    lambda h, tt=tt, half=half, wcols=wcols,
                                    bank=bank, go=go: h.tensor_scalar(
                                        F["f_acc"][:, tt * c.Dm + half * 512:
                                                   tt * c.Dm + half * 512
                                                   + wcols],
                                        bank[:, 0:wcols],
                                        self.gates_all[:, go:go + 1], None,
                                        Op.mult),
                                    deps=[evm, ev_gate_done,
                                          ev_ffn_prev_group])
                            else:
                                ev_facc = self._psf_busy[half % 2] = self.dve(
                                    lambda h, tt=tt, half=half, wcols=wcols,
                                    bank=bank, go=go:
                                    h.scalar_tensor_tensor(
                                        F["f_acc"][:, tt * c.Dm + half * 512:
                                                   tt * c.Dm + half * 512
                                                   + wcols],
                                        bank[:, 0:wcols],
                                        self.gates_all[:, go:go + 1],
                                        F["f_acc"][:, tt * c.Dm + half * 512:
                                                   tt * c.Dm + half * 512
                                                   + wcols],
                                        Op.mult, Op.add),
                                    deps=[evm, ev_gate_done,
                                          facc_ev.get((tt, half))])
                            facc_ev[(tt, half)] = ev_facc
                    ev_hq_free = ev_facc
                    ev_w2_free = ev_facc

            ev_last = None
            for ps in range(c.SPG):
                row = g * c.GT + ps * c.C
                ld_x1 = self.dma(
                    "sp", F["x1b"][:, :].rearrange("p (t d) -> p t d", t=c.TT),
                    self.dram["x1_spill"][row:row + c.C, :].rearrange(
                        "(t p) d -> p t d", p=P),
                    [("q_x1", 16 * (g + 1) * c.SPG), ev_facc, ev_last]
                    + ([("q_dbg", self.counts.get("q_dbg", 0))]
                       if c.debug else []),
                    "q_x1r")
                for tt in range(c.TT):
                    gtt = ps * c.TT + tt
                    fat = F["f_acc"][:, gtt * c.Dm:(gtt + 1) * c.Dm]
                    nst = cdiv(c.Dm, 512)
                    ev_bs4 = None
                    for si in range(nst):
                        w = min(512, c.Dm - si * 512)
                        ev_bs4 = self.dve(
                            lambda h, si=si, w=w, fat=fat: h.bn_stats(
                                F["flnst"][:, si * 6:si * 6 + 6],
                                fat[:, si * 512:si * 512 + w]),
                            deps=[ev_facc, self._flnst_busy])
                    ev_ag4 = self.dve(lambda h, nst=nst: h.bn_aggr(
                        F["flnst"][:, 12:14], F["flnst"][:, 0:6 * nst]),
                        deps=[ev_bs4])
                    ev_v4 = self.dve(lambda h: h.tensor_scalar_add(
                        F["flnst"][:, 11:12], F["flnst"][:, 13:14], c.EPS),
                        deps=[ev_ag4])
                    ev_l4 = self.act(lambda h: h.activation(
                        F["flnst"][:, 14:15], F["flnst"][:, 11:12], AF.Ln),
                        deps=[ev_v4])
                    ev_x4 = self.act(lambda h: h.activation(
                        F["flnst"][:, 14:15], F["flnst"][:, 14:15], AF.Exp,
                        scale=-0.5), deps=[ev_l4])
                    ev_m4 = self.dve(lambda h: h.tensor_mul(
                        F["flnst"][:, 15:16], F["flnst"][:, 12:13],
                        F["flnst"][:, 14:15]), deps=[ev_x4])
                    ev = self.dve(lambda h: h.tensor_scalar_mul(
                        F["flnst"][:, 15:16], F["flnst"][:, 15:16], -1.0),
                        deps=[ev_m4])
                    ev_t4 = self.act(lambda h, fat=fat, tt=tt: h.activation(
                        F["fout"][:, tt * c.Dm:(tt + 1) * c.Dm], fat,
                        AF.Identity, bias=F["flnst"][:, 15:16],
                        scale=F["flnst"][:, 14:15]),
                        deps=[ev, ev_x4, getattr(self, "_fout_free", None)])
                    self._flnst_busy = ev_t4
                    ev_last = self.dve(lambda h, tt=tt: h.tensor_add(
                        F["fout"][:, tt * c.Dm:(tt + 1) * c.Dm],
                        F["fout"][:, tt * c.Dm:(tt + 1) * c.Dm],
                        F["x1b"][:, tt * c.Dm:(tt + 1) * c.Dm]),
                        deps=[ev_t4, ld_x1])
                ev_out = self.dma(
                    "sp", self.dram["out_post"][row:row + c.C, :].rearrange(
                        "(t p) d -> p t d", p=P),
                    F["fout"][:, :].rearrange("p (t d) -> p t d", t=c.TT),
                    [ev_last], "q_out")
                self._fout_free = ev_out
                n_out += 1
            ev_ffn_prev_group = ev_out
        self.n_out_dmas = n_out

    def emit_final_wait(self):
        c = self.cfg
        deps = [("q_out", 16 * self.n_out_dmas)]
        if c.debug:
            for qq in ("q_dbg", "q_dbg1", "q_dbg2", "q_dbgp", "q_dbgg"):
                if self.counts.get(qq, 0):
                    deps.append((qq, self.counts[qq]))
        self.op("pool", lambda h: None, deps=deps)


_NC_CACHE = {}


def _get_nc(debug=False):
    key = ("full", debug)
    if key not in _NC_CACHE:
        _NC_CACHE[key] = Builder(CFG(debug=debug)).build()
    return _NC_CACHE[key]


def _tile_win(w_in, perm, lo, hi, cfg):
    DMB = cfg.DMB
    w_cat_T = np.concatenate([w_in[perm, :].T,
                              w_in[cfg.DI + lo:cfg.DI + hi, :].T], axis=1)
    ncol = w_cat_T.shape[1] // P
    t = w_cat_T.reshape(DMB, P, ncol, P).transpose(2, 0, 1, 3)
    return np.ascontiguousarray(t.reshape(ncol * DMB * P, P))


def make_in_maps(inputs, cfg):
    x = np.ascontiguousarray(inputs["L0_attn_x"], dtype=np.float32)
    B, L, Dm = x.shape
    DI, DH = cfg.DI, cfg.DH
    w_in = np.asarray(inputs["L0_attn_in_proj_w"], np.float32)
    in_maps = []
    for c in range(8):
        b, D = c // 2, c % 2
        lo, hi = D * DH, (D + 1) * DH
        perm = np.concatenate([np.arange(lo, hi),
                               np.arange(0, lo),
                               np.arange(hi, DI)])
        xb = x[b]
        x_post = np.ascontiguousarray(
            xb.reshape(cfg.NCH, cfg.C, Dm)[D::2].reshape(cfg.TP, Dm))
        m = {
            "x_seq": np.ascontiguousarray(xb),
            "x_post": x_post,
            "w_in_tiled": _tile_win(w_in, perm, lo, hi, cfg),
            "conv_w": np.ascontiguousarray(
                np.asarray(inputs["L0_attn_conv_w"], np.float32)[perm]),
            "conv_b": np.ascontiguousarray(
                np.asarray(inputs["L0_attn_conv_b"], np.float32)[perm]),
            "x_proj_T": np.ascontiguousarray(
                np.asarray(inputs["L0_attn_x_proj_w"], np.float32)[:, perm].T),
            "dt_w_T": np.ascontiguousarray(
                np.asarray(inputs["L0_attn_dt_proj_w"],
                           np.float32)[lo:hi].T),
            "dt_b": np.ascontiguousarray(
                np.asarray(inputs["L0_attn_dt_proj_b"], np.float32)[lo:hi]),
            "A_log_h": np.ascontiguousarray(
                np.asarray(inputs["L0_attn_A_log"], np.float32)[lo:hi]),
            "Dp_h": np.ascontiguousarray(
                np.asarray(inputs["L0_attn_Dp"], np.float32)[lo:hi]),
            "op_w_T": np.ascontiguousarray(
                np.asarray(inputs["L0_attn_out_proj_w"],
                           np.float32)[:, lo:hi].T),
            "norm_w": np.asarray(inputs["L0_attn_norm_w"], np.float32),
            "router_w": np.asarray(inputs["L0_ffn_router_w"], np.float32),
            "router_b": np.asarray(inputs["L0_ffn_router_b"], np.float32),
            "w1": np.asarray(inputs["L0_ffn_w1"], np.float32),
            "w3": np.asarray(inputs["L0_ffn_w3"], np.float32),
            "w2": np.asarray(inputs["L0_ffn_w2"], np.float32),
        }
        in_maps.append(m)
    return in_maps


def kernel(**inputs):
    cfg = CFG()
    nc = _get_nc()
    in_maps = make_in_maps(inputs, cfg)
    res = run_bass_kernel_spmd(nc, in_maps, core_ids=list(range(8)))
    x = np.asarray(inputs["L0_attn_x"])
    B, L, Dm = x.shape
    out = np.empty((B, L, Dm), np.float32)
    for c in range(8):
        b, D = c // 2, c % 2
        op = res.results[c]["out_post"].reshape(cfg.TP // cfg.C, cfg.C, Dm)
        out.reshape(B, cfg.NCH, cfg.C, Dm)[b, D::2] = op
    return out
